# version 04
# baseline (speedup 1.0000x reference)
import numpy as np
from contextlib import ExitStack

import concourse.bass as bass
import concourse.mybir as mybir
from concourse.bass_utils import run_bass_kernel_spmd

F32 = mybir.dt.float32
BF16 = mybir.dt.bfloat16
AF = mybir.ActivationFunctionType
ALU = mybir.AluOpType
AX = mybir.AxisListType

ENGINES = ("tensor", "vector", "scalar", "gpsimd", "sync")

P = 128
T = 2048
D = 1024
NT = T // P
ND = D // P
TC = T // 512


def _ap_key(ap):
    t = getattr(ap, "tensor", None)
    if t is None:
        t = ap
    n = getattr(t, "name", None)
    if n is None:
        n = str(t)
    return n


class Prog:

    DMA_SLOTS = {"sync": 8, "scalar": 2, "gpsimd": 2}

    def __init__(self, nc, stack):
        self.nc = nc
        self.stack = stack
        self.sem_cmp = {e: stack.enter_context(nc.semaphore(f"sc_{e}")) for e in ENGINES}
        self.dma_sems = {e: [stack.enter_context(nc.semaphore(f"sd_{e}{i}"))
                             for i in range(n)]
                         for e, n in self.DMA_SLOTS.items()}
        self.cnt = {(e, "cmp"): 0 for e in ENGINES}
        for e, n in self.DMA_SLOTS.items():
            for i in range(n):
                self.cnt[(e, ("dma", i))] = 0
        self.rr = {e: 0 for e in self.DMA_SLOTS}
        self.writers = {}
        self.readers = {}
        self.waited = {e: {} for e in ENGINES}

    def _sem_of(self, se, kind):
        if kind == "cmp":
            return self.sem_cmp[se]
        return self.dma_sems[se][kind[1]]

    def _wait(self, e, src):
        se, kind, cnt = src
        if se == e == "tensor" and kind == "cmp":
            return
        w = self.waited[e]
        if w.get((se, kind), 0) >= cnt:
            return
        getattr(self.nc, e).wait_ge(self._sem_of(se, kind), cnt)
        w[(se, kind)] = cnt

    def _deps(self, e, reads, writes):
        deps = {}
        for ap in reads:
            k = _ap_key(ap)
            for (ek, cnt) in self.writers.get(k, {}).items():
                deps[ek] = max(deps.get(ek, 0), cnt)
        for ap in writes:
            k = _ap_key(ap)
            for dct in (self.writers.get(k, {}), self.readers.get(k, {})):
                for (ek, cnt) in dct.items():
                    deps[ek] = max(deps.get(ek, 0), cnt)
        for (se, kind), cnt in sorted(deps.items(), key=str):
            self._wait(e, (se, kind, cnt))

    def _record(self, e, kind, cnt, reads, writes):
        for ap in reads:
            self.readers.setdefault(_ap_key(ap), {})[(e, kind)] = cnt
        for ap in writes:
            k = _ap_key(ap)
            self.writers[k] = {(e, kind): cnt}
            self.readers[k] = {}

    def emit(self, e, fn, reads=(), writes=(), dma=False):
        if dma:
            slot = self.rr[e] % len(self.dma_sems[e])
            self.rr[e] += 1
            kind = ("dma", slot)
            prev = self.cnt[(e, kind)]
            self._deps(e, reads, writes)
            if prev > 0:
                self._wait(e, (e, kind, prev))
            inst = fn(getattr(self.nc, e))
            self.cnt[(e, kind)] = prev + 16
            inst.then_inc(self.dma_sems[e][slot], 16)
            self._record(e, kind, prev + 16, reads, writes)
            return inst
        self._deps(e, reads, writes)
        inst = fn(getattr(self.nc, e))
        self.cnt[(e, "cmp")] += 1
        cnt = self.cnt[(e, "cmp")]
        inst.then_inc(self.sem_cmp[e], 1)
        self._record(e, "cmp", cnt, reads, writes)
        return inst

    def barrier(self):
        for e in ENGINES:
            for (se, kind), c in self.cnt.items():
                if c:
                    self._wait(e, (se, kind, c))

    def dma(self, out, in_, e="sync", **kw):
        self.emit(e, lambda g: g.dma_start(out=out, in_=in_, **kw),
                  reads=[in_], writes=[out], dma=True)

    def mm(self, out, lhsT, rhs, start, stop):
        self.emit("tensor", lambda g: g.matmul(out, lhsT, rhs, start=start, stop=stop),
                  reads=[lhsT, rhs], writes=[out])

    def transpose(self, out, in_, ident):
        self.emit("tensor", lambda g: g.transpose(out, in_, ident),
                  reads=[in_, ident], writes=[out])

    def act(self, out, in_, func, bias=None, scale=None, engine="scalar"):
        kw = {}
        reads = [in_]
        if bias is not None:
            kw["bias"] = bias
            if not isinstance(bias, (int, float)):
                reads.append(bias)
            elif float(bias) not in (0.0, 1.0):
                reads.append(self.ctx.regconst(float(bias)))
        if scale is not None:
            kw["scale"] = scale
            if not isinstance(scale, (int, float)):
                reads.append(scale)
        self.emit(engine, lambda g: g.activation(out, in_, func, **kw),
                  reads=reads, writes=[out])

    def tt(self, out, a, b, op, engine="vector"):
        self.emit(engine, lambda g: g.tensor_tensor(out, a, b, op),
                  reads=[a, b], writes=[out])

    def ts(self, out, in_, s1, op0, s2=None, op1=None, engine="vector"):
        if engine == "gpsimd":
            engine = "vector"
        reads = [in_]
        if not isinstance(s1, (int, float)):
            reads.append(s1)
        if s2 is not None and not isinstance(s2, (int, float)):
            reads.append(s2)
        if s2 is None:
            self.emit(engine, lambda g: g.tensor_scalar(out, in_, s1, None, op0),
                      reads=reads, writes=[out])
        else:
            self.emit(engine, lambda g: g.tensor_scalar(out, in_, s1, s2, op0, op1),
                      reads=reads, writes=[out])

    def stt(self, out, in0, scalar, in1, op0, op1, engine="vector"):
        if engine == "gpsimd":
            engine = "vector"
        reads = [in0, in1]
        if not isinstance(scalar, (int, float)):
            reads.append(scalar)
        self.emit(engine, lambda g: g.scalar_tensor_tensor(out, in0, scalar, in1, op0, op1),
                  reads=reads, writes=[out])

    def red(self, out, in_, op, engine="vector", negate=None):
        kw = {}
        if negate is not None:
            kw["negate"] = negate
        self.emit(engine, lambda g: g.tensor_reduce(out, in_, AX.X, op, **kw),
                  reads=[in_], writes=[out])

    def copy(self, out, in_, engine="vector"):
        if engine == "scalar":
            self.emit(engine, lambda g: g.copy(out, in_), reads=[in_], writes=[out])
        else:
            self.emit(engine, lambda g: g.tensor_copy(out, in_), reads=[in_], writes=[out])

    def memset(self, ap, val, engine="vector"):
        self.emit(engine, lambda g: g.memset(ap, val), reads=[], writes=[ap])

    def recip(self, out, in_):
        self.emit("vector", lambda g: g.reciprocal(out, in_), reads=[in_], writes=[out])

    def max8(self, out, in_):
        self.emit("vector", lambda g: g.max(out=out, in_=in_), reads=[in_], writes=[out])

    def scan(self, out, a, b, initial, op0=ALU.mult, op1=ALU.add):
        reads = [a, b]
        if not isinstance(initial, (int, float)):
            reads.append(initial)
        self.emit("vector", lambda g: g.tensor_tensor_scan(out, a, b, initial, op0, op1),
                  reads=reads, writes=[out])


_NAME_N = [0]


class Ctx:

    def __init__(self, p, stack):
        self.p = p
        self.stack = stack
        self.nc = p.nc
        self.hbm = {}
        self.xres = self.sb("xres", [P, ND * T], F32)
        self.xn_bf = self.sb("xn_bf", [P, ND * T], BF16)
        self.ones_row = self.sb("ones_row", [1, P], F32)
        self.ones_col = self.sb("ones_col", [P, 1], F32)
        self.ones_col_bf = self.sb("ones_col_bf", [P, 1], BF16)
        p.memset(self.ones_row, 1.0)
        p.memset(self.ones_col, 1.0)
        p.memset(self.ones_col_bf, 1.0)

    def sb(self, name, shape, dtype=F32):
        _NAME_N[0] += 1
        return self.stack.enter_context(
            self.nc.sbuf_tensor(f"{name}_{_NAME_N[0]}", shape, dtype))[:]

    def ps(self, name, shape, dtype=F32):
        _NAME_N[0] += 1
        return self.stack.enter_context(
            self.nc.psum_tensor(f"{name}_{_NAME_N[0]}", shape, dtype))[:]

    def xres_t(self, dt):
        return self.xres[:, dt * T:(dt + 1) * T]

    def xn_t(self, dt):
        return self.xn_bf[:, dt * T:(dt + 1) * T]

    def scoped(self, st):
        c = Ctx.__new__(Ctx)
        c.__dict__.update(self.__dict__)
        c.stack = st
        return c

    def regconst(self, v):
        if v not in self._consts:
            i = len(self._consts)
            assert i < 16, "const pool exhausted"
            t = self.cstpool[:, i:i + 1]
            self.p.emit("vector", lambda g: g.memset(t, v), reads=[], writes=[t])
            self.nc.const_aps.register(F32, v, t)
            self._consts[v] = t
        return self._consts[v]

    def setup_consts(self):
        self._consts = {}
        self.stack0 = self.stack
        self.p.ctx = self
        self.cstpool = self.sb("cstpool", [P, 16], F32)
        self.ident_f = self.sb("ident_f", [P, P], F32)
        self.ident_b = self.sb("ident_b", [P, P], BF16)
        self.p.dma(self.ident_f, hin(self, "c_ident_f", np.eye(P, dtype=np.float32)))
        self.p.dma(self.ident_b, hin(self, "c_ident_b", _bf(np.eye(P))))
        self.pp = [self.ps(f"pool{i}", [P, 512], F32) for i in range(6)]
        self.ppb = self.ps("poolbf", [P, 512], BF16)
        self.wstg = self.sb("wstg", [P, 2048], F32)


def _bf(a):
    import ml_dtypes
    return np.asarray(a, dtype=ml_dtypes.bfloat16)



_HOST = {}


def hin(ctx, name, arr, dtype=None):
    if name in ctx.hbm:
        return ctx.hbm[name]
    a = np.ascontiguousarray(arr)
    if dtype is None:
        dtype = F32 if a.dtype == np.float32 else BF16
    ap = ctx.nc.dram_tensor(name, list(a.shape), dtype, kind="ExternalInput").ap()
    ctx.hbm[name] = ap
    _HOST[name] = a
    return ap



def wload(ctx, stack, hbm_ap, d_in, d_out, tag):
    p = ctx.p
    nkt = (d_in + P - 1) // P
    wsb = ctx.sb(f"w_{tag}", [P, nkt * d_out], BF16)
    assert 2 * d_out <= 2048
    stg = ctx.wstg
    for kt in range(nkt):
        rows = min(P, d_in - kt * P)
        s = stg[:, (kt % 2) * d_out:(kt % 2) * d_out + d_out]
        p.dma(s[0:rows, :], hbm_ap[kt * P: kt * P + rows, :])
        p.copy(wsb[0:rows, kt * d_out:(kt + 1) * d_out], s[0:rows, :], engine="gpsimd")
    return wsb


def vload_pp(ctx, hbm_ap, n, tag, ident):
    p = ctx.p
    if n <= P:
        stg = ctx.sb(f"vst_{tag}", [1, n], F32)
        p.dma(stg, hbm_ap.rearrange("(a n) -> a n", a=1) if len(hbm_ap.shape) == 1 else hbm_ap)
        ps = ctx.pp[5][0:n, 0:1]
        p.transpose(ps, stg, ident[0:1, 0:1])
        out = ctx.sb(f"vpp_{tag}", [P, 1], F32)
        p.memset(out, 0.0)
        p.copy(out[0:n, :], ps, engine="scalar")
        return out
    nk = n // P
    assert nk <= P
    stg = ctx.sb(f"vst_{tag}", [nk, P], F32)
    p.dma(stg, hbm_ap.rearrange("(k p) -> k p", p=P))
    ps = ctx.pp[5][:, 0:nk]
    p.transpose(ps, stg, ident[0:nk, 0:nk])
    out = ctx.sb(f"vpp_{tag}", [P, nk], F32)
    p.copy(out, ps, engine="scalar")
    return out


def linear(ctx, wsb, d_in, d_out, in_sb, out_sb, in_cols=T, bias_pp=None,
           func=AF.Identity, psA=None, psB=None, out_scale=None, accum_res=False,
           in_col_off=0, w_col_off=0, w_row_tiles=None, in_T_stride=None,
           scale_pp=None, out_col_off=0):
    p = ctx.p
    nkt = w_row_tiles if w_row_tiles is not None else d_in // P
    nmt = (d_out + P - 1) // P
    wst = wsb.shape[1] // nkt
    its = in_T_stride if in_T_stride is not None else in_cols
    ncol = 512
    ncs = in_cols // ncol
    if psA is None:
        psA = ctx.pp[0]
        psB = ctx.pp[1]
    pss = [psA, psB]
    scr = ctx.sb("lin_scr", [P, ncol], F32) if accum_res else None
    i = 0
    for mt in range(nmt):
        mw = min(P, d_out - mt * P)
        for c in range(ncs):
            ps = pss[i % 2]
            i += 1
            for kt in range(nkt):
                lhs = wsb[:, kt * wst + w_col_off + mt * P: kt * wst + w_col_off + mt * P + mw]
                rhs = in_sb[:, kt * its + in_col_off + c * ncol: kt * its + in_col_off + (c + 1) * ncol]
                p.mm(ps[0:mw, :], lhs, rhs, start=(kt == 0), stop=(kt == nkt - 1))
            b = bias_pp[:, mt:mt + 1] if bias_pp is not None else None
            sc = out_scale if out_scale is not None else 1.0
            if scale_pp is not None:
                sc = scale_pp[:, mt:mt + 1]
            if accum_res:
                p.act(scr, ps[0:mw, :], func, bias=b, scale=sc)
                xr = ctx.xres[:, mt * T + in_col_off + c * ncol: mt * T + in_col_off + (c + 1) * ncol]
                p.tt(xr, xr, scr, ALU.add, engine="vector" if (mt + c) % 2 else "gpsimd")
            else:
                o = out_sb[:, mt * in_cols + out_col_off + c * ncol: mt * in_cols + out_col_off + (c + 1) * ncol]
                p.act(o[0:mw, :] if o.shape[0] > mw else o, ps[0:mw, :], func, bias=b, scale=sc)


def bcast_row(ctx, out_sb, row_ap, ps=None, cols=None):
    p = ctx.p
    n = cols if cols is not None else row_ap.shape[-1]
    if ps is None:
        ps = ctx.pp[5]
    for c0 in range(0, n, 512):
        c1 = min(c0 + 512, n)
        p.mm(ps[:, 0:c1 - c0], ctx.ones_row, row_ap[0:1, c0:c1], start=True, stop=True)
        p.copy(out_sb[:, c0:c1], ps[:, 0:c1 - c0], engine="scalar")


def colsum(ctx, ps_row, tile_bf, start, stop):
    ctx.p.mm(ps_row, ctx.ones_col_bf, tile_bf, start=start, stop=stop)


def softplus(ctx, out, in_, scr1, scr2):
    p = ctx.p
    p.act(scr1, in_, AF.Abs)
    p.act(scr1, scr1, AF.Exp, scale=-1.0)
    p.act(scr1, scr1, AF.Ln, bias=1.0)
    p.act(scr2, in_, AF.Relu)
    p.tt(out, scr1, scr2, ALU.add)


def layernorm(ctx, need_f32=False, keep_stats=False):
    p = ctx.p
    xn_f32 = ctx.sb("ln_xnf", [P, ND * T], F32) if need_f32 else None
    if keep_stats:
        bm = ctx.sb("ln_bm", [P, T], F32)
        br = ctx.sb("ln_br", [P, T], F32)
    with ExitStack() as st:
        c = ctx.scoped(st)
        if not keep_stats:
            bm = c.sb("ln_bm", [P, T], F32)
            br = c.sb("ln_br", [P, T], F32)
        sqc = c.sb("ln_sqc", [P, 512], F32)
        r_sum = c.sb("ln_rs", [1, 512], F32)
        r_rstd = c.sb("ln_rv", [1, 512], F32)
        ps_a = ctx.pp[4][0:1, :]
        ps_b = ctx.pp[5][0:1, :]
        ps_c = ctx.pp[3]
        for cc in range(TC):
            cs = slice(cc * 512, (cc + 1) * 512)
            for dt in range(ND):
                p.mm(ps_a, ctx.ones_col, ctx.xres[:, dt * T + cc * 512: dt * T + (cc + 1) * 512],
                     start=(dt == 0), stop=(dt == ND - 1))
            for dt in range(ND):
                p.act(sqc, ctx.xres[:, dt * T + cc * 512: dt * T + (cc + 1) * 512], AF.Square)
                p.mm(ps_b, ctx.ones_col, sqc, start=(dt == 0), stop=(dt == ND - 1))
            p.ts(r_sum, ps_a, 1.0 / D, ALU.mult)
            p.act(r_rstd, r_sum, AF.Square)
            p.stt(r_rstd, ps_b, 1.0 / D, r_rstd, ALU.mult, ALU.subtract)
            p.act(r_rstd, r_rstd, AF.Ln, bias=1e-5)
            p.act(r_rstd, r_rstd, AF.Exp, scale=-0.5)
            p.mm(ps_c, ctx.ones_row, r_sum, start=True, stop=True)
            p.copy(bm[:, cs], ps_c, engine="scalar")
            p.mm(ps_c, ctx.ones_row, r_rstd, start=True, stop=True)
            p.copy(br[:, cs], ps_c, engine="scalar")
        sq = c.sb("ln_sq", [P, T], F32)
        for dt in range(ND):
            p.tt(sq, ctx.xres_t(dt), bm, ALU.subtract)
            if need_f32:
                xt = xn_f32[:, dt * T:(dt + 1) * T]
                p.tt(xt, sq, br, ALU.mult)
                p.copy(ctx.xn_t(dt), xt, engine="gpsimd")
            else:
                p.tt(ctx.xn_t(dt), sq, br, ALU.mult)
        p.barrier()
    if keep_stats:
        return xn_f32, bm, br
    return xn_f32, None, None



def linear_sbacc(ctx, wsb, nkt, d_out, in_sb, in_cols, out_acc, out_T_stride,
                 first, scr, in_col_off=0, w_col_off=0, wst=None, in_T_stride=None,
                 psA=None, psB=None):
    p = ctx.p
    nmt = d_out // P
    wst = wst if wst is not None else wsb.shape[1] // nkt
    its = in_T_stride if in_T_stride is not None else in_cols
    ncol = 512
    ncs = in_cols // ncol
    pss = [psA, psB]
    i = 0
    for mt in range(nmt):
        for c in range(ncs):
            ps = pss[i % 2]; i += 1
            for kt in range(nkt):
                lhs = wsb[:, kt * wst + w_col_off + mt * P: kt * wst + w_col_off + (mt + 1) * P]
                rhs = in_sb[:, kt * its + in_col_off + c * ncol: kt * its + in_col_off + (c + 1) * ncol]
                p.mm(ps, lhs, rhs, start=(kt == 0), stop=(kt == nkt - 1))
            o = out_acc[:, mt * out_T_stride + c * ncol: mt * out_T_stride + (c + 1) * ncol]
            if first:
                p.copy(o, ps, engine="scalar")
            else:
                p.copy(scr, ps, engine="scalar")
                eng = "vector" if (i % 2) else "gpsimd"
                p.tt(o, o, scr, ALU.add, engine=eng)


def residual_add(ctx, src_sb, scale_pp=None, src_stride=T, nmt=ND):
    p = ctx.p
    for mt in range(nmt):
        xr = ctx.xres_t(mt)
        s = src_sb[:, mt * src_stride:(mt + 1) * src_stride]
        eng = "vector" if mt % 2 else "gpsimd"
        if scale_pp is not None:
            p.stt(xr, s, scale_pp[:, mt:mt + 1], xr, ALU.mult, ALU.add, engine=eng)
        else:
            p.tt(xr, xr, s, ALU.add, engine=eng)



def input_transpose(ctx, x0_ap):
    p = ctx.p
    with ExitStack() as st:
        c = ctx.scoped(st)
        _input_transpose(c, x0_ap)
        p.barrier()


def _input_transpose(ctx, x0_ap):
    p = ctx.p
    stg = [ctx.sb(f"itr{i}", [P, D], F32) for i in range(2)]
    pss = [ctx.pp[0][:, 0:P], ctx.pp[1][:, 0:P]]
    for tt in range(NT):
        s = stg[tt % 2]
        p.dma(s, x0_ap[tt * P:(tt + 1) * P, :])
        for dt in range(ND):
            ps = pss[dt % 2]
            p.transpose(ps, s[:, dt * P:(dt + 1) * P], ctx.ident_f)
            p.copy(ctx.xres[:, dt * T + tt * P: dt * T + (tt + 1) * P], ps, engine="scalar")


def output_transpose(ctx, xout_ap):
    p = ctx.p
    with ExitStack() as st:
        c = ctx.scoped(st)
        _output_transpose(c, xout_ap)
        p.barrier()


def _output_transpose(ctx, xout_ap):
    p = ctx.p
    stg = [ctx.sb(f"otr{i}", [P, D], F32) for i in range(2)]
    pss = [ctx.pp[0][:, 0:P], ctx.pp[1][:, 0:P]]
    for tt in range(NT):
        s = stg[tt % 2]
        for dt in range(ND):
            ps = pss[dt % 2]
            p.transpose(ps, ctx.xres[:, dt * T + tt * P: dt * T + (tt + 1) * P], ctx.ident_f)
            p.copy(s[:, dt * P:(dt + 1) * P], ps, engine="scalar")
        p.dma(xout_ap[tt * P:(tt + 1) * P, :], s)


def fm_dma_in(ctx, xin_ap):
    for dt in range(ND):
        ctx.p.dma(ctx.xres_t(dt), xin_ap[dt * P:(dt + 1) * P, :])


def fm_dma_out(ctx, xout_ap):
    for dt in range(ND):
        ctx.p.dma(xout_ap[dt * P:(dt + 1) * P, :], ctx.xres_t(dt))



def st_l2_ffn(ctx, ins, uid):
    p = ctx.p
    H = 4096
    w1T = hin(ctx, "l2f_w1T", ins["L2_ffn_w1"].T)
    w2T = hin(ctx, "l2f_w2T", ins["L2_ffn_w2"].T)
    b1 = hin(ctx, "l2f_b1", ins["L2_ffn_b1"])
    b2 = hin(ctx, "l2f_b2", ins["L2_ffn_b2"])
    with ExitStack() as st:
        c = ctx.scoped(st)
        _, bm, br = layernorm(c)
        b1pp = load_pp(c, "l2f_b1pp", ins["L2_ffn_b1"])
        b2pp = load_pp(c, "l2f_b2pp", ins["L2_ffn_b2"])
        stg = c.sb("bstg", [P, 1024], F32)
        ast = c.sb("bast", [P, 1024], F32)
        a1row = c.sb("a1row", [1, H], F32)
        a2row = c.sb("a2row", [1, D], F32)
        psr = [c.pp[2][0:1, :], c.pp[3][0:1, :]]
        for hc in range(4):
            for kt in range(ND):
                p.dma(stg, w1T[kt * P:(kt + 1) * P, hc * 1024:(hc + 1) * 1024])
                p.act(ast, stg, AF.Abs)
                for cc in range(2):
                    p.mm(psr[cc], ctx.ones_col, ast[:, cc * 512:(cc + 1) * 512],
                         start=(kt == 0), stop=(kt == ND - 1))
            for cc in range(2):
                p.copy(a1row[:, hc * 1024 + cc * 512: hc * 1024 + (cc + 1) * 512], psr[cc], engine="scalar")
        for kt in range(32):
            p.dma(stg, w2T[kt * P:(kt + 1) * P, :])
            p.act(ast, stg, AF.Abs)
            for cc in range(2):
                p.mm(psr[cc], ctx.ones_col, ast[:, cc * 512:(cc + 1) * 512],
                     start=(kt == 0), stop=(kt == 31))
        for cc in range(2):
            p.copy(a2row[:, cc * 512:(cc + 1) * 512], psr[cc], engine="scalar")
        p.ts(a1row, a1row, 1.0 / D, ALU.mult)
        p.ts(a2row, a2row, 1.0 / H, ALU.mult)
        dsc = ctx.nc.dram_tensor(f"l2f_dsc{uid}", [H + D], F32, kind="Internal").ap()
        p.dma(dsc.rearrange("(a b) -> a b", a=1)[0:1, 0:H], a1row)
        p.dma(dsc.rearrange("(a b) -> a b", a=1)[0:1, H:H + D], a2row)
        a1pp = vload_pp(c, dsc[0:H], H, f"a1pp{uid}", ctx.ident_f)
        a2pp = vload_pp(c, dsc[H:H + D], D, f"a2pp{uid}", ctx.ident_f)
        w1s = c.sb("w1s", [P, ND * 1024], BF16)
        w2s = c.sb("w2s", [P, 8 * 1024], BF16)
        hbf = c.sb("hbf", [P, 8 * 1024], BF16)
        oacc = c.sb("oacc", [P, ND * 512], F32)
        scr = c.sb("bscr", [P, 512], F32)
        psA = c.pp[0]
        psB = c.pp[1]
        for tq in range(4):
            toff = tq * 512
            for hc in range(4):
                for kt in range(ND):
                    p.dma(stg, w1T[kt * P:(kt + 1) * P, hc * 1024:(hc + 1) * 1024])
                    p.act(w1s[:, kt * 1024:(kt + 1) * 1024], stg, AF.Sign)
                for mt in range(8):
                    ps = psA if mt % 2 == 0 else psB
                    for kt in range(ND):
                        p.mm(ps, w1s[:, kt * 1024 + mt * P: kt * 1024 + (mt + 1) * P],
                             ctx.xn_bf[:, kt * T + toff: kt * T + toff + 512],
                             start=(kt == 0), stop=(kt == ND - 1))
                    gi = hc * 8 + mt
                    p.act(hbf[:, mt * 1024: mt * 1024 + 512],
                          ps, AF.Gelu,
                          bias=b1pp[:, gi:gi + 1], scale=a1pp[:, gi:gi + 1])
                for kt in range(8):
                    p.dma(stg, w2T[(hc * 8 + kt) * P:(hc * 8 + kt + 1) * P, :])
                    p.act(w2s[:, kt * 1024:(kt + 1) * 1024], stg, AF.Sign)
                linear_sbacc(ctx, w2s, 8, D, hbf, 512, oacc, 512,
                             first=(hc == 0), scr=scr, in_T_stride=1024,
                             psA=psA, psB=psB)
            for mt in range(ND):
                p.ts(oacc[:, mt * 512:(mt + 1) * 512], oacc[:, mt * 512:(mt + 1) * 512],
                     a2pp[:, mt:mt + 1], ALU.mult, b2pp[:, mt:mt + 1], ALU.add,
                     engine="vector" if mt % 2 else "gpsimd")
                xr = ctx.xres[:, mt * T + toff: mt * T + toff + 512]
                p.tt(xr, xr, oacc[:, mt * 512:(mt + 1) * 512], ALU.add,
                     engine="gpsimd" if mt % 2 else "vector")
        p.barrier()



def st_l3_ffn(ctx, ins, uid):
    p = ctx.p
    names = ["cn_w1", "cn_b1", "cn_w2", "cn_b2", "cn_w3", "cn_b3", "cn_w4", "cn_b4",
             "gc_w1", "gc_b1", "gc_w2", "gc_b2", "gc_w3", "gc_b3", "adaptive_range"]
    h = {n: hin(ctx, f"l3f_{n}", ins[f"L3_ffn_{n}"]) for n in names}
    with ExitStack() as st:
        c = ctx.scoped(st)
        xn, bm, br = layernorm(c, need_f32=True)
        w1 = wload(c, st, h["cn_w1"], D, D, f"cw1{uid}")
        w2 = wload(c, st, h["cn_w2"], D, 512, f"cw2{uid}")
        w3 = wload(c, st, h["cn_w3"], 512, 256, f"cw3{uid}")
        w4 = wload(c, st, h["cn_w4"], 256, 1, f"cw4{uid}")
        b1pp = load_pp(c, "l3f_b1pp", ins["L3_ffn_cn_b1"])
        b2pp = load_pp(c, "l3f_b2pp", ins["L3_ffn_cn_b2"])
        b3pp = load_pp(c, "l3f_b3pp", ins["L3_ffn_cn_b3"])
        b4pp = load_pp(c, "l3f_b4pp", ins["L3_ffn_cn_b4"])
        arsb = load_pp(c, "l3f_arpp", ins["L3_ffn_adaptive_range"])
        gfs = c.sb("gfs", [P, 1], F32)
        ar2 = c.sb("ar2", [P, 1], F32)
        psA = c.pp[0]
        psB = c.pp[1]
        with ExitStack() as st_g:
            cg = c.scoped(st_g)
            g1 = wload(cg, st_g, h["gc_w1"], D, 512, f"gw1{uid}")
            g2 = wload(cg, st_g, h["gc_w2"], 512, 256, f"gw2{uid}")
            g3 = wload(cg, st_g, h["gc_w3"], 256, 1, f"gw3{uid}")
            gb1pp = load_pp(cg, "l3f_gb1pp", ins["L3_ffn_gc_b1"])
            gb2pp = load_pp(cg, "l3f_gb2pp", ins["L3_ffn_gc_b2"])
            gb3pp = load_pp(cg, "l3f_gb3pp", ins["L3_ffn_gc_b3"])
            gc = cg.sb("gc", [P, ND], F32)
            gcb = cg.sb("gcb", [P, ND], BF16)
            gh1 = cg.sb("gh1", [P, 4], BF16)
            gh2 = cg.sb("gh2", [P, 2], BF16)
            for dt in range(ND):
                p.red(gc[:, dt:dt + 1], xn[:, dt * T:(dt + 1) * T], ALU.add)
            p.ts(gc, gc, 1.0 / T, ALU.mult)
            p.copy(gcb, gc)
            for mt in range(4):
                for kt in range(ND):
                    p.mm(psA[:, 0:1], g1[:, kt * 512 + mt * P: kt * 512 + (mt + 1) * P],
                         gcb[:, kt:kt + 1], start=(kt == 0), stop=(kt == ND - 1))
                p.act(gh1[:, mt:mt + 1], psA[:, 0:1], AF.Gelu, bias=gb1pp[:, mt:mt + 1])
            for mt in range(2):
                for kt in range(4):
                    p.mm(psA[:, 0:1], g2[:, kt * 256 + mt * P: kt * 256 + (mt + 1) * P],
                         gh1[:, kt:kt + 1], start=(kt == 0), stop=(kt == 3))
                p.act(gh2[:, mt:mt + 1], psA[:, 0:1], AF.Gelu, bias=gb2pp[:, mt:mt + 1])
            for kt in range(2):
                p.mm(psA[0:1, 0:1], g3[:, kt * 1: kt * 1 + 1], gh2[:, kt:kt + 1],
                     start=(kt == 0), stop=(kt == 1))
            p.act(gfs[0:1, :], psA[0:1, 0:1], AF.Sigmoid, bias=gb3pp[0:1, 0:1])
            p.ts(gfs[0:1, :], gfs[0:1, :], 0.7, ALU.mult, -0.5, ALU.add)
            p.ts(ar2, arsb, 2.0, ALU.mult)
            p.barrier()
        out = xn
        outb = ctx.xn_bf
        hbuf = c.sb("hbuf", [P, ND * 512], BF16)
        h1b = hbuf
        h2b = c.sb("h2b", [P, 4 * 512], BF16)
        h3b = hbuf[:, 0:2 * 512]
        cpl = ctx.wstg[:, 0:512]
        t1 = ctx.wstg[:, 512:1024]
        snap = ctx.wstg[:, 1024:1536]
        tfr = ctx.wstg[0:1, 1536:2048]
        HT = 512
        pairs = [(0, 4, 1.0, -1.0), (1, 5, 1.0, -1.0),
                 (2, 6, -1.0, 1.0), (3, 7, -1.0, 1.0)]
        for cyc in range(3):
            for th in range(4):
                toff = th * HT
                linear(c, w1, D, D, outb, h1b, in_cols=HT, in_col_off=toff,
                       in_T_stride=T, bias_pp=b1pp, func=AF.Gelu, psA=psA, psB=psB)
                linear(c, w2, D, 512, h1b, h2b, in_cols=HT, bias_pp=b2pp,
                       func=AF.Gelu, psA=psA, psB=psB)
                linear(c, w3, 512, 256, h2b, h3b, in_cols=HT, bias_pp=b3pp,
                       func=AF.Gelu, psA=psA, psB=psB)
                for kt in range(2):
                    p.mm(psA[0:1, :], w4[:, kt:kt + 1],
                         h3b[:, kt * HT: kt * HT + 512],
                         start=(kt == 0), stop=(kt == 1))
                p.act(tfr, psA[0:1, :], AF.Sigmoid, bias=b4pp[0:1, 0:1])
                p.ts(tfr, tfr, 0.3, ALU.mult, gfs[0:1, 0:1], ALU.add)
                p.ts(tfr, tfr, ar2[0:1, 0:1], ALU.mult, 0.1, ALU.add)
                bcast_row(c, cpl, tfr, ps=psB, cols=HT)
                for (a, b, sa, sgb) in pairs:
                    oa = out[:, a * T + toff: a * T + toff + HT]
                    ob = out[:, b * T + toff: b * T + toff + HT]
                    p.copy(snap, oa, engine="gpsimd")
                    p.tt(t1, cpl, ob, ALU.mult)
                    p.stt(oa, t1, sa, oa, ALU.mult, ALU.add)
                    p.tt(t1, cpl, snap, ALU.mult)
                    p.stt(ob, t1, sgb, ob, ALU.mult, ALU.add, engine="gpsimd")
                if cyc < 2:
                    for dt in range(ND):
                        p.copy(outb[:, dt * T + toff: dt * T + toff + HT],
                               out[:, dt * T + toff: dt * T + toff + HT],
                               engine="gpsimd" if dt % 2 else "vector")
        residual_add(ctx, out)
        p.barrier()



def pp_host(vec):
    v = np.asarray(vec, dtype=np.float32).reshape(-1)
    n = v.shape[0]
    if n < P:
        out = np.zeros((P, 1), np.float32)
        out[:n, 0] = v
        return out
    return np.ascontiguousarray(v.reshape(n // P, P).T)


def load_pp(ctx, name, vec):
    h = hin(ctx, name, pp_host(vec))
    sb = ctx.sb(f"pp_{name}", list(h.shape), F32)
    ctx.p.dma(sb, h)
    return sb


def rope_apply2(c, x_f32, rows, rotT_bf, cosE, sinE, out_bf, psA, scr_bf, scr_f, scr_f2):
    p = c.p
    for cc in range(TC):
        cs = slice(cc * 512, (cc + 1) * 512)
        p.copy(scr_bf[0:rows, 0:512], x_f32[0:rows, cs])
        p.mm(psA[0:rows, :], rotT_bf[0:rows, 0:rows], scr_bf[0:rows, 0:512],
             start=True, stop=True)
        p.copy(scr_f[0:rows, :], psA[0:rows, :], engine="scalar")
        p.tt(scr_f[0:rows, :], scr_f[0:rows, :], sinE[0:rows, cs], ALU.mult)
        p.tt(scr_f2[0:rows, :], x_f32[0:rows, cs], cosE[0:rows, cs], ALU.mult,
             engine="gpsimd")
        p.tt(out_bf[0:rows, cs], scr_f[0:rows, :], scr_f2[0:rows, :], ALU.add)


def attn_group(c, heads, dv, vtok, vst, jts_fn, mask_fn, scale, clip, out_fn,
               psS, psO, psD, psB2, s_f, pb, den, rc, rcb, o_f):
    p = c.p
    for (parts, h_id, voff) in heads:
        for ic in range(TC):
            jts = jts_fn(ic)
            nj = len(jts)
            for ji, jt in enumerate(jts):
                npart = len(parts)
                for pi, (kap, qap, dk) in enumerate(parts):
                    p.mm(psS, kap[0:dk, jt * P:(jt + 1) * P],
                         qap[0:dk, ic * 512:(ic + 1) * 512],
                         start=(pi == 0), stop=(pi == npart - 1))
                mask = mask_fn(ic, jt)
                src = psS
                if clip is not None:
                    p.ts(s_f, psS, clip, ALU.min, -clip, ALU.max)
                    src = s_f
                if mask is not None:
                    if src is psS:
                        p.tt(s_f, psS, mask, ALU.add)
                    else:
                        p.tt(s_f, s_f, mask, ALU.add)
                    src = s_f
                pbt = pb[ji % 2]
                p.act(pbt, src, AF.Exp, scale=scale)
                colsum(c, psD, pbt, start=(ji == 0), stop=(ji == nj - 1))
                p.mm(psO[0:dv, :], vtok[:, jt * vst + voff: jt * vst + voff + dv],
                     pbt, start=(ji == 0), stop=(ji == nj - 1))
            p.copy(den, psD, engine="scalar")
            p.recip(rc, den)
            p.mm(psB2, c.ones_row, rc, start=True, stop=True)
            p.copy(rcb, psB2, engine="scalar")
            p.copy(o_f[0:dv, :], psO[0:dv, :], engine="scalar")
            p.tt(o_f[0:dv, :], o_f[0:dv, :], rcb[0:dv, :], ALU.mult)
            out_fn(h_id, ic, o_f)


def make_attn_scratch(c, tag):
    psS = c.pp[2]
    psO = c.pp[3]
    psD = c.pp[4][0:1, :]
    psB2 = c.pp[5]
    s_f = c.sb(f"asf_{tag}", [P, 512], F32)
    pb = [c.sb(f"apb{i}_{tag}", [P, 512], BF16) for i in range(2)]
    den = c.sb(f"aden_{tag}", [1, 512], F32)
    rc = c.sb(f"arc_{tag}", [1, 512], F32)
    rcb = c.sb(f"arcb_{tag}", [P, 512], F32)
    o_f = c.sb(f"aof_{tag}", [P, 512], F32)
    return dict(psS=psS, psO=psO, psD=psD, psB2=psB2, s_f=s_f, pb=pb, den=den,
                rc=rc, rcb=rcb, o_f=o_f)


def causal_mask_host(q):
    jl = np.arange(P)[:, None]
    il = np.arange(512)[None, :]
    j = q * P + jl
    return np.where(j > il, -30000.0, 0.0).astype(np.float32)


def window_mask_host(q, w=512):
    jl = np.arange(P)[:, None]
    il = np.arange(512)[None, :]
    j = q * P + jl
    d = il - j
    return np.where((d < 0) | (d > w - 1), -30000.0, 0.0).astype(np.float32)


def rope_tables_l0b():
    half = 32
    theta = 1.0 / 10000.0 ** (np.arange(half, dtype=np.float64) / half)
    pos = np.arange(T, dtype=np.float64)
    fr = pos[:, None] * theta[None, :]
    cos = np.cos(fr).T
    sin = np.sin(fr).T
    cosE = np.concatenate([cos, cos], 0)
    sinE = np.concatenate([sin, sin], 0)
    R = np.zeros((64, 64), np.float64)
    for i in range(32):
        R[i, i + 32] = -1.0
        R[i + 32, i] = 1.0
    return cosE, sinE, R


def rope_tables_l2b():
    n = 32
    j = np.arange(n, dtype=np.float64)
    freqs = 1.0 * (40.0 / 1.0) ** (j / (n - 1))
    pos = np.arange(T, dtype=np.float64) / T
    th = 2.0 * np.pi * pos[:, None] * freqs[None, :]
    cos = np.cos(th).T
    sin = np.sin(th).T
    cosE = np.concatenate([cos, cos], 0)
    sinE = np.concatenate([sin, sin], 0)
    R = np.zeros((64, 64), np.float64)
    for i in range(32):
        R[i, i + 32] = -1.0
        R[i + 32, i] = 1.0
    return cosE, sinE, R


def rope_tables_l1b():
    d = 32
    inv = 10000.0 ** (-np.arange(0, d, 2, dtype=np.float64) / d)
    ang = np.arange(T, dtype=np.float64)[:, None] * inv[None, :]
    cos = np.cos(ang).T
    sin = np.sin(ang).T
    cosE = np.repeat(cos, 2, axis=0)
    sinE = np.repeat(sin, 2, axis=0)
    R = np.zeros((32, 32), np.float64)
    for k in range(16):
        R[2 * k, 2 * k + 1] = -1.0
        R[2 * k + 1, 2 * k] = 1.0
    return cosE, sinE, R


def _tile4(a, reps):
    return np.ascontiguousarray(np.tile(a, (reps, 1)).astype(np.float32))


def _blockdiag(R, reps):
    n = R.shape[0]
    out = np.zeros((n * reps, n * reps), np.float64)
    for i in range(reps):
        out[i * n:(i + 1) * n, i * n:(i + 1) * n] = R
    return out



def proj_fm_group(c, wsb, bias_pp, bi0, dst_f32, nkt=ND, wst=512, wcol0=0):
    p = c.p
    for cc in range(TC):
        ps = c.pp[0] if cc % 2 == 0 else c.pp[1]
        for kt in range(nkt):
            p.mm(ps, wsb[:, kt * wst + wcol0: kt * wst + wcol0 + P],
                 c.xn_bf[:, kt * T + cc * 512: kt * T + (cc + 1) * 512],
                 start=(kt == 0), stop=(kt == nkt - 1))
        b = bias_pp[:, bi0:bi0 + 1] if bias_pp is not None else None
        p.act(dst_f32[:, cc * 512:(cc + 1) * 512], ps, AF.Identity, bias=b)


def wo_from_dram(c, odram, woT_hbm, bias_pp, uid, d_in=D):
    p = c.p
    nkt = d_in // P
    with ExitStack() as st:
        c2 = c.scoped(st)
        wo = wload(c2, st, woT_hbm, d_in, D, f"wo{uid}")
        ostg = c2.sb(f"ostg{uid}", [P, nkt * 512], BF16)
        scr = c2.sb(f"oscr{uid}", [P, 512], F32)
        for cc in range(TC):
            for kt in range(nkt):
                p.dma(ostg[:, kt * 512:(kt + 1) * 512],
                      odram[kt * P:(kt + 1) * P, cc * 512:(cc + 1) * 512])
            for mt in range(ND):
                ps = c.pp[0] if mt % 2 == 0 else c.pp[1]
                for kt in range(nkt):
                    p.mm(ps, wo[:, kt * D + mt * P: kt * D + (mt + 1) * P],
                         ostg[:, kt * 512:(kt + 1) * 512],
                         start=(kt == 0), stop=(kt == nkt - 1))
                b = bias_pp[:, mt:mt + 1] if bias_pp is not None else None
                p.act(scr, ps, AF.Identity, bias=b)
                xr = c.xres[:, mt * T + cc * 512: mt * T + (cc + 1) * 512]
                p.tt(xr, xr, scr, ALU.add, engine="vector" if mt % 2 else "gpsimd")
        p.barrier()


def st_l2_attn_b(ctx, ins, uid):
    p = ctx.p
    qkvT = hin(ctx, "l2b_qkvT", ins["L2_attn_b_qkv_w"].T)
    projT = hin(ctx, "l2b_projT", ins["L2_attn_b_proj_w"].T)
    cosE, sinE, R = rope_tables_l2b()
    cosT = hin(ctx, "l2b_cos", _tile4(cosE, 2))
    sinT = hin(ctx, "l2b_sin", _tile4(sinE, 2))
    rotT = hin(ctx, "l2b_rot", _bf(_blockdiag(R, 2).T))
    odram = ctx.nc.dram_tensor(f"l2b_od{uid}", [D, T], BF16, kind="Internal").ap()
    with ExitStack() as st:
        c = ctx.scoped(st)
        layernorm(c)
        qb_pp = load_pp(c, "l2b_qb", ins["L2_attn_b_qkv_b"][0:1024])
        kb_pp = load_pp(c, "l2b_kb", ins["L2_attn_b_qkv_b"][1024:2048])
        pjb_pp = load_pp(c, "l2b_pjb", ins["L2_attn_b_proj_b"])
        vb_row = hin(ctx, "l2b_vb_row",
                     np.ascontiguousarray(ins["L2_attn_b_qkv_b"][None, 2048:3072].astype(np.float32)))
        cosS = c.sb("l2b_cosS", [P, T], F32); p.dma(cosS, cosT)
        sinS = c.sb("l2b_sinS", [P, T], F32); p.dma(sinS, sinT)
        rotS = c.sb("l2b_rotS", [P, P], BF16); p.dma(rotS, rotT)
        vbr = c.sb("l2b_vbr", [1, 1024], F32); p.dma(vbr, vb_row)
        vbb = c.sb("l2b_vbb", [P, 1024], F32)
        bcast_row(c, vbb, vbr)
        for g4 in range(4):
            with ExitStack() as st2:
                c2 = c.scoped(st2)
                col0 = g4 * 256
                wq = wload(c2, st2, qkvT[:, col0:col0 + 256], D, 256, f"l2bq{g4}{uid}")
                wk = wload(c2, st2, qkvT[:, 1024 + col0:1024 + col0 + 256], D, 256, f"l2bk{g4}{uid}")
                wv = wload(c2, st2, qkvT[:, 2048 + col0:2048 + col0 + 256], D, 256, f"l2bv{g4}{uid}")
                qf = c2.sb("l2b_qf", [P, T], F32)
                sbf = c2.sb("l2b_sbf", [P, 512], BF16)
                sf1 = c2.sb("l2b_sf1", [P, 512], F32)
                sf2 = c2.sb("l2b_sf2", [P, 512], F32)
                qro = c2.sb("l2b_qro", [P, 2 * T], BF16)
                kro = c2.sb("l2b_kro", [P, 2 * T], BF16)
                vtok = c2.sb("l2b_vtok", [P, NT * 256], BF16)
                for gg in range(2):
                    gi = g4 * 2 + gg
                    for (w, bpp, dst) in ((wq, qb_pp, qro), (wk, kb_pp, kro)):
                        proj_fm_group(c2, w, bpp, gi, qf, wst=256, wcol0=gg * P)
                        rope_apply2(c2, qf, P, rotS, cosS, sinS,
                                    dst[:, gg * T:(gg + 1) * T], c2.pp[0], sbf, sf1, sf2)
                for tt in range(NT):
                    ps = c2.pp[0] if tt % 2 == 0 else c2.pp[1]
                    for kt in range(ND):
                        p.mm(ps[:, 0:256], ctx.xn_bf[:, kt * T + tt * P: kt * T + (tt + 1) * P],
                             wv[:, kt * 256:(kt + 1) * 256],
                             start=(kt == 0), stop=(kt == ND - 1))
                    p.copy(sf1[:, 0:256], ps[:, 0:256], engine="scalar")
                    p.tt(vtok[:, tt * 256:(tt + 1) * 256], sf1[:, 0:256],
                         vbb[:, col0:col0 + 256], ALU.add)
                sc = make_attn_scratch(c2, f"l2b{g4}")
                heads = []
                for hl in range(4):
                    tq = (hl * 64) // P
                    r0 = (hl * 64) % P
                    kap = kro[r0:r0 + 64, tq * T:(tq + 1) * T]
                    qap = qro[r0:r0 + 64, tq * T:(tq + 1) * T]
                    heads.append(([(kap, qap, 64)], g4 * 4 + hl, hl * 64))
                obf = c2.sb("l2b_obf", [64, 512], BF16)

                def out_fn(h_id, ic, o_f):
                    p.copy(obf, o_f[0:64, :])
                    p.dma(odram[h_id * 64:(h_id + 1) * 64, ic * 512:(ic + 1) * 512], obf)
                attn_group(c2, heads, 64, vtok, 256, lambda ic: list(range(NT)),
                           lambda ic, jt: None, 0.125, None, out_fn, **sc)
                p.barrier()
        wo_from_dram(c, odram, projT, pjb_pp, f"l2b{uid}")
        p.barrier()


def st_l2_attn_a(ctx, ins, uid):
    p = ctx.p
    I = lambda n: ins[f"L2_attn_a_{n}"]
    wdq = hin(ctx, "l2a_wdq", I("W_DQ"))
    wdkv = hin(ctx, "l2a_wdkv", I("W_DKV"))
    wuq = hin(ctx, "l2a_wuq", I("W_UQ"))
    wuk = hin(ctx, "l2a_wuk", I("W_UK"))
    wuv = hin(ctx, "l2a_wuv", I("W_UV"))
    wqr = hin(ctx, "l2a_wqr", I("W_QR"))
    wkr = hin(ctx, "l2a_wkr", I("W_KR"))
    woT = hin(ctx, "l2a_wo", I("W_O"))
    odram = ctx.nc.dram_tensor(f"l2a_od{uid}", [D, T], BF16, kind="Internal").ap()
    with ExitStack() as st:
        c = ctx.scoped(st)
        layernorm(c)
        bdq = load_pp(c, "l2a_bdq", I("b_DQ"))
        bdkv = load_pp(c, "l2a_bdkv", I("b_DKV"))
        buq = load_pp(c, "l2a_buq", I("b_UQ"))
        buk = load_pp(c, "l2a_buk", I("b_UK"))
        buv_row = hin(ctx, "l2a_buv_row", np.ascontiguousarray(I("b_UV")[None, :].astype(np.float32)))
        bqr = load_pp(c, "l2a_bqr", I("b_QR"))
        bkr = load_pp(c, "l2a_bkr", I("b_KR"))
        bo = load_pp(c, "l2a_bo", I("b_O"))
        cq = c.sb("l2a_cq", [32, T], BF16)
        ckv = c.sb("l2a_ckv", [32, T], BF16)
        kr = c.sb("l2a_kr", [32, T], BF16)
        wdq_s = wload(c, st, wdq, D, 32, f"l2a_dq{uid}")
        wdkv_s = wload(c, st, wdkv, D, 32, f"l2a_dkv{uid}")
        wkr_s = wload(c, st, wkr, D, 32, f"l2a_kr{uid}")
        for (wsb, bpp, dst) in ((wdq_s, bdq, cq), (wdkv_s, bdkv, ckv), (wkr_s, bkr, kr)):
            for cc in range(TC):
                ps = c.pp[0] if cc % 2 == 0 else c.pp[1]
                for kt in range(ND):
                    p.mm(ps[0:32, :], wsb[:, kt * 32: kt * 32 + 32],
                         ctx.xn_bf[:, kt * T + cc * 512: kt * T + (cc + 1) * 512],
                         start=(kt == 0), stop=(kt == ND - 1))
                p.act(dst[:, cc * 512:(cc + 1) * 512], ps[0:32, :], AF.Identity,
                      bias=bpp[0:32, 0:1])
        wuq_s = wload(c, st, wuq, 32, 1024, f"l2a_uq{uid}")
        wuk_s = wload(c, st, wuk, 32, 1024, f"l2a_uk{uid}")
        wuv_s = wload(c, st, wuv, 32, 1024, f"l2a_uv{uid}")
        wqr_s = wload(c, st, wqr, 32, 256, f"l2a_qr{uid}")
        qr = c.sb("l2a_qr", [P, 2 * T], BF16)
        buvr = c.sb("l2a_buvr", [1, 1024], F32); p.dma(buvr, buv_row)
        buvb = c.sb("l2a_buvb", [P, 1024], F32)
        bcast_row(c, buvb, buvr)
        for mt in range(2):
            for cc in range(TC):
                ps = c.pp[0] if cc % 2 == 0 else c.pp[1]
                p.mm(ps, wqr_s[:, mt * P:(mt + 1) * P][0:32, :], cq[:, cc * 512:(cc + 1) * 512],
                     start=True, stop=True)
                p.act(qr[:, mt * T + cc * 512: mt * T + (cc + 1) * 512], ps, AF.Identity,
                      bias=bqr[:, mt:mt + 1])
        qrx = c.sb("l2a_qrx", [64, 2 * T], BF16)
        for tile in range(2):
            p.dma(qrx[:, tile * T:(tile + 1) * T], qr[64:128, tile * T:(tile + 1) * T])
        kr3 = c.sb("l2a_kr3", [96, T], BF16)
        for b in range(3):
            p.dma(kr3[b * 32:(b + 1) * 32, :], kr)
        for g2 in range(4):
            with ExitStack() as st2:
                c2 = c.scoped(st2)
                qc = c2.sb("l2a_qc", [P, 2 * T], BF16)
                kc = c2.sb("l2a_kc", [P, 2 * T], BF16)
                vtok = c2.sb("l2a_vt", [P, NT * 256], BF16)
                sf1 = c2.sb("l2a_sf1", [P, 512], F32)
                for mt in range(2):
                    col = g2 * 256 + mt * P
                    for cc in range(TC):
                        ps = c2.pp[0] if cc % 2 == 0 else c2.pp[1]
                        p.mm(ps, wuq_s[:, col:col + P][0:32, :], cq[:, cc * 512:(cc + 1) * 512],
                             start=True, stop=True)
                        p.act(qc[:, mt * T + cc * 512: mt * T + (cc + 1) * 512], ps,
                              AF.Identity, bias=buq[:, g2 * 2 + mt: g2 * 2 + mt + 1])
                        ps2 = c2.pp[2]
                        p.mm(ps2, wuk_s[:, col:col + P][0:32, :], ckv[:, cc * 512:(cc + 1) * 512],
                             start=True, stop=True)
                        p.act(kc[:, mt * T + cc * 512: mt * T + (cc + 1) * 512], ps2,
                              AF.Identity, bias=buk[:, g2 * 2 + mt: g2 * 2 + mt + 1])
                for tt in range(NT):
                    ps = c2.pp[0] if tt % 2 == 0 else c2.pp[1]
                    p.mm(ps[:, 0:256], ckv[:, tt * P:(tt + 1) * P],
                         wuv_s[0:32, g2 * 256:(g2 + 1) * 256], start=True, stop=True)
                    p.copy(sf1[:, 0:256], ps[:, 0:256], engine="scalar")
                    p.tt(vtok[:, tt * 256:(tt + 1) * 256], sf1[:, 0:256],
                         buvb[:, g2 * 256:(g2 + 1) * 256], ALU.add)
                sc = make_attn_scratch(c2, f"l2a{g2}")
                heads = []
                for hl in range(2):
                    h_id = g2 * 2 + hl
                    kcap = kc[:, hl * T:(hl + 1) * T]
                    qcap = qc[:, hl * T:(hl + 1) * T]
                    rr = h_id * 32
                    tile, off = rr // P, rr % P
                    if off < 64:
                        qrap = qr[off:off + 32, tile * T:(tile + 1) * T]
                        krap = kr3[off:off + 32, :]
                    else:
                        qrap = qrx[off - 64:off - 32, tile * T:(tile + 1) * T]
                        krap = kr3[off - 64:off - 32, :]
                    heads.append(([(kcap, qcap, 128), (krap, qrap, 32)], h_id, hl * 128))
                obf = c2.sb("l2a_obf", [P, 512], BF16)

                def out_fn(h_id, ic, o_f):
                    p.copy(obf, o_f[0:P, :])
                    p.dma(odram[h_id * 128:(h_id + 1) * 128, ic * 512:(ic + 1) * 512], obf)
                attn_group(c2, heads, 128, vtok, 256, lambda ic: list(range(NT)),
                           lambda ic, jt: None, 1.0 / np.sqrt(160.0), None, out_fn, **sc)
                p.barrier()
        wo_from_dram(c, odram, woT, bo, f"l2a{uid}")
        p.barrier()


def st_l1_attn_b(ctx, ins, uid):
    p = ctx.p
    I = lambda n: ins[f"L1_attn_b_{n}"]
    wdq = hin(ctx, "l1b_wdq", I("W_DQ"))
    wdkv = hin(ctx, "l1b_wdkv", I("W_DKV"))
    wuq = hin(ctx, "l1b_wuq", I("W_UQ"))
    wuk = hin(ctx, "l1b_wuk", I("W_UK"))
    wuv = hin(ctx, "l1b_wuv", I("W_UV"))
    wqr = hin(ctx, "l1b_wqr", I("W_QR"))
    wkr = hin(ctx, "l1b_wkr", I("W_KR"))
    woT = hin(ctx, "l1b_wo", I("W_O"))
    cosE, sinE, R = rope_tables_l1b()
    cosT = hin(ctx, "l1b_cos", _tile4(cosE, 4))
    sinT = hin(ctx, "l1b_sin", _tile4(sinE, 4))
    rotT = hin(ctx, "l1b_rot", _bf(_blockdiag(R, 4).T))
    for q in range(4):
        hin(ctx, f"c_cmask{q}", causal_mask_host(q))
    odram = ctx.nc.dram_tensor(f"l1b_od{uid}", [D, T], BF16, kind="Internal").ap()
    with ExitStack() as st:
        c = ctx.scoped(st)
        layernorm(c)
        cosS = c.sb("l1b_cosS", [P, T], F32); p.dma(cosS, ctx.hbm["l1b_cos"])
        sinS = c.sb("l1b_sinS", [P, T], F32); p.dma(sinS, ctx.hbm["l1b_sin"])
        rotS = c.sb("l1b_rotS", [P, P], BF16); p.dma(rotS, ctx.hbm["l1b_rot"])
        masks = [c.sb(f"l1b_m{q}", [P, 512], F32) for q in range(4)]
        for q in range(4):
            p.dma(masks[q], ctx.hbm[f"c_cmask{q}"])
        cq = c.sb("l1b_cq", [P, 2 * T], BF16)
        ckv = c.sb("l1b_ckv", [P, 2 * T], BF16)
        kr = c.sb("l1b_kr", [32, T], BF16)
        kr2 = c.sb("l1b_kr2", [64, T], BF16)
        sf1 = c.sb("l1b_sf1", [P, 512], F32)
        sf2 = c.sb("l1b_sf2", [P, 512], F32)
        with ExitStack() as stk:
            ck = c.scoped(stk)
            wdq_s = wload(ck, stk, wdq, D, 256, f"l1b_dq{uid}")
            wdkv_s = wload(ck, stk, wdkv, D, 256, f"l1b_dkv{uid}")
            wkr_s = wload(ck, stk, wkr, D, 32, f"l1b_kr{uid}")
            krf = ck.sb("l1b_krf", [32, T], F32)
            sbf32 = ck.sb("l1b_sbf32", [32, 512], BF16)
            for (wsb, dst, dout) in ((wdq_s, cq, 256), (wdkv_s, ckv, 256)):
                linear(ck, wsb, D, dout, ctx.xn_bf, dst, psA=c.pp[0], psB=c.pp[1])
            for cc in range(TC):
                ps = c.pp[0]
                for kt in range(ND):
                    p.mm(ps[0:32, :], wkr_s[:, kt * 32: kt * 32 + 32],
                         ctx.xn_bf[:, kt * T + cc * 512: kt * T + (cc + 1) * 512],
                         start=(kt == 0), stop=(kt == ND - 1))
                p.copy(krf[:, cc * 512:(cc + 1) * 512], ps[0:32, :], engine="scalar")
            rope_apply2(ck, krf, 32, rotS, cosS, sinS, kr, c.pp[0], sbf32, sf1, sf2)
            for b in range(2):
                p.dma(kr2[b * 32:(b + 1) * 32, :], kr)
            p.barrier()

        def jts_fn(ic):
            return list(range(ic * 4 + 4))

        def mask_fn(ic, jt):
            q = jt - ic * 4
            return masks[q] if q >= 0 else None
        for g4 in range(4):
            with ExitStack() as st2:
                c2 = c.scoped(st2)
                wuq_s = wload(c2, st2, wuq[:, g4 * 256:(g4 + 1) * 256], 256, 256, f"l1b_uq{g4}{uid}")
                wuk_s = wload(c2, st2, wuk[:, g4 * 256:(g4 + 1) * 256], 256, 256, f"l1b_uk{g4}{uid}")
                wuv_s = wload(c2, st2, wuv[:, g4 * 256:(g4 + 1) * 256], 256, 256, f"l1b_uv{g4}{uid}")
                wqr_s = wload(c2, st2, wqr[:, g4 * P:(g4 + 1) * P], 256, P, f"l1b_qr{g4}{uid}")
                qrr = c2.sb("l1b_qrr", [P, T], BF16)
                qr2 = c2.sb("l1b_qr2", [64, T], BF16)
                with ExitStack() as str_:
                    cr = c2.scoped(str_)
                    qrf = cr.sb("l1b_qrf", [P, T], F32)
                    sbf = cr.sb("l1b_sbf", [P, 512], BF16)
                    for cc in range(TC):
                        ps = c2.pp[0]
                        for kt in range(2):
                            p.mm(ps, wqr_s[:, kt * P:(kt + 1) * P],
                                 cq[:, kt * T + cc * 512: kt * T + (cc + 1) * 512],
                                 start=(kt == 0), stop=(kt == 1))
                        p.copy(qrf[:, cc * 512:(cc + 1) * 512], ps, engine="scalar")
                    rope_apply2(cr, qrf, P, rotS, cosS, sinS, qrr, c2.pp[0], sbf, sf1, sf2)
                    p.dma(qr2, qrr[64:128, :])
                    p.barrier()
                qc = c2.sb("l1b_qc", [P, 2 * T], BF16)
                kc = c2.sb("l1b_kc", [P, 2 * T], BF16)
                vtok = c2.sb("l1b_vt", [P, NT * 256], BF16)
                linear(c2, wuq_s, 256, 256, cq, qc, psA=c2.pp[0], psB=c2.pp[1])
                linear(c2, wuk_s, 256, 256, ckv, kc, psA=c2.pp[0], psB=c2.pp[1])
                for tt in range(NT):
                    ps = c2.pp[0] if tt % 2 == 0 else c2.pp[1]
                    for kt in range(2):
                        p.mm(ps[:, 0:256], ckv[:, kt * T + tt * P: kt * T + (tt + 1) * P],
                             wuv_s[:, kt * 256:(kt + 1) * 256],
                             start=(kt == 0), stop=(kt == 1))
                    p.copy(vtok[:, tt * 256:(tt + 1) * 256], ps[:, 0:256], engine="scalar")
                sc = make_attn_scratch(c2, f"l1b{g4}")
                heads = []
                for hl in range(4):
                    h_id = g4 * 4 + hl
                    r0 = (hl * 64) % P
                    tq = (hl * 64) // P
                    kcap = kc[r0:r0 + 64, tq * T:(tq + 1) * T]
                    qcap = qc[r0:r0 + 64, tq * T:(tq + 1) * T]
                    r0r = hl * 32
                    if r0r < 64:
                        qrap = qrr[r0r:r0r + 32, :]
                        krap = kr2[r0r:r0r + 32, :]
                    else:
                        qrap = qr2[r0r - 64:r0r - 32, :]
                        krap = kr2[r0r - 64:r0r - 32, :]
                    heads.append(([(kcap, qcap, 64), (krap, qrap, 32)], h_id, hl * 64))
                obf = c2.sb("l1b_obf", [64, 512], BF16)

                def out_fn(h_id, ic, o_f):
                    p.copy(obf, o_f[0:64, :])
                    p.dma(odram[h_id * 64:(h_id + 1) * 64, ic * 512:(ic + 1) * 512], obf)
                attn_group(c2, heads, 64, vtok, 256, jts_fn, mask_fn,
                           1.0 / np.sqrt(96.0), 80.0, out_fn, **sc)
                p.barrier()
        wo_from_dram(c, odram, woT, None, f"l1b{uid}")
        p.barrier()




def st_l0_attn_b(ctx, ins, uid):
    p = ctx.p
    wqkv = hin(ctx, "l0b_wqkv", ins["L0_attn_b_Wqkv"])
    woT = hin(ctx, "l0b_wo", ins["L0_attn_b_Wout"])
    cosE, sinE, R = rope_tables_l0b()
    cosT = hin(ctx, "l0b_cos", _tile4(cosE, 2))
    sinT = hin(ctx, "l0b_sin", _tile4(sinE, 2))
    rotT = hin(ctx, "l0b_rot", _bf(_blockdiag(R, 2).T))
    for q in range(-4, 4):
        hin(ctx, f"c_wmask{q}", window_mask_host(q))
    odram = ctx.nc.dram_tensor(f"l0b_od{uid}", [D, T], BF16, kind="Internal").ap()
    with ExitStack() as st:
        c = ctx.scoped(st)
        layernorm(c)
        cosS = c.sb("l0b_cosS", [P, T], F32); p.dma(cosS, ctx.hbm["l0b_cos"])
        sinS = c.sb("l0b_sinS", [P, T], F32); p.dma(sinS, ctx.hbm["l0b_sin"])
        rotS = c.sb("l0b_rotS", [P, P], BF16); p.dma(rotS, ctx.hbm["l0b_rot"])
        masks = {}
        for q in range(-4, 4):
            masks[q] = c.sb(f"l0b_m{q}", [P, 512], F32)
            p.dma(masks[q], ctx.hbm[f"c_wmask{q}"])

        def jts_fn(ic):
            return list(range(max(0, ic * 4 - 4), ic * 4 + 4))

        def mask_fn(ic, jt):
            return masks[jt - ic * 4]
        for g4 in range(4):
            with ExitStack() as st2:
                c2 = c.scoped(st2)
                col0 = g4 * 256
                wq = wload(c2, st2, wqkv[:, col0:col0 + 256], D, 256, f"l0bq{g4}{uid}")
                wk = wload(c2, st2, wqkv[:, 1024 + col0:1024 + col0 + 256], D, 256, f"l0bk{g4}{uid}")
                wv = wload(c2, st2, wqkv[:, 2048 + col0:2048 + col0 + 256], D, 256, f"l0bv{g4}{uid}")
                qf = c2.sb("l0b_qf", [P, T], F32)
                sbf = c2.sb("l0b_sbf", [P, 512], BF16)
                sf1 = c2.sb("l0b_sf1", [P, 512], F32)
                sf2 = c2.sb("l0b_sf2", [P, 512], F32)
                qro = c2.sb("l0b_qro", [P, 2 * T], BF16)
                kro = c2.sb("l0b_kro", [P, 2 * T], BF16)
                vtok = c2.sb("l0b_vtok", [P, NT * 256], BF16)
                for gg in range(2):
                    for (w, dst) in ((wq, qro), (wk, kro)):
                        proj_fm_group(c2, w, None, 0, qf, wst=256, wcol0=gg * P)
                        rope_apply2(c2, qf, P, rotS, cosS, sinS,
                                    dst[:, gg * T:(gg + 1) * T], c2.pp[0], sbf, sf1, sf2)
                for tt in range(NT):
                    ps = c2.pp[0] if tt % 2 == 0 else c2.pp[1]
                    for kt in range(ND):
                        p.mm(ps[:, 0:256], ctx.xn_bf[:, kt * T + tt * P: kt * T + (tt + 1) * P],
                             wv[:, kt * 256:(kt + 1) * 256],
                             start=(kt == 0), stop=(kt == ND - 1))
                    p.copy(vtok[:, tt * 256:(tt + 1) * 256], ps[:, 0:256], engine="scalar")
                sc = make_attn_scratch(c2, f"l0b{g4}")
                heads = []
                for hl in range(4):
                    tq = (hl * 64) // P
                    r0 = (hl * 64) % P
                    kap = kro[r0:r0 + 64, tq * T:(tq + 1) * T]
                    qap = qro[r0:r0 + 64, tq * T:(tq + 1) * T]
                    heads.append(([(kap, qap, 64)], g4 * 4 + hl, hl * 64))
                obf = c2.sb("l0b_obf", [64, 512], BF16)

                def out_fn(h_id, ic, o_f):
                    p.copy(obf, o_f[0:64, :])
                    p.dma(odram[h_id * 64:(h_id + 1) * 64, ic * 512:(ic + 1) * 512], obf)
                attn_group(c2, heads, 64, vtok, 256, jts_fn, mask_fn,
                           0.125, None, out_fn, **sc)
                p.barrier()
        wo_from_dram(c, odram, woT, None, f"l0b{uid}")
        p.barrier()


def st_l3_attn_a(ctx, ins, uid):
    p = ctx.p
    Wqk = ins["L3_attn_a_W_qk"].reshape(16, 64, 2, D)
    bqk = ins["L3_attn_a_b_qk"].reshape(16, 64, 2)
    wqT = hin(ctx, "l3a_wq", np.ascontiguousarray(Wqk[:, :, 0, :].reshape(D, D).T))
    wkT = hin(ctx, "l3a_wk", np.ascontiguousarray(Wqk[:, :, 1, :].reshape(D, D).T))
    with ExitStack() as st:
        c = ctx.scoped(st)
        layernorm(c)
        bq = load_pp(c, "l3a_bq", np.ascontiguousarray(bqk[:, :, 0].reshape(D)))
        bk = load_pp(c, "l3a_bk", np.ascontiguousarray(bqk[:, :, 1].reshape(D)))
        s_all = c.sb("l3a_sall", [P, NT * 512], F32)
        mrow = c.sb("l3a_mrow", [1, 512], F32)
        mbc = c.sb("l3a_mbc", [P, 512], F32)
        pb = c.sb("l3a_pb", [P, 512], BF16)
        z1 = c.sb("l3a_z1", [P, 512], F32)
        z2 = c.sb("l3a_z2", [P, 512], F32)
        of = c.sb("l3a_of", [P, 512], F32)
        CP = 1.0507009873554805 * (2048.0 ** -0.5)
        CN = 1.0507009873554805 * 1.6732632423543772 * (2048.0 ** -0.5)
        for hh in range(2):
            with ExitStack() as st2:
                c2 = c.scoped(st2)
                qb = c2.sb("l3a_qb", [P, 4 * T], BF16)
                kb = c2.sb("l3a_kb", [P, 4 * T], BF16)
                vtok = c2.sb("l3a_vt", [P, NT * 512], BF16)
                for (wT, bpp, dst) in ((wqT, bq, qb), (wkT, bk, kb)):
                    with ExitStack() as stw:
                        cw = c2.scoped(stw)
                        w = wload(cw, stw, wT[:, hh * 512:(hh + 1) * 512], D, 512,
                                  f"l3aw{hh}{uid}")
                        for mt in range(4):
                            gmt = hh * 4 + mt
                            for cc in range(TC):
                                ps = c2.pp[0] if cc % 2 == 0 else c2.pp[1]
                                for kt in range(ND):
                                    p.mm(ps, w[:, kt * 512 + mt * P: kt * 512 + (mt + 1) * P],
                                         ctx.xn_bf[:, kt * T + cc * 512: kt * T + (cc + 1) * 512],
                                         start=(kt == 0), stop=(kt == ND - 1))
                                p.act(dst[:, mt * T + cc * 512: mt * T + (cc + 1) * 512],
                                      ps, AF.Identity, bias=bpp[:, gmt:gmt + 1])
                        p.barrier()
                for tt in range(NT):
                    for dt in range(4):
                        gdt = hh * 4 + dt
                        psb = ctx.ppb[:, 0:P]
                        p.transpose(psb, ctx.xn_bf[:, gdt * T + tt * P: gdt * T + (tt + 1) * P],
                                    ctx.ident_b)
                        p.copy(vtok[:, tt * 512 + dt * P: tt * 512 + (dt + 1) * P], psb,
                               engine="scalar")
                for hl in range(8):
                    h_id = hh * 8 + hl
                    r0 = (hl * 64) % P
                    tq = (hl * 64) // P
                    kap = kb[r0:r0 + 64, tq * T:(tq + 1) * T]
                    qap = qb[r0:r0 + 64, tq * T:(tq + 1) * T]
                    for ic in range(TC):
                        psS = c.pp[2]
                        psD = c.pp[4][0:1, :]
                        psO = c.pp[3][r0:r0 + 64, :]
                        for jt in range(NT):
                            p.mm(psS, kap[0:64, jt * P:(jt + 1) * P],
                                 qap[0:64, ic * 512:(ic + 1) * 512], start=True, stop=True)
                            p.copy(s_all[:, jt * 512:(jt + 1) * 512], psS, engine="scalar")
                        for jt in range(NT):
                            p.mm(psD, ctx.ones_col, s_all[:, jt * 512:(jt + 1) * 512],
                                 start=(jt == 0), stop=(jt == NT - 1))
                        p.ts(mrow, psD, 0.125 / T, ALU.mult)
                        p.mm(c.pp[5], ctx.ones_row, mrow, start=True, stop=True)
                        p.copy(mbc, c.pp[5], engine="scalar")
                        for jt in range(NT):
                            sa = s_all[:, jt * 512:(jt + 1) * 512]
                            p.stt(z1, sa, 0.125, mbc, ALU.mult, ALU.subtract)
                            p.act(z2, z1, AF.Exp)
                            p.ts(z2, z2, -1.0, ALU.add, CN, ALU.mult, engine="gpsimd")
                            p.ts(z2, z2, 0.0, ALU.min, engine="gpsimd")
                            p.act(z1, z1, AF.Relu, scale=CP)
                            p.tt(pb, z1, z2, ALU.add)
                            p.mm(psO,
                                 vtok[:, jt * 512 + hl * 64: jt * 512 + hl * 64 + 64],
                                 pb, start=(jt == 0), stop=(jt == NT - 1))
                        p.copy(of[r0:r0 + 64, :], psO, engine="scalar")
                        xr = ctx.xres[(h_id * 64) % P:(h_id * 64) % P + 64,
                                      ((h_id * 64) // P) * T + ic * 512:
                                      ((h_id * 64) // P) * T + (ic + 1) * 512]
                        p.tt(xr, xr, of[r0:r0 + 64, :], ALU.add)
                p.barrier()
        p.barrier()


def st_l0_attn_a(ctx, ins, uid):
    p = ctx.p
    I = lambda n: ins[f"L0_attn_a_{n}"]
    wq = hin(ctx, "l0a_wq", I("Wq"))
    wk = hin(ctx, "l0a_wk", I("Wk"))
    wv = hin(ctx, "l0a_wv", I("Wv"))
    wo = hin(ctx, "l0a_wo", I("Wo"))
    with ExitStack() as st:
        c = ctx.scoped(st)
        layernorm(c)
        zb = c.sb("l0a_zb", [P, ND * T], BF16)
        st_z = ExitStack()
        c = c.scoped(st_z)
        vecs = {}
        for nm in ("q", "k", "v"):
            g = load_pp(c, f"l0a_g{nm}", I(f"g{nm}"))
            b = load_pp(c, f"l0a_b{nm}", I(f"b{nm}"))
            m = load_pp(c, f"l0a_m{nm}", I(f"m{nm}"))
            v = load_pp(c, f"l0a_v{nm}", I(f"v{nm}"))
            s4 = c.sb(f"l0a_s4{nm}", [P, ND], F32)
            b4 = c.sb(f"l0a_b4{nm}", [P, ND], F32)
            p.act(s4, v, AF.Ln, bias=1e-5)
            p.act(s4, s4, AF.Exp, scale=-0.5)
            p.tt(s4, s4, g, ALU.mult)
            p.tt(b4, m, s4, ALU.mult)
            p.tt(b4, b, b4, ALU.subtract)
            p.ts(b4, b4, 4.0, ALU.mult, -4.0, ALU.add)
            p.ts(s4, s4, 4.0, ALU.mult)
            vecs[nm] = (s4, b4)
        tQr = c.sb("l0a_tq", [1, T], F32)
        tKr = c.sb("l0a_tk", [1, T], F32)
        vtok = c.sb("l0a_vt", [P, NT * 1024], BF16)
        qtile = c.sb("l0a_qt", [P, 512], BF16)
        for nm, trow in (("q", tQr), ("k", tKr), ("v", None)):
            with ExitStack() as stw:
                cw = c.scoped(stw)
                w = wload(cw, stw, {"q": wq, "k": wk, "v": wv}[nm], D, D, f"l0a{nm}{uid}")
                s4, b4 = vecs[nm]
                psr = [c.pp[2][0:1, :], c.pp[3][0:1, :], c.pp[4][0:1, :], c.pp[5][0:1, :]]
                for mt in range(ND):
                    for cc in range(TC):
                        ps = c.pp[0] if (mt * TC + cc) % 2 == 0 else c.pp[1]
                        for kt in range(ND):
                            p.mm(ps, w[:, kt * D + mt * P: kt * D + (mt + 1) * P],
                                 ctx.xn_bf[:, kt * T + cc * 512: kt * T + (cc + 1) * 512],
                                 start=(kt == 0), stop=(kt == ND - 1))
                        p.act(qtile, ps, AF.Sigmoid, bias=b4[:, mt:mt + 1],
                              scale=s4[:, mt:mt + 1])
                        if trow is not None:
                            p.mm(psr[cc], ctx.ones_col_bf, qtile,
                                 start=(mt == 0), stop=(mt == ND - 1))
                        else:
                            psb = ctx.ppb[:, 0:P]
                            for q4 in range(4):
                                p.transpose(psb, qtile[:, q4 * P:(q4 + 1) * P], ctx.ident_b)
                                p.copy(vtok[:, (cc * 4 + q4) * 1024 + mt * P:
                                            (cc * 4 + q4) * 1024 + (mt + 1) * P],
                                       psb, engine="scalar")
                if trow is not None:
                    for cc in range(TC):
                        p.ts(trow[:, cc * 512:(cc + 1) * 512], psr[cc],
                             -1.0 / D, ALU.mult, 1.0, ALU.add)
                p.barrier()
        dsc = ctx.nc.dram_tensor(f"l0a_dsc{uid}", [T], F32, kind="Internal").ap()
        p.dma(dsc.rearrange("(a b) -> a b", a=1), tKr)
        tkst = c.sb("l0a_tkst", [NT, P], F32)
        p.dma(tkst, dsc.rearrange("(k p) -> k p", p=P))
        tkc = c.sb("l0a_tkc", [P, NT], F32)
        p.transpose(c.pp[5][:, 0:NT], tkst, ctx.ident_f[0:NT, 0:NT])
        p.copy(tkc, c.pp[5][:, 0:NT], engine="scalar")
        tqb = c.sb("l0a_tqb", [P, T], F32)
        bcast_row(c, tqb, tQr)
        d1 = c.sb("l0a_d1", [P, 512], F32)
        d2 = c.sb("l0a_d2", [P, 512], F32)
        d3 = c.sb("l0a_d3", [P, 512], F32)
        abf = c.sb("l0a_abf", [P, 512], BF16)
        for ic in range(TC):
            for mh in range(2):
                for jt in range(NT):
                    p.ts(d1, tqb[:, ic * 512:(ic + 1) * 512], tkc[:, jt:jt + 1], ALU.subtract)
                    p.act(d2, d1, AF.Sign)
                    p.act(d3, d1, AF.Abs)
                    p.act(d3, d3, AF.Exp, scale=-2.0)
                    p.stt(d2, d2, -0.9, d3, ALU.mult, ALU.mult, engine="gpsimd")
                    p.ts(abf, d2, 0.9, ALU.add, engine="gpsimd")
                    for dt in range(4):
                        gdt = mh * 4 + dt
                        p.mm(c.pp[dt],
                             vtok[:, jt * 1024 + gdt * P: jt * 1024 + (gdt + 1) * P],
                             abf, start=(jt == 0), stop=(jt == NT - 1))
                for dt in range(4):
                    gdt = mh * 4 + dt
                    p.copy(zb[:, gdt * T + ic * 512: gdt * T + (ic + 1) * 512],
                           c.pp[dt], engine="scalar")
        p.barrier()
        st_z.close()
        c = ctx.scoped(st)
        bo = load_pp(c, "l0a_bo", I("bo"))
        with ExitStack() as stw:
            cw = c.scoped(stw)
            wos = wload(cw, stw, wo, D, D, f"l0ao{uid}")
            linear(cw, wos, D, D, zb, None, bias_pp=bo, accum_res=True,
                   psA=c.pp[0], psB=c.pp[1])
            p.barrier()
        p.barrier()




def _gates_topk(c, ctx, wgT_hbm, E, K_sel, uid, bias_row_np=None, clipmin=None):
    p = c.p
    wg = wload(c, c.stack, wgT_hbm, D, E, f"gw{uid}")
    gatesT = c.sb(f"gatesT{uid}", [E, T], F32)
    sc = c.sb(f"gsc{uid}", [P, E], F32)
    prob = c.sb(f"gpr{uid}", [P, E], F32)
    corr = c.sb(f"gco{uid}", [P, E], F32)
    m8 = c.sb(f"gm8{uid}", [P, 8], F32)
    r1 = c.sb(f"gr1{uid}", [P, 1], F32)
    bbc = None
    if bias_row_np is not None:
        br = hin(ctx, f"gbias{uid}", np.ascontiguousarray(bias_row_np[None, :].astype(np.float32)))
        brs = c.sb(f"gbr{uid}", [1, E], F32)
        p.dma(brs, br)
        bbc = c.sb(f"gbb{uid}", [P, E], F32)
        ps = c.pp[5][:, 0:E]
        p.mm(ps, ctx.ones_row, brs, start=True, stop=True)
        p.copy(bbc, ps, engine="scalar")
    for tt in range(NT):
        ps = c.pp[0][:, 0:E]
        for kt in range(ND):
            p.mm(ps, ctx.xn_bf[:, kt * T + tt * P: kt * T + (tt + 1) * P],
                 wg[:, kt * E:(kt + 1) * E], start=(kt == 0), stop=(kt == ND - 1))
        p.red(r1, ps, ALU.max, negate=True)
        p.act(sc, ps, AF.Exp, bias=r1)
        p.red(r1, sc, ALU.add)
        p.recip(r1, r1)
        p.ts(prob, sc, r1, ALU.mult)
        if bbc is not None:
            p.tt(corr, prob, bbc, ALU.add)
        else:
            p.copy(corr, prob)
        p.max8(m8, corr)
        p.ts(sc, corr, m8[:, K_sel - 1:K_sel], ALU.is_ge)
        p.tt(prob, prob, sc, ALU.mult)
        p.red(r1, prob, ALU.add)
        if clipmin is not None:
            p.ts(r1, r1, clipmin, ALU.max)
        p.recip(r1, r1)
        p.ts(prob, prob, r1, ALU.mult)
        ps2 = c.pp[1][0:E, 0:P]
        p.transpose(ps2, prob, ctx.ident_f)
        p.copy(gatesT[:, tt * P:(tt + 1) * P], ps2, engine="scalar")
    gdram = c.nc.dram_tensor(f"gdram{uid}", [E, T], F32, kind="Internal").ap()
    p.dma(gdram, gatesT)
    return gatesT, gdram


def st_l0_ffn(ctx, ins, uid):
    p = ctx.p
    E, HID, R = 8, 2048, 16
    wgT = hin(ctx, "l0f_wg", np.ascontiguousarray(ins["L0_ffn_Wg"].T))
    for e in range(E):
        hin(ctx, f"l0f_w1T{e}", np.ascontiguousarray(ins["L0_ffn_W1"][e].T))
        hin(ctx, f"l0f_w3T{e}", np.ascontiguousarray(ins["L0_ffn_W3"][e].T))
        hin(ctx, f"l0f_w2T{e}", np.ascontiguousarray(ins["L0_ffn_W2"][e].T))
        hin(ctx, f"l0f_a1T{e}", np.ascontiguousarray(ins["L0_ffn_A1"][e].T))
        hin(ctx, f"l0f_a3T{e}", np.ascontiguousarray(ins["L0_ffn_A3"][e].T))
        hin(ctx, f"l0f_a2T{e}", np.ascontiguousarray(ins["L0_ffn_A2"][e].T))
        hin(ctx, f"l0f_b1s{e}", np.ascontiguousarray(ins["L0_ffn_B1"][e].T))
        hin(ctx, f"l0f_b3s{e}", np.ascontiguousarray(ins["L0_ffn_B3"][e].T))
        hin(ctx, f"l0f_b2s{e}", np.ascontiguousarray(ins["L0_ffn_B2"][e].T))
        hin(ctx, f"l0f_bias1pp{e}", pp_host(ins["L0_ffn_b1"][e]))
        hin(ctx, f"l0f_bias3pp{e}", pp_host(ins["L0_ffn_b3"][e]))
        hin(ctx, f"l0f_bias2pp{e}", pp_host(ins["L0_ffn_b2"][e]))
    with ExitStack() as st:
        c = ctx.scoped(st)
        layernorm(c)
        gatesT, gdram = _gates_topk(c, ctx, wgT, E, 2, f"l0f{uid}")
        grow = c.sb("l0f_grow", [1, 512], F32)
        yacc = c.sb("l0f_yacc", [P, ND * 512], F32)
        gsb = c.sb("l0f_g", [P, NT * 512], BF16)
        h3q = c.sb("l0f_h3", [P, 4 * 512], BF16)
        terb = c.sb("l0f_ter", [16, 512], BF16)
        ter2 = c.sb("l0f_ter2", [16, 512], BF16)
        gbc = c.sb("l0f_gbc", [P, 512], F32)
        scr = c.sb("l0f_scr", [P, 512], F32)
        for tq in range(TC):
            toff = tq * 512
            for e in range(E):
                with ExitStack() as st2:
                    c2 = c.scoped(st2)
                    a1 = wload(c2, st2, ctx.hbm[f"l0f_a1T{e}"], D, 16, f"a1_{e}{uid}")
                    a3 = wload(c2, st2, ctx.hbm[f"l0f_a3T{e}"], D, 16, f"a3_{e}{uid}")
                    a2 = wload(c2, st2, ctx.hbm[f"l0f_a2T{e}"], HID, 16, f"a2_{e}{uid}")
                    b1s = c2.sb("b1s", [16, HID], BF16)
                    b3s = c2.sb("b3s", [16, HID], BF16)
                    b2s = c2.sb("b2s", [16, D], BF16)
                    for (bb, hb, w_) in ((b1s, f"l0f_b1s{e}", HID), (b3s, f"l0f_b3s{e}", HID),
                                         (b2s, f"l0f_b2s{e}", D)):
                        p.dma(ctx.wstg[0:16, 0:w_], ctx.hbm[hb])
                        p.copy(bb, ctx.wstg[0:16, 0:w_], engine="gpsimd")
                    bias1 = c2.sb("bi1", [P, 16], F32); p.dma(bias1, ctx.hbm[f"l0f_bias1pp{e}"])
                    bias3 = c2.sb("bi3", [P, 16], F32); p.dma(bias3, ctx.hbm[f"l0f_bias3pp{e}"])
                    bias2 = c2.sb("bi2", [P, ND], F32); p.dma(bias2, ctx.hbm[f"l0f_bias2pp{e}"])
                    for (aa, tb) in ((a1, terb), (a3, ter2)):
                        ps = c2.pp[0][0:16, :]
                        for kt in range(ND):
                            p.mm(ps, aa[:, kt * 16: kt * 16 + 16],
                                 ctx.xn_bf[:, kt * T + toff: kt * T + toff + 512],
                                 start=(kt == 0), stop=(kt == ND - 1))
                        p.copy(tb, ps, engine="scalar")
                    p.dma(grow, gdram[e:e + 1, toff:toff + 512])
                    p.mm(c2.pp[5], ctx.ones_row, grow, start=True, stop=True)
                    p.copy(gbc, c2.pp[5], engine="scalar")
                    for hc in range(4):
                      with ExitStack() as sth:
                        ch = c2.scoped(sth)
                        w1 = wload(ch, sth, ctx.hbm[f"l0f_w1T{e}"][:, hc * 512:(hc + 1) * 512],
                                   D, 512, f"w1_{e}_{hc}_{tq}{uid}")
                        for mt in range(4):
                            ps = c2.pp[0] if mt % 2 == 0 else c2.pp[1]
                            for kt in range(ND):
                                p.mm(ps, w1[:, kt * 512 + mt * P: kt * 512 + (mt + 1) * P],
                                     ctx.xn_bf[:, kt * T + toff: kt * T + toff + 512],
                                     start=(kt == 0), stop=False)
                            p.mm(ps, b1s[:, hc * 512 + mt * P: hc * 512 + (mt + 1) * P],
                                 terb, start=False, stop=True)
                            p.act(gsb[:, (hc * 4 + mt) * 512:(hc * 4 + mt + 1) * 512], ps,
                                  AF.Silu, bias=bias1[:, hc * 4 + mt: hc * 4 + mt + 1])
                        w3 = wload(ch, sth, ctx.hbm[f"l0f_w3T{e}"][:, hc * 512:(hc + 1) * 512],
                                   D, 512, f"w3_{e}_{hc}_{tq}{uid}")
                        for mt in range(4):
                            ps = c2.pp[0] if mt % 2 == 0 else c2.pp[1]
                            for kt in range(ND):
                                p.mm(ps, w3[:, kt * 512 + mt * P: kt * 512 + (mt + 1) * P],
                                     ctx.xn_bf[:, kt * T + toff: kt * T + toff + 512],
                                     start=(kt == 0), stop=False)
                            p.mm(ps, b3s[:, hc * 512 + mt * P: hc * 512 + (mt + 1) * P],
                                 ter2, start=False, stop=True)
                            p.act(h3q[:, mt * 512:(mt + 1) * 512], ps, AF.Identity,
                                  bias=bias3[:, hc * 4 + mt: hc * 4 + mt + 1])
                        for mt in range(4):
                            gslice = gsb[:, (hc * 4 + mt) * 512:(hc * 4 + mt + 1) * 512]
                            p.tt(gslice, gslice, h3q[:, mt * 512:(mt + 1) * 512], ALU.mult,
                                 engine="vector" if mt % 2 else "gpsimd")
                        p.barrier()
                    ps = c2.pp[0][0:16, :]
                    for kt in range(NT):
                        p.mm(ps, a2[:, kt * 16: kt * 16 + 16],
                             gsb[:, kt * 512:(kt + 1) * 512],
                             start=(kt == 0), stop=(kt == NT - 1))
                    p.copy(ter2, ps, engine="scalar")
                    w2 = wload(c2, st2, ctx.hbm[f"l0f_w2T{e}"], HID, D, f"w2_{e}_{tq}{uid}")
                    for mt in range(ND):
                        ps = c2.pp[0] if mt % 2 == 0 else c2.pp[1]
                        for kt in range(NT):
                            p.mm(ps, w2[:, kt * D + mt * P: kt * D + (mt + 1) * P],
                                 gsb[:, kt * 512:(kt + 1) * 512],
                                 start=(kt == 0), stop=False)
                        p.mm(ps, b2s[:, mt * P:(mt + 1) * P], ter2, start=False, stop=True)
                        p.act(scr, ps, AF.Identity, bias=bias2[:, mt:mt + 1])
                        ya = yacc[:, mt * 512:(mt + 1) * 512]
                        if e == 0:
                            p.tt(ya, scr, gbc, ALU.mult,
                                 engine="vector" if mt % 2 else "gpsimd")
                        else:
                            p.tt(scr, scr, gbc, ALU.mult,
                                 engine="vector" if mt % 2 else "gpsimd")
                            p.tt(ya, ya, scr, ALU.add,
                                 engine="gpsimd" if mt % 2 else "vector")
                    p.barrier()
            for mt in range(ND):
                xr = ctx.xres[:, mt * T + toff: mt * T + toff + 512]
                p.tt(xr, xr, yacc[:, mt * 512:(mt + 1) * 512], ALU.add,
                     engine="vector" if mt % 2 else "gpsimd")
        p.barrier()


def st_l1_ffn(ctx, ins, uid):
    p = ctx.p
    E, I_ = 32, 512
    rwT = hin(ctx, "l1f_rw", np.ascontiguousarray(ins["L1_ffn_router_weight"].T))
    sgT = hin(ctx, "l1f_sg", np.ascontiguousarray(ins["L1_ffn_shared_gate_w"].T))
    suT = hin(ctx, "l1f_su", np.ascontiguousarray(ins["L1_ffn_shared_up_w"].T))
    sdT = hin(ctx, "l1f_sd", np.ascontiguousarray(ins["L1_ffn_shared_down_w"].T))
    for e in range(E):
        hin(ctx, f"l1f_gupT{e}", np.ascontiguousarray(ins["L1_ffn_gate_up_proj"][e].T))
        hin(ctx, f"l1f_dT{e}", np.ascontiguousarray(ins["L1_ffn_down_proj"][e].T))
    with ExitStack() as st:
        c = ctx.scoped(st)
        layernorm(c)
        gatesT, gdram = _gates_topk(c, ctx, rwT, E, 6, f"l1f{uid}",
                             bias_row_np=ins["L1_ffn_e_score_bias"], clipmin=1e-12)
        grow = c.sb("l1f_grow", [1, 512], F32)
        with ExitStack() as stsh:
            csh = c.scoped(stsh)
            gsh = csh.sb("l1f_gsh", [P, ND * T], BF16)
            with ExitStack() as st2:
                c2 = csh.scoped(st2)
                sg = wload(c2, st2, sgT, D, D, f"sg{uid}")
                linear(c2, sg, D, D, ctx.xn_bf, gsh, func=AF.Silu, psA=c2.pp[0], psB=c2.pp[1])
                p.barrier()
            with ExitStack() as st2:
                c2 = csh.scoped(st2)
                su = wload(c2, st2, suT, D, D, f"su{uid}")
                ush = c2.sb("l1f_ush", [P, T], F32)
                for mt in range(ND):
                    for cc in range(TC):
                        ps = c2.pp[0] if cc % 2 == 0 else c2.pp[1]
                        for kt in range(ND):
                            p.mm(ps, su[:, kt * D + mt * P: kt * D + (mt + 1) * P],
                                 ctx.xn_bf[:, kt * T + cc * 512: kt * T + (cc + 1) * 512],
                                 start=(kt == 0), stop=(kt == ND - 1))
                        p.copy(ush[:, cc * 512:(cc + 1) * 512], ps, engine="scalar")
                    gs = gsh[:, mt * T:(mt + 1) * T]
                    p.tt(gs, gs, ush, ALU.mult)
                p.barrier()
            with ExitStack() as st2:
                c2 = csh.scoped(st2)
                sd = wload(c2, st2, sdT, D, D, f"sd{uid}")
                linear(c2, sd, D, D, gsh, None, accum_res=True, psA=c2.pp[0], psB=c2.pp[1])
                p.barrier()
            p.barrier()
        yacc = c.sb("l1f_yacc", [P, ND * 1024], F32)
        hmb = c.sb("l1f_hm", [P, 4 * 1024], BF16)
        usb = c.sb("l1f_us", [P, 4 * 1024], BF16)
        gbc = c.sb("l1f_gbc", [P, 1024], F32)
        scr = c.sb("l1f_scr", [P, 512], F32)
        for th in range(2):
            toff = th * 1024
            for e in range(E):
                with ExitStack() as st2:
                    c2 = c.scoped(st2)
                    gup = wload(c2, st2, ctx.hbm[f"l1f_gupT{e}"], D, 1024, f"gup{e}_{th}{uid}")
                    dwn = wload(c2, st2, ctx.hbm[f"l1f_dT{e}"], I_, 1024, f"dwn{e}_{th}{uid}")
                    for cc in range(2):
                        p.dma(grow, gdram[e:e + 1, toff + cc * 512: toff + (cc + 1) * 512])
                        p.mm(c2.pp[5], ctx.ones_row, grow, start=True, stop=True)
                        p.copy(gbc[:, cc * 512:(cc + 1) * 512], c2.pp[5], engine="scalar")
                    for mt in range(8):
                        for cc in range(2):
                            ps = c2.pp[0] if (mt * 2 + cc) % 2 == 0 else c2.pp[1]
                            for kt in range(ND):
                                p.mm(ps, gup[:, kt * 1024 + mt * P: kt * 1024 + (mt + 1) * P],
                                     ctx.xn_bf[:, kt * T + toff + cc * 512: kt * T + toff + (cc + 1) * 512],
                                     start=(kt == 0), stop=(kt == ND - 1))
                            if mt < 4:
                                p.act(hmb[:, mt * 1024 + cc * 512: mt * 1024 + (cc + 1) * 512],
                                      ps, AF.Silu)
                            else:
                                p.copy(usb[:, (mt - 4) * 1024 + cc * 512: (mt - 4) * 1024 + (cc + 1) * 512],
                                       ps, engine="scalar")
                    for mt in range(4):
                        hs = hmb[:, mt * 1024:(mt + 1) * 1024]
                        p.tt(hs, hs, usb[:, mt * 1024:(mt + 1) * 1024], ALU.mult,
                             engine="vector" if mt % 2 else "gpsimd")
                    for mt in range(ND):
                        for cc in range(2):
                            ps = c2.pp[0] if (mt * 2 + cc) % 2 == 0 else c2.pp[1]
                            for kt in range(4):
                                p.mm(ps, dwn[:, kt * 1024 + mt * P: kt * 1024 + (mt + 1) * P],
                                     hmb[:, kt * 1024 + cc * 512: kt * 1024 + (cc + 1) * 512],
                                     start=(kt == 0), stop=(kt == 3))
                            p.copy(scr, ps, engine="scalar")
                            ya = yacc[:, mt * 1024 + cc * 512: mt * 1024 + (cc + 1) * 512]
                            gb = gbc[:, cc * 512:(cc + 1) * 512]
                            if e == 0:
                                p.tt(ya, scr, gb, ALU.mult,
                                     engine="vector" if (mt + cc) % 2 else "gpsimd")
                            else:
                                p.tt(scr, scr, gb, ALU.mult,
                                     engine="vector" if (mt + cc) % 2 else "gpsimd")
                                p.tt(ya, ya, scr, ALU.add,
                                     engine="gpsimd" if (mt + cc) % 2 else "vector")
                    p.barrier()
            for mt in range(ND):
                xr = ctx.xres[:, mt * T + toff: mt * T + toff + 1024]
                p.tt(xr, xr, yacc[:, mt * 1024:(mt + 1) * 1024], ALU.add,
                     engine="vector" if mt % 2 else "gpsimd")
        p.barrier()




def st_l1_attn_a(ctx, ins, uid):
    p = ctx.p
    I = lambda n: ins[f"L1_attn_a_{n}"]
    wB = hin(ctx, "l1a_wB", np.ascontiguousarray(I("W_B").T))
    wC = hin(ctx, "l1a_wC", np.ascontiguousarray(I("W_C").T))
    wdt1 = hin(ctx, "l1a_wdt1", np.ascontiguousarray(I("W_dt1").T))
    wdt2 = hin(ctx, "l1a_wdt2", np.ascontiguousarray(I("W_dt2").T))
    logA = hin(ctx, "l1a_logA", pp_host(np.ascontiguousarray(I("log_A").T)))
    with ExitStack() as st:
        c = ctx.scoped(st)
        xn, bm, br = layernorm(c, keep_stats=True)
        bdt2 = load_pp(c, "l1a_bdt2", I("b_dt2"))
        dsk = load_pp(c, "l1a_dsk", I("D_skip"))
        acol = c.sb("l1a_acol", [P, P], F32)
        p.dma(acol, logA)
        p.act(acol, acol, AF.Exp)
        p.ts(acol, acol, -1.0, ALU.mult)
        rcol = c.sb("l1a_rcol", [P, P], F32)
        p.ts(rcol, acol, 1e-8, ALU.add)
        p.recip(rcol, rcol)
        wB_s = wload(c, st, wB, D, 16, f"l1aB{uid}")
        wC_s = wload(c, st, wC, D, 16, f"l1aC{uid}")
        wdt1_s = wload(c, st, wdt1, D, 64, f"l1adt1{uid}")
        wdt2_s = wload(c, st, wdt2, 64, 1024, f"l1adt2{uid}")
        Brow = c.sb("l1a_Brow", [16, T], F32)
        Crow = c.sb("l1a_Crow", [16, T], F32)
        bcd = ctx.nc.dram_tensor(f"l1a_bcd{uid}", [32, T], F32, kind="Internal").ap()
        brow1 = c.sb("l1a_brow1", [1, 512], F32)
        crow1 = c.sb("l1a_crow1", [1, 512], F32)
        terb = c.sb("l1a_terb", [64, T], BF16)
        for (wsb, dst, rows) in ((wB_s, Brow, 16), (wC_s, Crow, 16), (wdt1_s, terb, 64)):
            for cc in range(TC):
                ps = c.pp[0][0:rows, :]
                for kt in range(ND):
                    p.mm(ps, wsb[:, kt * rows: kt * rows + rows],
                         ctx.xn_bf[:, kt * T + cc * 512: kt * T + (cc + 1) * 512],
                         start=(kt == 0), stop=(kt == ND - 1))
                p.copy(dst[:, cc * 512:(cc + 1) * 512], ps[0:rows, :], engine="scalar")
        p.dma(bcd[0:16, :], Brow)
        p.dma(bcd[16:32, :], Crow)
        xnf = c.sb("l1a_xnf", [P, T], F32)
        delta = c.sb("l1a_delta", [P, T], F32)
        s1 = c.sb("l1a_s1", [P, T], F32)
        s2 = c.sb("l1a_s2", [P, T], F32)
        av = s1
        bv = s2
        hv = ctx.wstg
        yac = c.sb("l1a_yac", [P, T], F32)
        for dt in range(ND):
            p.tt(xnf, ctx.xres_t(dt), bm, ALU.subtract)
            p.tt(xnf, xnf, br, ALU.mult)
            for cc in range(TC):
                ps = c.pp[0] if cc % 2 == 0 else c.pp[1]
                p.mm(ps, wdt2_s[0:64, dt * P:(dt + 1) * P],
                     terb[:, cc * 512:(cc + 1) * 512], start=True, stop=True)
                p.act(delta[:, cc * 512:(cc + 1) * 512], ps, AF.Identity,
                      bias=bdt2[:, dt:dt + 1])
            softplus(c, delta, delta, s1, s2)
            for n in range(16):
                col = n * ND + dt
                p.act(av, delta, AF.Exp, scale=acol[:, col:col + 1])
                p.ts(bv, av, 1.0, ALU.subtract, rcol[:, col:col + 1], ALU.mult)
                p.tt(bv, bv, xnf, ALU.mult, engine="gpsimd")
                for cc in range(TC):
                    cs = slice(cc * 512, (cc + 1) * 512)
                    p.dma(brow1, bcd[n:n + 1, cs])
                    p.mm(c.pp[5], ctx.ones_row, brow1, start=True, stop=True)
                    p.tt(bv[:, cs], bv[:, cs], c.pp[5], ALU.mult)
                p.scan(hv, av, bv, 0.0)
                for cc in range(TC):
                    cs = slice(cc * 512, (cc + 1) * 512)
                    p.dma(crow1, bcd[16 + n:17 + n, cs])
                    p.mm(c.pp[5], ctx.ones_row, crow1, start=True, stop=True)
                    p.tt(hv[:, cs], hv[:, cs], c.pp[5], ALU.mult)
                if n == 0:
                    p.copy(yac, hv, engine="gpsimd")
                else:
                    p.tt(yac, yac, hv, ALU.add)
            p.stt(yac, xnf, dsk[:, dt:dt + 1], yac, ALU.mult, ALU.add)
            xr = ctx.xres_t(dt)
            p.tt(xr, xr, yac, ALU.add, engine="gpsimd")
        p.barrier()


def st_l3_attn_b(ctx, ins, uid):
    p = ctx.p
    I = lambda n: ins[f"L3_attn_b_{n}"]
    wIn = hin(ctx, "l3b_wIn", I("W_in"))
    wX = hin(ctx, "l3b_wX", I("W_x"))
    wOut = hin(ctx, "l3b_wOut", I("W_out"))
    convw = hin(ctx, "l3b_convw", pp_host(np.ascontiguousarray(I("conv_w").T)))
    logA = hin(ctx, "l3b_logA", pp_host(np.ascontiguousarray(I("A_log").T)))
    DI, NDI = 2048, 16
    with ExitStack() as st:
        c = ctx.scoped(st)
        layernorm(c)
        convb = load_pp(c, "l3b_convb", I("conv_b"))
        dtw = load_pp(c, "l3b_dtw", I("dt_w"))
        dtb = load_pp(c, "l3b_dtb", I("dt_b"))
        dpp = load_pp(c, "l3b_D", I("D"))
        acol = c.sb("l3b_acol", [P, 256], F32)
        p.dma(acol, logA)
        p.act(acol, acol, AF.Exp)
        p.ts(acol, acol, -1.0, ALU.mult)
        rcol = c.sb("l3b_rcol", [P, 256], F32)
        p.ts(rcol, acol, 1e-8, ALU.add)
        p.recip(rcol, rcol)
        cwc = c.sb("l3b_cwc", [P, 64], F32)
        p.dma(cwc, convw)
        xcd = ctx.nc.dram_tensor(f"l3b_xcd{uid}", [DI, T], BF16, kind="Internal").ap()
        xbstg = c.sb("l3b_xbstg", [P, T], BF16)
        ssm = c.sb("l3b_ssm", [33, T], F32)

        def xconv_tile(cc_, gdt, out_f32):
            with ExitStack() as st2:
                c2 = cc_.scoped(st2)
                wq_ = wload(c2, st2, wIn[:, gdt * P:(gdt + 1) * P], D, P, f"l3bw{gdt}{uid}x")
                pad = c2.sb("l3b_pad", [P, T + 3], F32)
                p.memset(pad[:, 0:3], 0.0)
                for cc in range(TC):
                    ps = c2.pp[0] if cc % 2 == 0 else c2.pp[1]
                    for kt in range(ND):
                        p.mm(ps, wq_[:, kt * P:(kt + 1) * P],
                             ctx.xn_bf[:, kt * T + cc * 512: kt * T + (cc + 1) * 512],
                             start=(kt == 0), stop=(kt == ND - 1))
                    p.copy(pad[:, 3 + cc * 512: 3 + (cc + 1) * 512], ps, engine="scalar")
                p.ts(out_f32, pad[:, 0:T], cwc[:, gdt:gdt + 1], ALU.mult)
                for k in range(1, 4):
                    p.stt(out_f32, pad[:, k:k + T], cwc[:, k * NDI + gdt: k * NDI + gdt + 1],
                          out_f32, ALU.mult, ALU.add, engine="gpsimd" if k % 2 else "vector")
                p.act(out_f32, out_f32, AF.Silu, bias=convb[:, gdt:gdt + 1])
                p.barrier()
        xcf = c.sb("l3b_xcf", [P, T], F32)
        for gdt in range(NDI):
            xconv_tile(c, gdt, xcf)
            p.copy(xbstg, xcf, engine="gpsimd")
            p.dma(xcd[gdt * P:(gdt + 1) * P, :], xbstg)
        with ExitStack() as st2:
            c2 = c.scoped(st2)
            wxs = wload(c2, st2, wX, DI, 33, f"l3bx{uid}")
            xk = [c2.sb(f"l3b_xk{i}", [P, 512], BF16) for i in range(2)]
            for cc in range(TC):
                ps = c2.pp[0][0:33, :]
                for kt in range(NDI):
                    xs = xk[kt % 2]
                    p.dma(xs, xcd[kt * P:(kt + 1) * P, cc * 512:(cc + 1) * 512])
                    p.mm(ps, wxs[:, kt * 33: kt * 33 + 33], xs,
                         start=(kt == 0), stop=(kt == NDI - 1))
                p.copy(ssm[:, cc * 512:(cc + 1) * 512], ps[0:33, :], engine="scalar")
            p.barrier()
        ssmd = ctx.nc.dram_tensor(f"l3b_ssmd{uid}", [33, T], F32, kind="Internal").ap()
        p.dma(ssmd, ssm)
        stB = ExitStack()
        cB = c.scoped(stB)
        delta = cB.sb("l3b_delta", [P, T], F32)
        dxc = cB.sb("l3b_dxc", [P, T], F32)
        yac = cB.sb("l3b_yac", [P, T], F32)
        s1 = cB.sb("l3b_s1", [P, 1024], F32)
        s2 = cB.sb("l3b_s2", [P, 1024], F32)
        brow1 = cB.sb("l3b_br1", [1, 512], F32)
        crow1 = cB.sb("l3b_cr1", [1, 512], F32)
        hv = ctx.wstg[:, 512:1536]
        hsave = ctx.wstg[:, 1536:1537]
        HF = 1024
        for gdt in range(NDI):
            xconv_tile(c, gdt, xcf)
            for hf in range(2):
                dsl = delta[:, hf * HF:(hf + 1) * HF]
                for q2 in range(2):
                    cs = slice(hf * HF + q2 * 512, hf * HF + q2 * 512 + 512)
                    ds2 = slice(q2 * 512, q2 * 512 + 512)
                    p.dma(brow1, ssmd[0:1, cs])
                    p.mm(c.pp[5], ctx.ones_row, brow1, start=True, stop=True)
                    p.ts(dsl[:, ds2], c.pp[5], dtw[:, gdt:gdt + 1], ALU.mult,
                         dtb[:, gdt:gdt + 1], ALU.add)
                softplus(c, dsl, dsl, s1, s2)
            p.tt(dxc, delta, xcf, ALU.mult, engine="gpsimd")
            for n in range(NDI):
                col = n * NDI + gdt
                for hf in range(2):
                    hsl = slice(hf * HF, (hf + 1) * HF)
                    p.act(s1, delta[:, hsl], AF.Exp, scale=acol[:, col:col + 1])
                    p.copy(s2, dxc[:, hsl], engine="gpsimd")
                    for q2 in range(2):
                        cs = slice(hf * HF + q2 * 512, hf * HF + q2 * 512 + 512)
                        ds = slice(q2 * 512, q2 * 512 + 512)
                        p.dma(brow1, ssmd[1 + n:2 + n, cs])
                        p.mm(c.pp[5], ctx.ones_row, brow1, start=True, stop=True)
                        p.tt(s2[:, ds], s2[:, ds], c.pp[5], ALU.mult)
                    p.scan(hv, s1, s2, 0.0 if hf == 0 else hsave)
                    if hf == 0:
                        p.copy(hsave, hv[:, HF - 1:HF], engine="gpsimd")
                    for q2 in range(2):
                        cs = slice(hf * HF + q2 * 512, hf * HF + q2 * 512 + 512)
                        ds = slice(q2 * 512, q2 * 512 + 512)
                        p.dma(crow1, ssmd[17 + n:18 + n, cs])
                        p.mm(c.pp[4], ctx.ones_row, crow1, start=True, stop=True)
                        p.tt(hv[:, ds], hv[:, ds], c.pp[4], ALU.mult)
                    ysl = yac[:, hsl]
                    if n == 0:
                        p.copy(ysl, hv, engine="gpsimd")
                    else:
                        p.tt(ysl, ysl, hv, ALU.add)
            p.stt(yac, xcf, dpp[:, gdt:gdt + 1], yac, ALU.mult, ALU.add)
            with ExitStack() as st2:
                c2 = c.scoped(st2)
                wz = wload(c2, st2, wIn[:, DI + gdt * P: DI + (gdt + 1) * P], D, P,
                           f"l3bz{gdt}{uid}")
                for cc in range(TC):
                    ps = c2.pp[0] if cc % 2 == 0 else c2.pp[1]
                    for kt in range(ND):
                        p.mm(ps, wz[:, kt * P:(kt + 1) * P],
                             ctx.xn_bf[:, kt * T + cc * 512: kt * T + (cc + 1) * 512],
                             start=(kt == 0), stop=(kt == ND - 1))
                    p.act(xcf[:, cc * 512:(cc + 1) * 512], ps, AF.Silu)
                p.tt(xbstg, yac, xcf, ALU.mult)
                p.dma(xcd[gdt * P:(gdt + 1) * P, :], xbstg)
                p.barrier()
        p.barrier()
        stB.close()
        wo_from_dram(c, xcd, wOut, None, f"l3bo{uid}", d_in=DI)
        p.barrier()



def build_and_run(stage_fns, inputs, x_list, tr_in=False, tr_out=False):
    global _HOST
    _HOST = {}
    nc = bass.Bass("TRN2", target_bir_lowering=False, debug=False)
    with ExitStack() as stack:
        p = Prog(nc, stack)
        ctx = Ctx(p, stack)
        ctx.setup_consts()
        if tr_in:
            x0 = nc.dram_tensor("x0", [T, D], F32, kind="ExternalInput").ap()
            input_transpose(ctx, x0)
        else:
            xin = nc.dram_tensor("xin", [D, T], F32, kind="ExternalInput").ap()
            fm_dma_in(ctx, xin)
        p.barrier()
        for fn, uid in stage_fns:
            fn(ctx, inputs, uid)
        if tr_out:
            xo = nc.dram_tensor("xout", [T, D], F32, kind="ExternalOutput").ap()
            output_transpose(ctx, xo)
        else:
            xo = nc.dram_tensor("xout", [D, T], F32, kind="ExternalOutput").ap()
            fm_dma_out(ctx, xo)
        p.barrier()
        host = dict(_HOST)
        key = "x0" if tr_in else "xin"
        in_maps = [dict(host, **{key: np.ascontiguousarray(x)}) for x in x_list]
        res = run_bass_kernel_spmd(nc, in_maps, core_ids=list(range(len(x_list))))
        return [r["xout"] for r in res.results]


def kernel(**inputs):
    x = np.asarray(inputs["L0_attn_a_x"], dtype=np.float32)
    B = x.shape[0]
    groups = launch_groups()
    outs = None
    for gi, (stages, tr_in, tr_out) in enumerate(groups):
        if tr_in:
            xs = [x[b] for b in range(B)]
        else:
            xs = outs
        outs = build_and_run(stages, inputs, xs, tr_in=tr_in, tr_out=tr_out)
    return np.stack(outs, axis=0)


def launch_groups():
    S = STAGES
    n = len(S)
    return [([S[i]], i == 0, i == n - 1) for i in range(n)]


STAGES = [
    (st_l0_attn_a, "a"), (st_l0_ffn, "a"), (st_l0_attn_b, "a"), (st_l0_ffn, "b"),
    (st_l1_attn_a, "a"), (st_l1_ffn, "a"), (st_l1_attn_b, "a"), (st_l1_ffn, "b"),
    (st_l2_attn_a, "a"), (st_l2_ffn, "a"), (st_l2_attn_b, "a"), (st_l2_ffn, "b"),
    (st_l3_attn_a, "a"), (st_l3_ffn, "a"), (st_l3_attn_b, "a"), (st_l3_ffn, "b"),
]


# version 05
# speedup vs baseline: 1.0113x; 1.0113x over previous
import numpy as np
from contextlib import ExitStack

import concourse.bass as bass
import concourse.mybir as mybir
from concourse.bass_utils import run_bass_kernel_spmd

F32 = mybir.dt.float32
BF16 = mybir.dt.bfloat16
AF = mybir.ActivationFunctionType
ALU = mybir.AluOpType
AX = mybir.AxisListType

ENGINES = ("tensor", "vector", "scalar", "gpsimd", "sync")

P = 128
T = 2048
D = 1024
NT = T // P
ND = D // P
TC = T // 512


def _ap_key(ap):
    t = getattr(ap, "tensor", None)
    if t is None:
        t = ap
    n = getattr(t, "name", None)
    if n is None:
        n = str(t)
    return n


class Prog:

    DMA_SLOTS = {"sync": 8, "scalar": 2, "gpsimd": 2}

    def __init__(self, nc, stack):
        self.nc = nc
        self.stack = stack
        self.sem_cmp = {e: stack.enter_context(nc.semaphore(f"sc_{e}")) for e in ENGINES}
        self.dma_sems = {e: [stack.enter_context(nc.semaphore(f"sd_{e}{i}"))
                             for i in range(n)]
                         for e, n in self.DMA_SLOTS.items()}
        self.cnt = {(e, "cmp"): 0 for e in ENGINES}
        for e, n in self.DMA_SLOTS.items():
            for i in range(n):
                self.cnt[(e, ("dma", i))] = 0
        self.rr = {e: 0 for e in self.DMA_SLOTS}
        self.writers = {}
        self.readers = {}
        self.waited = {e: {} for e in ENGINES}

    def _sem_of(self, se, kind):
        if kind == "cmp":
            return self.sem_cmp[se]
        return self.dma_sems[se][kind[1]]

    def _wait(self, e, src):
        se, kind, cnt = src
        if se == e == "tensor" and kind == "cmp":
            return
        w = self.waited[e]
        if w.get((se, kind), 0) >= cnt:
            return
        getattr(self.nc, e).wait_ge(self._sem_of(se, kind), cnt)
        w[(se, kind)] = cnt

    def _deps(self, e, reads, writes):
        deps = {}
        for ap in reads:
            k = _ap_key(ap)
            for (ek, cnt) in self.writers.get(k, {}).items():
                deps[ek] = max(deps.get(ek, 0), cnt)
        for ap in writes:
            k = _ap_key(ap)
            for dct in (self.writers.get(k, {}), self.readers.get(k, {})):
                for (ek, cnt) in dct.items():
                    deps[ek] = max(deps.get(ek, 0), cnt)
        for (se, kind), cnt in sorted(deps.items(), key=str):
            self._wait(e, (se, kind, cnt))

    def _record(self, e, kind, cnt, reads, writes):
        for ap in reads:
            self.readers.setdefault(_ap_key(ap), {})[(e, kind)] = cnt
        for ap in writes:
            k = _ap_key(ap)
            self.writers[k] = {(e, kind): cnt}
            self.readers[k] = {}

    def emit(self, e, fn, reads=(), writes=(), dma=False):
        if dma:
            slot = self.rr[e] % len(self.dma_sems[e])
            self.rr[e] += 1
            kind = ("dma", slot)
            prev = self.cnt[(e, kind)]
            self._deps(e, reads, writes)
            if prev > 0:
                self._wait(e, (e, kind, prev))
            inst = fn(getattr(self.nc, e))
            self.cnt[(e, kind)] = prev + 16
            inst.then_inc(self.dma_sems[e][slot], 16)
            self._record(e, kind, prev + 16, reads, writes)
            return inst
        self._deps(e, reads, writes)
        inst = fn(getattr(self.nc, e))
        self.cnt[(e, "cmp")] += 1
        cnt = self.cnt[(e, "cmp")]
        inst.then_inc(self.sem_cmp[e], 1)
        self._record(e, "cmp", cnt, reads, writes)
        return inst

    def barrier(self):
        for e in ENGINES:
            for (se, kind), c in self.cnt.items():
                if c:
                    self._wait(e, (se, kind, c))

    def dma(self, out, in_, e="sync", **kw):
        self.emit(e, lambda g: g.dma_start(out=out, in_=in_, **kw),
                  reads=[in_], writes=[out], dma=True)

    def mm(self, out, lhsT, rhs, start, stop):
        self.emit("tensor", lambda g: g.matmul(out, lhsT, rhs, start=start, stop=stop),
                  reads=[lhsT, rhs], writes=[out])

    def transpose(self, out, in_, ident):
        self.emit("tensor", lambda g: g.transpose(out, in_, ident),
                  reads=[in_, ident], writes=[out])

    def act(self, out, in_, func, bias=None, scale=None, engine="scalar"):
        kw = {}
        reads = [in_]
        if bias is not None:
            kw["bias"] = bias
            if not isinstance(bias, (int, float)):
                reads.append(bias)
            elif float(bias) not in (0.0, 1.0):
                reads.append(self.ctx.regconst(float(bias)))
        if scale is not None:
            kw["scale"] = scale
            if not isinstance(scale, (int, float)):
                reads.append(scale)
        self.emit(engine, lambda g: g.activation(out, in_, func, **kw),
                  reads=reads, writes=[out])

    def tt(self, out, a, b, op, engine="vector"):
        self.emit(engine, lambda g: g.tensor_tensor(out, a, b, op),
                  reads=[a, b], writes=[out])

    def ts(self, out, in_, s1, op0, s2=None, op1=None, engine="vector"):
        if engine == "gpsimd":
            engine = "vector"
        reads = [in_]
        if not isinstance(s1, (int, float)):
            reads.append(s1)
        if s2 is not None and not isinstance(s2, (int, float)):
            reads.append(s2)
        if s2 is None:
            self.emit(engine, lambda g: g.tensor_scalar(out, in_, s1, None, op0),
                      reads=reads, writes=[out])
        else:
            self.emit(engine, lambda g: g.tensor_scalar(out, in_, s1, s2, op0, op1),
                      reads=reads, writes=[out])

    def stt(self, out, in0, scalar, in1, op0, op1, engine="vector"):
        if engine == "gpsimd":
            engine = "vector"
        reads = [in0, in1]
        if not isinstance(scalar, (int, float)):
            reads.append(scalar)
        self.emit(engine, lambda g: g.scalar_tensor_tensor(out, in0, scalar, in1, op0, op1),
                  reads=reads, writes=[out])

    def red(self, out, in_, op, engine="vector", negate=None):
        kw = {}
        if negate is not None:
            kw["negate"] = negate
        self.emit(engine, lambda g: g.tensor_reduce(out, in_, AX.X, op, **kw),
                  reads=[in_], writes=[out])

    def copy(self, out, in_, engine="vector"):
        if engine == "scalar":
            self.emit(engine, lambda g: g.copy(out, in_), reads=[in_], writes=[out])
        else:
            self.emit(engine, lambda g: g.tensor_copy(out, in_), reads=[in_], writes=[out])

    def memset(self, ap, val, engine="vector"):
        self.emit(engine, lambda g: g.memset(ap, val), reads=[], writes=[ap])

    def recip(self, out, in_):
        self.emit("vector", lambda g: g.reciprocal(out, in_), reads=[in_], writes=[out])

    def max8(self, out, in_):
        self.emit("vector", lambda g: g.max(out=out, in_=in_), reads=[in_], writes=[out])

    def scan(self, out, a, b, initial, op0=ALU.mult, op1=ALU.add):
        reads = [a, b]
        if not isinstance(initial, (int, float)):
            reads.append(initial)
        self.emit("vector", lambda g: g.tensor_tensor_scan(out, a, b, initial, op0, op1),
                  reads=reads, writes=[out])


_NAME_N = [0]


class Ctx:

    def __init__(self, p, stack):
        self.p = p
        self.stack = stack
        self.nc = p.nc
        self.hbm = {}
        self.xres = self.sb("xres", [P, ND * T], F32)
        self.xn_bf = self.sb("xn_bf", [P, ND * T], BF16)
        self.ones_row = self.sb("ones_row", [1, P], F32)
        self.ones_col = self.sb("ones_col", [P, 1], F32)
        self.ones_col_bf = self.sb("ones_col_bf", [P, 1], BF16)
        p.memset(self.ones_row, 1.0)
        p.memset(self.ones_col, 1.0)
        p.memset(self.ones_col_bf, 1.0)

    def sb(self, name, shape, dtype=F32):
        _NAME_N[0] += 1
        return self.stack.enter_context(
            self.nc.sbuf_tensor(f"{name}_{_NAME_N[0]}", shape, dtype))[:]

    def ps(self, name, shape, dtype=F32):
        _NAME_N[0] += 1
        return self.stack.enter_context(
            self.nc.psum_tensor(f"{name}_{_NAME_N[0]}", shape, dtype))[:]

    def xres_t(self, dt):
        return self.xres[:, dt * T:(dt + 1) * T]

    def xn_t(self, dt):
        return self.xn_bf[:, dt * T:(dt + 1) * T]

    def scoped(self, st):
        c = Ctx.__new__(Ctx)
        c.__dict__.update(self.__dict__)
        c.stack = st
        return c

    def regconst(self, v):
        if v not in self._consts:
            i = len(self._consts)
            assert i < 16, "const pool exhausted"
            t = self.cstpool[:, i:i + 1]
            self.p.emit("vector", lambda g: g.memset(t, v), reads=[], writes=[t])
            self.nc.const_aps.register(F32, v, t)
            self._consts[v] = t
        return self._consts[v]

    def setup_consts(self):
        self._consts = {}
        self.stack0 = self.stack
        self.p.ctx = self
        self.cstpool = self.sb("cstpool", [P, 16], F32)
        self.ident_f = self.sb("ident_f", [P, P], F32)
        self.ident_b = self.sb("ident_b", [P, P], BF16)
        self.p.dma(self.ident_f, hin(self, "c_ident_f", np.eye(P, dtype=np.float32)))
        self.p.dma(self.ident_b, hin(self, "c_ident_b", _bf(np.eye(P))))
        self.pp = [self.ps(f"pool{i}", [P, 512], F32) for i in range(6)]
        self.ppb = self.ps("poolbf", [P, 512], BF16)
        self.wstg = self.sb("wstg", [P, 2048], F32)


def _bf(a):
    import ml_dtypes
    return np.asarray(a, dtype=ml_dtypes.bfloat16)



_HOST = {}


def hin(ctx, name, arr, dtype=None):
    if name in ctx.hbm:
        return ctx.hbm[name]
    a = np.ascontiguousarray(arr)
    if dtype is None:
        dtype = F32 if a.dtype == np.float32 else BF16
    ap = ctx.nc.dram_tensor(name, list(a.shape), dtype, kind="ExternalInput").ap()
    ctx.hbm[name] = ap
    _HOST[name] = a
    return ap



def wload(ctx, stack, hbm_ap, d_in, d_out, tag):
    p = ctx.p
    nkt = (d_in + P - 1) // P
    wsb = ctx.sb(f"w_{tag}", [P, nkt * d_out], BF16)
    assert 2 * d_out <= 2048
    stg = ctx.wstg
    for kt in range(nkt):
        rows = min(P, d_in - kt * P)
        s = stg[:, (kt % 2) * d_out:(kt % 2) * d_out + d_out]
        p.dma(s[0:rows, :], hbm_ap[kt * P: kt * P + rows, :])
        p.copy(wsb[0:rows, kt * d_out:(kt + 1) * d_out], s[0:rows, :], engine="gpsimd")
    return wsb


def vload_pp(ctx, hbm_ap, n, tag, ident):
    p = ctx.p
    if n <= P:
        stg = ctx.sb(f"vst_{tag}", [1, n], F32)
        p.dma(stg, hbm_ap.rearrange("(a n) -> a n", a=1) if len(hbm_ap.shape) == 1 else hbm_ap)
        ps = ctx.pp[5][0:n, 0:1]
        p.transpose(ps, stg, ident[0:1, 0:1])
        out = ctx.sb(f"vpp_{tag}", [P, 1], F32)
        p.memset(out, 0.0)
        p.copy(out[0:n, :], ps, engine="scalar")
        return out
    nk = n // P
    assert nk <= P
    stg = ctx.sb(f"vst_{tag}", [nk, P], F32)
    p.dma(stg, hbm_ap.rearrange("(k p) -> k p", p=P))
    ps = ctx.pp[5][:, 0:nk]
    p.transpose(ps, stg, ident[0:nk, 0:nk])
    out = ctx.sb(f"vpp_{tag}", [P, nk], F32)
    p.copy(out, ps, engine="scalar")
    return out


def linear(ctx, wsb, d_in, d_out, in_sb, out_sb, in_cols=T, bias_pp=None,
           func=AF.Identity, psA=None, psB=None, out_scale=None, accum_res=False,
           in_col_off=0, w_col_off=0, w_row_tiles=None, in_T_stride=None,
           scale_pp=None, out_col_off=0):
    p = ctx.p
    nkt = w_row_tiles if w_row_tiles is not None else d_in // P
    nmt = (d_out + P - 1) // P
    wst = wsb.shape[1] // nkt
    its = in_T_stride if in_T_stride is not None else in_cols
    ncol = 512
    ncs = in_cols // ncol
    if psA is None:
        psA = ctx.pp[0]
        psB = ctx.pp[1]
    pss = [psA, psB]
    scr = ctx.sb("lin_scr", [P, ncol], F32) if accum_res else None
    i = 0
    for mt in range(nmt):
        mw = min(P, d_out - mt * P)
        for c in range(ncs):
            ps = pss[i % 2]
            i += 1
            for kt in range(nkt):
                lhs = wsb[:, kt * wst + w_col_off + mt * P: kt * wst + w_col_off + mt * P + mw]
                rhs = in_sb[:, kt * its + in_col_off + c * ncol: kt * its + in_col_off + (c + 1) * ncol]
                p.mm(ps[0:mw, :], lhs, rhs, start=(kt == 0), stop=(kt == nkt - 1))
            b = bias_pp[:, mt:mt + 1] if bias_pp is not None else None
            sc = out_scale if out_scale is not None else 1.0
            if scale_pp is not None:
                sc = scale_pp[:, mt:mt + 1]
            if accum_res:
                p.act(scr, ps[0:mw, :], func, bias=b, scale=sc)
                xr = ctx.xres[:, mt * T + in_col_off + c * ncol: mt * T + in_col_off + (c + 1) * ncol]
                p.tt(xr, xr, scr, ALU.add, engine="vector" if (mt + c) % 2 else "gpsimd")
            else:
                o = out_sb[:, mt * in_cols + out_col_off + c * ncol: mt * in_cols + out_col_off + (c + 1) * ncol]
                p.act(o[0:mw, :] if o.shape[0] > mw else o, ps[0:mw, :], func, bias=b, scale=sc)


def bcast_row(ctx, out_sb, row_ap, ps=None, cols=None):
    p = ctx.p
    n = cols if cols is not None else row_ap.shape[-1]
    if ps is None:
        ps = ctx.pp[5]
    for c0 in range(0, n, 512):
        c1 = min(c0 + 512, n)
        p.mm(ps[:, 0:c1 - c0], ctx.ones_row, row_ap[0:1, c0:c1], start=True, stop=True)
        p.copy(out_sb[:, c0:c1], ps[:, 0:c1 - c0], engine="scalar")


def colsum(ctx, ps_row, tile_bf, start, stop):
    ctx.p.mm(ps_row, ctx.ones_col_bf, tile_bf, start=start, stop=stop)


def softplus(ctx, out, in_, scr1, scr2):
    p = ctx.p
    p.act(scr1, in_, AF.Abs)
    p.act(scr1, scr1, AF.Exp, scale=-1.0)
    p.act(scr1, scr1, AF.Ln, bias=1.0)
    p.act(scr2, in_, AF.Relu)
    p.tt(out, scr1, scr2, ALU.add)


def layernorm(ctx, need_f32=False, keep_stats=False):
    p = ctx.p
    xn_f32 = ctx.sb("ln_xnf", [P, ND * T], F32) if need_f32 else None
    if keep_stats:
        bm = ctx.sb("ln_bm", [P, T], F32)
        br = ctx.sb("ln_br", [P, T], F32)
    with ExitStack() as st:
        c = ctx.scoped(st)
        if not keep_stats:
            bm = c.sb("ln_bm", [P, T], F32)
            br = c.sb("ln_br", [P, T], F32)
        sqc = c.sb("ln_sqc", [P, 512], F32)
        r_sum = c.sb("ln_rs", [1, 512], F32)
        r_rstd = c.sb("ln_rv", [1, 512], F32)
        ps_a = ctx.pp[4][0:1, :]
        ps_b = ctx.pp[5][0:1, :]
        ps_c = ctx.pp[3]
        for cc in range(TC):
            cs = slice(cc * 512, (cc + 1) * 512)
            for dt in range(ND):
                p.mm(ps_a, ctx.ones_col, ctx.xres[:, dt * T + cc * 512: dt * T + (cc + 1) * 512],
                     start=(dt == 0), stop=(dt == ND - 1))
            for dt in range(ND):
                p.act(sqc, ctx.xres[:, dt * T + cc * 512: dt * T + (cc + 1) * 512], AF.Square)
                p.mm(ps_b, ctx.ones_col, sqc, start=(dt == 0), stop=(dt == ND - 1))
            p.ts(r_sum, ps_a, 1.0 / D, ALU.mult)
            p.act(r_rstd, r_sum, AF.Square)
            p.stt(r_rstd, ps_b, 1.0 / D, r_rstd, ALU.mult, ALU.subtract)
            p.act(r_rstd, r_rstd, AF.Ln, bias=1e-5)
            p.act(r_rstd, r_rstd, AF.Exp, scale=-0.5)
            p.mm(ps_c, ctx.ones_row, r_sum, start=True, stop=True)
            p.copy(bm[:, cs], ps_c, engine="scalar")
            p.mm(ps_c, ctx.ones_row, r_rstd, start=True, stop=True)
            p.copy(br[:, cs], ps_c, engine="scalar")
        sq = c.sb("ln_sq", [P, T], F32)
        for dt in range(ND):
            p.tt(sq, ctx.xres_t(dt), bm, ALU.subtract)
            if need_f32:
                xt = xn_f32[:, dt * T:(dt + 1) * T]
                p.tt(xt, sq, br, ALU.mult)
                p.copy(ctx.xn_t(dt), xt, engine="gpsimd")
            else:
                p.tt(ctx.xn_t(dt), sq, br, ALU.mult)
        p.barrier()
    if keep_stats:
        return xn_f32, bm, br
    return xn_f32, None, None



def linear_sbacc(ctx, wsb, nkt, d_out, in_sb, in_cols, out_acc, out_T_stride,
                 first, scr, in_col_off=0, w_col_off=0, wst=None, in_T_stride=None,
                 psA=None, psB=None):
    p = ctx.p
    nmt = d_out // P
    wst = wst if wst is not None else wsb.shape[1] // nkt
    its = in_T_stride if in_T_stride is not None else in_cols
    ncol = 512
    ncs = in_cols // ncol
    pss = [psA, psB]
    i = 0
    for mt in range(nmt):
        for c in range(ncs):
            ps = pss[i % 2]; i += 1
            for kt in range(nkt):
                lhs = wsb[:, kt * wst + w_col_off + mt * P: kt * wst + w_col_off + (mt + 1) * P]
                rhs = in_sb[:, kt * its + in_col_off + c * ncol: kt * its + in_col_off + (c + 1) * ncol]
                p.mm(ps, lhs, rhs, start=(kt == 0), stop=(kt == nkt - 1))
            o = out_acc[:, mt * out_T_stride + c * ncol: mt * out_T_stride + (c + 1) * ncol]
            if first:
                p.copy(o, ps, engine="scalar")
            else:
                p.copy(scr, ps, engine="scalar")
                eng = "vector" if (i % 2) else "gpsimd"
                p.tt(o, o, scr, ALU.add, engine=eng)


def residual_add(ctx, src_sb, scale_pp=None, src_stride=T, nmt=ND):
    p = ctx.p
    for mt in range(nmt):
        xr = ctx.xres_t(mt)
        s = src_sb[:, mt * src_stride:(mt + 1) * src_stride]
        eng = "vector" if mt % 2 else "gpsimd"
        if scale_pp is not None:
            p.stt(xr, s, scale_pp[:, mt:mt + 1], xr, ALU.mult, ALU.add, engine=eng)
        else:
            p.tt(xr, xr, s, ALU.add, engine=eng)



def input_transpose(ctx, x0_ap):
    p = ctx.p
    with ExitStack() as st:
        c = ctx.scoped(st)
        _input_transpose(c, x0_ap)
        p.barrier()


def _input_transpose(ctx, x0_ap):
    p = ctx.p
    stg = [ctx.sb(f"itr{i}", [P, D], F32) for i in range(2)]
    pss = [ctx.pp[0][:, 0:P], ctx.pp[1][:, 0:P]]
    for tt in range(NT):
        s = stg[tt % 2]
        p.dma(s, x0_ap[tt * P:(tt + 1) * P, :])
        for dt in range(ND):
            ps = pss[dt % 2]
            p.transpose(ps, s[:, dt * P:(dt + 1) * P], ctx.ident_f)
            p.copy(ctx.xres[:, dt * T + tt * P: dt * T + (tt + 1) * P], ps, engine="scalar")


def output_transpose(ctx, xout_ap):
    p = ctx.p
    with ExitStack() as st:
        c = ctx.scoped(st)
        _output_transpose(c, xout_ap)
        p.barrier()


def _output_transpose(ctx, xout_ap):
    p = ctx.p
    stg = [ctx.sb(f"otr{i}", [P, D], F32) for i in range(2)]
    pss = [ctx.pp[0][:, 0:P], ctx.pp[1][:, 0:P]]
    for tt in range(NT):
        s = stg[tt % 2]
        for dt in range(ND):
            ps = pss[dt % 2]
            p.transpose(ps, ctx.xres[:, dt * T + tt * P: dt * T + (tt + 1) * P], ctx.ident_f)
            p.copy(s[:, dt * P:(dt + 1) * P], ps, engine="scalar")
        p.dma(xout_ap[tt * P:(tt + 1) * P, :], s)


def fm_dma_in(ctx, xin_ap):
    for dt in range(ND):
        ctx.p.dma(ctx.xres_t(dt), xin_ap[dt * P:(dt + 1) * P, :])


def fm_dma_out(ctx, xout_ap):
    for dt in range(ND):
        ctx.p.dma(xout_ap[dt * P:(dt + 1) * P, :], ctx.xres_t(dt))



def st_l2_ffn(ctx, ins, uid):
    p = ctx.p
    H = 4096
    w1T = hin(ctx, "l2f_w1T", ins["L2_ffn_w1"].T)
    w2T = hin(ctx, "l2f_w2T", ins["L2_ffn_w2"].T)
    b1 = hin(ctx, "l2f_b1", ins["L2_ffn_b1"])
    b2 = hin(ctx, "l2f_b2", ins["L2_ffn_b2"])
    with ExitStack() as st:
        c = ctx.scoped(st)
        _, bm, br = layernorm(c)
        b1pp = load_pp(c, "l2f_b1pp", ins["L2_ffn_b1"])
        b2pp = load_pp(c, "l2f_b2pp", ins["L2_ffn_b2"])
        stg = c.sb("bstg", [P, 1024], F32)
        ast = c.sb("bast", [P, 1024], F32)
        a1row = c.sb("a1row", [1, H], F32)
        a2row = c.sb("a2row", [1, D], F32)
        psr = [c.pp[2][0:1, :], c.pp[3][0:1, :]]
        for hc in range(4):
            for kt in range(ND):
                p.dma(stg, w1T[kt * P:(kt + 1) * P, hc * 1024:(hc + 1) * 1024])
                p.act(ast, stg, AF.Abs)
                for cc in range(2):
                    p.mm(psr[cc], ctx.ones_col, ast[:, cc * 512:(cc + 1) * 512],
                         start=(kt == 0), stop=(kt == ND - 1))
            for cc in range(2):
                p.copy(a1row[:, hc * 1024 + cc * 512: hc * 1024 + (cc + 1) * 512], psr[cc], engine="scalar")
        for kt in range(32):
            p.dma(stg, w2T[kt * P:(kt + 1) * P, :])
            p.act(ast, stg, AF.Abs)
            for cc in range(2):
                p.mm(psr[cc], ctx.ones_col, ast[:, cc * 512:(cc + 1) * 512],
                     start=(kt == 0), stop=(kt == 31))
        for cc in range(2):
            p.copy(a2row[:, cc * 512:(cc + 1) * 512], psr[cc], engine="scalar")
        p.ts(a1row, a1row, 1.0 / D, ALU.mult)
        p.ts(a2row, a2row, 1.0 / H, ALU.mult)
        dsc = ctx.nc.dram_tensor(f"l2f_dsc{uid}", [H + D], F32, kind="Internal").ap()
        p.dma(dsc.rearrange("(a b) -> a b", a=1)[0:1, 0:H], a1row)
        p.dma(dsc.rearrange("(a b) -> a b", a=1)[0:1, H:H + D], a2row)
        a1pp = vload_pp(c, dsc[0:H], H, f"a1pp{uid}", ctx.ident_f)
        a2pp = vload_pp(c, dsc[H:H + D], D, f"a2pp{uid}", ctx.ident_f)
        w1s = c.sb("w1s", [P, ND * 1024], BF16)
        w2s = c.sb("w2s", [P, 8 * 1024], BF16)
        hbf = c.sb("hbf", [P, 8 * 1024], BF16)
        oacc = c.sb("oacc", [P, ND * 512], F32)
        scr = c.sb("bscr", [P, 512], F32)
        psA = c.pp[0]
        psB = c.pp[1]
        for tq in range(4):
            toff = tq * 512
            for hc in range(4):
                for kt in range(ND):
                    p.dma(stg, w1T[kt * P:(kt + 1) * P, hc * 1024:(hc + 1) * 1024])
                    p.act(w1s[:, kt * 1024:(kt + 1) * 1024], stg, AF.Sign)
                for mt in range(8):
                    ps = psA if mt % 2 == 0 else psB
                    for kt in range(ND):
                        p.mm(ps, w1s[:, kt * 1024 + mt * P: kt * 1024 + (mt + 1) * P],
                             ctx.xn_bf[:, kt * T + toff: kt * T + toff + 512],
                             start=(kt == 0), stop=(kt == ND - 1))
                    gi = hc * 8 + mt
                    p.act(hbf[:, mt * 1024: mt * 1024 + 512],
                          ps, AF.Gelu,
                          bias=b1pp[:, gi:gi + 1], scale=a1pp[:, gi:gi + 1])
                for kt in range(8):
                    p.dma(stg, w2T[(hc * 8 + kt) * P:(hc * 8 + kt + 1) * P, :])
                    p.act(w2s[:, kt * 1024:(kt + 1) * 1024], stg, AF.Sign)
                linear_sbacc(ctx, w2s, 8, D, hbf, 512, oacc, 512,
                             first=(hc == 0), scr=scr, in_T_stride=1024,
                             psA=psA, psB=psB)
            for mt in range(ND):
                p.ts(oacc[:, mt * 512:(mt + 1) * 512], oacc[:, mt * 512:(mt + 1) * 512],
                     a2pp[:, mt:mt + 1], ALU.mult, b2pp[:, mt:mt + 1], ALU.add,
                     engine="vector" if mt % 2 else "gpsimd")
                xr = ctx.xres[:, mt * T + toff: mt * T + toff + 512]
                p.tt(xr, xr, oacc[:, mt * 512:(mt + 1) * 512], ALU.add,
                     engine="gpsimd" if mt % 2 else "vector")
        p.barrier()



def st_l3_ffn(ctx, ins, uid):
    p = ctx.p
    names = ["cn_w1", "cn_b1", "cn_w2", "cn_b2", "cn_w3", "cn_b3", "cn_w4", "cn_b4",
             "gc_w1", "gc_b1", "gc_w2", "gc_b2", "gc_w3", "gc_b3", "adaptive_range"]
    h = {n: hin(ctx, f"l3f_{n}", ins[f"L3_ffn_{n}"]) for n in names}
    with ExitStack() as st:
        c = ctx.scoped(st)
        xn, bm, br = layernorm(c, need_f32=True)
        w1 = wload(c, st, h["cn_w1"], D, D, f"cw1{uid}")
        w2 = wload(c, st, h["cn_w2"], D, 512, f"cw2{uid}")
        w3 = wload(c, st, h["cn_w3"], 512, 256, f"cw3{uid}")
        w4 = wload(c, st, h["cn_w4"], 256, 1, f"cw4{uid}")
        b1pp = load_pp(c, "l3f_b1pp", ins["L3_ffn_cn_b1"])
        b2pp = load_pp(c, "l3f_b2pp", ins["L3_ffn_cn_b2"])
        b3pp = load_pp(c, "l3f_b3pp", ins["L3_ffn_cn_b3"])
        b4pp = load_pp(c, "l3f_b4pp", ins["L3_ffn_cn_b4"])
        arsb = load_pp(c, "l3f_arpp", ins["L3_ffn_adaptive_range"])
        gfs = c.sb("gfs", [P, 1], F32)
        ar2 = c.sb("ar2", [P, 1], F32)
        psA = c.pp[0]
        psB = c.pp[1]
        with ExitStack() as st_g:
            cg = c.scoped(st_g)
            g1 = wload(cg, st_g, h["gc_w1"], D, 512, f"gw1{uid}")
            g2 = wload(cg, st_g, h["gc_w2"], 512, 256, f"gw2{uid}")
            g3 = wload(cg, st_g, h["gc_w3"], 256, 1, f"gw3{uid}")
            gb1pp = load_pp(cg, "l3f_gb1pp", ins["L3_ffn_gc_b1"])
            gb2pp = load_pp(cg, "l3f_gb2pp", ins["L3_ffn_gc_b2"])
            gb3pp = load_pp(cg, "l3f_gb3pp", ins["L3_ffn_gc_b3"])
            gc = cg.sb("gc", [P, ND], F32)
            gcb = cg.sb("gcb", [P, ND], BF16)
            gh1 = cg.sb("gh1", [P, 4], BF16)
            gh2 = cg.sb("gh2", [P, 2], BF16)
            for dt in range(ND):
                p.red(gc[:, dt:dt + 1], xn[:, dt * T:(dt + 1) * T], ALU.add)
            p.ts(gc, gc, 1.0 / T, ALU.mult)
            p.copy(gcb, gc)
            for mt in range(4):
                for kt in range(ND):
                    p.mm(psA[:, 0:1], g1[:, kt * 512 + mt * P: kt * 512 + (mt + 1) * P],
                         gcb[:, kt:kt + 1], start=(kt == 0), stop=(kt == ND - 1))
                p.act(gh1[:, mt:mt + 1], psA[:, 0:1], AF.Gelu, bias=gb1pp[:, mt:mt + 1])
            for mt in range(2):
                for kt in range(4):
                    p.mm(psA[:, 0:1], g2[:, kt * 256 + mt * P: kt * 256 + (mt + 1) * P],
                         gh1[:, kt:kt + 1], start=(kt == 0), stop=(kt == 3))
                p.act(gh2[:, mt:mt + 1], psA[:, 0:1], AF.Gelu, bias=gb2pp[:, mt:mt + 1])
            for kt in range(2):
                p.mm(psA[0:1, 0:1], g3[:, kt * 1: kt * 1 + 1], gh2[:, kt:kt + 1],
                     start=(kt == 0), stop=(kt == 1))
            p.act(gfs[0:1, :], psA[0:1, 0:1], AF.Sigmoid, bias=gb3pp[0:1, 0:1])
            p.ts(gfs[0:1, :], gfs[0:1, :], 0.7, ALU.mult, -0.5, ALU.add)
            p.ts(ar2, arsb, 2.0, ALU.mult)
            p.barrier()
        out = xn
        outb = ctx.xn_bf
        hbuf = c.sb("hbuf", [P, ND * 512], BF16)
        h1b = hbuf
        h2b = c.sb("h2b", [P, 4 * 512], BF16)
        h3b = hbuf[:, 0:2 * 512]
        cpl = ctx.wstg[:, 0:512]
        t1 = ctx.wstg[:, 512:1024]
        snap = ctx.wstg[:, 1024:1536]
        tfr = ctx.wstg[0:1, 1536:2048]
        HT = 512
        pairs = [(0, 4, 1.0, -1.0), (1, 5, 1.0, -1.0),
                 (2, 6, -1.0, 1.0), (3, 7, -1.0, 1.0)]
        for cyc in range(3):
            for th in range(4):
                toff = th * HT
                linear(c, w1, D, D, outb, h1b, in_cols=HT, in_col_off=toff,
                       in_T_stride=T, bias_pp=b1pp, func=AF.Gelu, psA=psA, psB=psB)
                linear(c, w2, D, 512, h1b, h2b, in_cols=HT, bias_pp=b2pp,
                       func=AF.Gelu, psA=psA, psB=psB)
                linear(c, w3, 512, 256, h2b, h3b, in_cols=HT, bias_pp=b3pp,
                       func=AF.Gelu, psA=psA, psB=psB)
                for kt in range(2):
                    p.mm(psA[0:1, :], w4[:, kt:kt + 1],
                         h3b[:, kt * HT: kt * HT + 512],
                         start=(kt == 0), stop=(kt == 1))
                p.act(tfr, psA[0:1, :], AF.Sigmoid, bias=b4pp[0:1, 0:1])
                p.ts(tfr, tfr, 0.3, ALU.mult, gfs[0:1, 0:1], ALU.add)
                p.ts(tfr, tfr, ar2[0:1, 0:1], ALU.mult, 0.1, ALU.add)
                bcast_row(c, cpl, tfr, ps=psB, cols=HT)
                for (a, b, sa, sgb) in pairs:
                    oa = out[:, a * T + toff: a * T + toff + HT]
                    ob = out[:, b * T + toff: b * T + toff + HT]
                    p.copy(snap, oa, engine="gpsimd")
                    p.tt(t1, cpl, ob, ALU.mult)
                    p.stt(oa, t1, sa, oa, ALU.mult, ALU.add)
                    p.tt(t1, cpl, snap, ALU.mult)
                    p.stt(ob, t1, sgb, ob, ALU.mult, ALU.add, engine="gpsimd")
                if cyc < 2:
                    for dt in range(ND):
                        p.copy(outb[:, dt * T + toff: dt * T + toff + HT],
                               out[:, dt * T + toff: dt * T + toff + HT],
                               engine="gpsimd" if dt % 2 else "vector")
        residual_add(ctx, out)
        p.barrier()



def pp_host(vec):
    v = np.asarray(vec, dtype=np.float32).reshape(-1)
    n = v.shape[0]
    if n < P:
        out = np.zeros((P, 1), np.float32)
        out[:n, 0] = v
        return out
    return np.ascontiguousarray(v.reshape(n // P, P).T)


def load_pp(ctx, name, vec):
    h = hin(ctx, name, pp_host(vec))
    sb = ctx.sb(f"pp_{name}", list(h.shape), F32)
    ctx.p.dma(sb, h)
    return sb


def rope_apply2(c, x_f32, rows, rotT_bf, cosE, sinE, out_bf, psA, scr_bf, scr_f, scr_f2):
    p = c.p
    for cc in range(TC):
        cs = slice(cc * 512, (cc + 1) * 512)
        p.copy(scr_bf[0:rows, 0:512], x_f32[0:rows, cs])
        p.mm(psA[0:rows, :], rotT_bf[0:rows, 0:rows], scr_bf[0:rows, 0:512],
             start=True, stop=True)
        p.copy(scr_f[0:rows, :], psA[0:rows, :], engine="scalar")
        p.tt(scr_f[0:rows, :], scr_f[0:rows, :], sinE[0:rows, cs], ALU.mult)
        p.tt(scr_f2[0:rows, :], x_f32[0:rows, cs], cosE[0:rows, cs], ALU.mult,
             engine="gpsimd")
        p.tt(out_bf[0:rows, cs], scr_f[0:rows, :], scr_f2[0:rows, :], ALU.add)


def attn_group(c, heads, dv, vtok, vst, jts_fn, mask_fn, scale, clip, out_fn,
               psS, psO, psD, psB2, s_f, pb, den, rc, rcb, o_f):
    p = c.p
    for (parts, h_id, voff) in heads:
        for ic in range(TC):
            jts = jts_fn(ic)
            nj = len(jts)
            for ji, jt in enumerate(jts):
                npart = len(parts)
                for pi, (kap, qap, dk) in enumerate(parts):
                    p.mm(psS, kap[0:dk, jt * P:(jt + 1) * P],
                         qap[0:dk, ic * 512:(ic + 1) * 512],
                         start=(pi == 0), stop=(pi == npart - 1))
                mask = mask_fn(ic, jt)
                src = psS
                if clip is not None:
                    p.ts(s_f, psS, clip, ALU.min, -clip, ALU.max)
                    src = s_f
                if mask is not None:
                    if src is psS:
                        p.tt(s_f, psS, mask, ALU.add)
                    else:
                        p.tt(s_f, s_f, mask, ALU.add)
                    src = s_f
                pbt = pb[ji % 2]
                p.act(pbt, src, AF.Exp, scale=scale)
                colsum(c, psD, pbt, start=(ji == 0), stop=(ji == nj - 1))
                p.mm(psO[0:dv, :], vtok[:, jt * vst + voff: jt * vst + voff + dv],
                     pbt, start=(ji == 0), stop=(ji == nj - 1))
            p.copy(den, psD, engine="scalar")
            p.recip(rc, den)
            p.mm(psB2, c.ones_row, rc, start=True, stop=True)
            p.copy(rcb, psB2, engine="scalar")
            p.copy(o_f[0:dv, :], psO[0:dv, :], engine="scalar")
            p.tt(o_f[0:dv, :], o_f[0:dv, :], rcb[0:dv, :], ALU.mult)
            out_fn(h_id, ic, o_f)


def make_attn_scratch(c, tag):
    psS = c.pp[2]
    psO = c.pp[3]
    psD = c.pp[4][0:1, :]
    psB2 = c.pp[5]
    s_f = c.sb(f"asf_{tag}", [P, 512], F32)
    pb = [c.sb(f"apb{i}_{tag}", [P, 512], BF16) for i in range(2)]
    den = c.sb(f"aden_{tag}", [1, 512], F32)
    rc = c.sb(f"arc_{tag}", [1, 512], F32)
    rcb = c.sb(f"arcb_{tag}", [P, 512], F32)
    o_f = c.sb(f"aof_{tag}", [P, 512], F32)
    return dict(psS=psS, psO=psO, psD=psD, psB2=psB2, s_f=s_f, pb=pb, den=den,
                rc=rc, rcb=rcb, o_f=o_f)


def causal_mask_host(q):
    jl = np.arange(P)[:, None]
    il = np.arange(512)[None, :]
    j = q * P + jl
    return np.where(j > il, -30000.0, 0.0).astype(np.float32)


def window_mask_host(q, w=512):
    jl = np.arange(P)[:, None]
    il = np.arange(512)[None, :]
    j = q * P + jl
    d = il - j
    return np.where((d < 0) | (d > w - 1), -30000.0, 0.0).astype(np.float32)


def rope_tables_l0b():
    half = 32
    theta = 1.0 / 10000.0 ** (np.arange(half, dtype=np.float64) / half)
    pos = np.arange(T, dtype=np.float64)
    fr = pos[:, None] * theta[None, :]
    cos = np.cos(fr).T
    sin = np.sin(fr).T
    cosE = np.concatenate([cos, cos], 0)
    sinE = np.concatenate([sin, sin], 0)
    R = np.zeros((64, 64), np.float64)
    for i in range(32):
        R[i, i + 32] = -1.0
        R[i + 32, i] = 1.0
    return cosE, sinE, R


def rope_tables_l2b():
    n = 32
    j = np.arange(n, dtype=np.float64)
    freqs = 1.0 * (40.0 / 1.0) ** (j / (n - 1))
    pos = np.arange(T, dtype=np.float64) / T
    th = 2.0 * np.pi * pos[:, None] * freqs[None, :]
    cos = np.cos(th).T
    sin = np.sin(th).T
    cosE = np.concatenate([cos, cos], 0)
    sinE = np.concatenate([sin, sin], 0)
    R = np.zeros((64, 64), np.float64)
    for i in range(32):
        R[i, i + 32] = -1.0
        R[i + 32, i] = 1.0
    return cosE, sinE, R


def rope_tables_l1b():
    d = 32
    inv = 10000.0 ** (-np.arange(0, d, 2, dtype=np.float64) / d)
    ang = np.arange(T, dtype=np.float64)[:, None] * inv[None, :]
    cos = np.cos(ang).T
    sin = np.sin(ang).T
    cosE = np.repeat(cos, 2, axis=0)
    sinE = np.repeat(sin, 2, axis=0)
    R = np.zeros((32, 32), np.float64)
    for k in range(16):
        R[2 * k, 2 * k + 1] = -1.0
        R[2 * k + 1, 2 * k] = 1.0
    return cosE, sinE, R


def _tile4(a, reps):
    return np.ascontiguousarray(np.tile(a, (reps, 1)).astype(np.float32))


def _blockdiag(R, reps):
    n = R.shape[0]
    out = np.zeros((n * reps, n * reps), np.float64)
    for i in range(reps):
        out[i * n:(i + 1) * n, i * n:(i + 1) * n] = R
    return out



def proj_fm_group(c, wsb, bias_pp, bi0, dst_f32, nkt=ND, wst=512, wcol0=0):
    p = c.p
    for cc in range(TC):
        ps = c.pp[0] if cc % 2 == 0 else c.pp[1]
        for kt in range(nkt):
            p.mm(ps, wsb[:, kt * wst + wcol0: kt * wst + wcol0 + P],
                 c.xn_bf[:, kt * T + cc * 512: kt * T + (cc + 1) * 512],
                 start=(kt == 0), stop=(kt == nkt - 1))
        b = bias_pp[:, bi0:bi0 + 1] if bias_pp is not None else None
        p.act(dst_f32[:, cc * 512:(cc + 1) * 512], ps, AF.Identity, bias=b)


def wo_from_dram(c, odram, woT_hbm, bias_pp, uid, d_in=D):
    p = c.p
    nkt = d_in // P
    with ExitStack() as st:
        c2 = c.scoped(st)
        wo = wload(c2, st, woT_hbm, d_in, D, f"wo{uid}")
        ostg = c2.sb(f"ostg{uid}", [P, nkt * 512], BF16)
        scr = c2.sb(f"oscr{uid}", [P, 512], F32)
        for cc in range(TC):
            for kt in range(nkt):
                p.dma(ostg[:, kt * 512:(kt + 1) * 512],
                      odram[kt * P:(kt + 1) * P, cc * 512:(cc + 1) * 512])
            for mt in range(ND):
                ps = c.pp[0] if mt % 2 == 0 else c.pp[1]
                for kt in range(nkt):
                    p.mm(ps, wo[:, kt * D + mt * P: kt * D + (mt + 1) * P],
                         ostg[:, kt * 512:(kt + 1) * 512],
                         start=(kt == 0), stop=(kt == nkt - 1))
                b = bias_pp[:, mt:mt + 1] if bias_pp is not None else None
                p.act(scr, ps, AF.Identity, bias=b)
                xr = c.xres[:, mt * T + cc * 512: mt * T + (cc + 1) * 512]
                p.tt(xr, xr, scr, ALU.add, engine="vector" if mt % 2 else "gpsimd")
        p.barrier()


def st_l2_attn_b(ctx, ins, uid):
    p = ctx.p
    qkvT = hin(ctx, "l2b_qkvT", ins["L2_attn_b_qkv_w"].T)
    projT = hin(ctx, "l2b_projT", ins["L2_attn_b_proj_w"].T)
    cosE, sinE, R = rope_tables_l2b()
    cosT = hin(ctx, "l2b_cos", _tile4(cosE, 2))
    sinT = hin(ctx, "l2b_sin", _tile4(sinE, 2))
    rotT = hin(ctx, "l2b_rot", _bf(_blockdiag(R, 2).T))
    odram = ctx.nc.dram_tensor(f"l2b_od{uid}", [D, T], BF16, kind="Internal").ap()
    with ExitStack() as st:
        c = ctx.scoped(st)
        layernorm(c)
        qb_pp = load_pp(c, "l2b_qb", ins["L2_attn_b_qkv_b"][0:1024])
        kb_pp = load_pp(c, "l2b_kb", ins["L2_attn_b_qkv_b"][1024:2048])
        pjb_pp = load_pp(c, "l2b_pjb", ins["L2_attn_b_proj_b"])
        vb_row = hin(ctx, "l2b_vb_row",
                     np.ascontiguousarray(ins["L2_attn_b_qkv_b"][None, 2048:3072].astype(np.float32)))
        cosS = c.sb("l2b_cosS", [P, T], F32); p.dma(cosS, cosT)
        sinS = c.sb("l2b_sinS", [P, T], F32); p.dma(sinS, sinT)
        rotS = c.sb("l2b_rotS", [P, P], BF16); p.dma(rotS, rotT)
        vbr = c.sb("l2b_vbr", [1, 1024], F32); p.dma(vbr, vb_row)
        vbb = c.sb("l2b_vbb", [P, 1024], F32)
        bcast_row(c, vbb, vbr)
        for g4 in range(4):
            with ExitStack() as st2:
                c2 = c.scoped(st2)
                col0 = g4 * 256
                wq = wload(c2, st2, qkvT[:, col0:col0 + 256], D, 256, f"l2bq{g4}{uid}")
                wk = wload(c2, st2, qkvT[:, 1024 + col0:1024 + col0 + 256], D, 256, f"l2bk{g4}{uid}")
                wv = wload(c2, st2, qkvT[:, 2048 + col0:2048 + col0 + 256], D, 256, f"l2bv{g4}{uid}")
                qf = c2.sb("l2b_qf", [P, T], F32)
                sbf = c2.sb("l2b_sbf", [P, 512], BF16)
                sf1 = c2.sb("l2b_sf1", [P, 512], F32)
                sf2 = c2.sb("l2b_sf2", [P, 512], F32)
                qro = c2.sb("l2b_qro", [P, 2 * T], BF16)
                kro = c2.sb("l2b_kro", [P, 2 * T], BF16)
                vtok = c2.sb("l2b_vtok", [P, NT * 256], BF16)
                for gg in range(2):
                    gi = g4 * 2 + gg
                    for (w, bpp, dst) in ((wq, qb_pp, qro), (wk, kb_pp, kro)):
                        proj_fm_group(c2, w, bpp, gi, qf, wst=256, wcol0=gg * P)
                        rope_apply2(c2, qf, P, rotS, cosS, sinS,
                                    dst[:, gg * T:(gg + 1) * T], c2.pp[0], sbf, sf1, sf2)
                for tt in range(NT):
                    ps = c2.pp[0] if tt % 2 == 0 else c2.pp[1]
                    for kt in range(ND):
                        p.mm(ps[:, 0:256], ctx.xn_bf[:, kt * T + tt * P: kt * T + (tt + 1) * P],
                             wv[:, kt * 256:(kt + 1) * 256],
                             start=(kt == 0), stop=(kt == ND - 1))
                    p.copy(sf1[:, 0:256], ps[:, 0:256], engine="scalar")
                    p.tt(vtok[:, tt * 256:(tt + 1) * 256], sf1[:, 0:256],
                         vbb[:, col0:col0 + 256], ALU.add)
                sc = make_attn_scratch(c2, f"l2b{g4}")
                heads = []
                for hl in range(4):
                    tq = (hl * 64) // P
                    r0 = (hl * 64) % P
                    kap = kro[r0:r0 + 64, tq * T:(tq + 1) * T]
                    qap = qro[r0:r0 + 64, tq * T:(tq + 1) * T]
                    heads.append(([(kap, qap, 64)], g4 * 4 + hl, hl * 64))
                obf = c2.sb("l2b_obf", [64, 512], BF16)

                def out_fn(h_id, ic, o_f):
                    p.copy(obf, o_f[0:64, :])
                    p.dma(odram[h_id * 64:(h_id + 1) * 64, ic * 512:(ic + 1) * 512], obf)
                attn_group(c2, heads, 64, vtok, 256, lambda ic: list(range(NT)),
                           lambda ic, jt: None, 0.125, None, out_fn, **sc)
                p.barrier()
        wo_from_dram(c, odram, projT, pjb_pp, f"l2b{uid}")
        p.barrier()


def st_l2_attn_a(ctx, ins, uid):
    p = ctx.p
    I = lambda n: ins[f"L2_attn_a_{n}"]
    wdq = hin(ctx, "l2a_wdq", I("W_DQ"))
    wdkv = hin(ctx, "l2a_wdkv", I("W_DKV"))
    wuq = hin(ctx, "l2a_wuq", I("W_UQ"))
    wuk = hin(ctx, "l2a_wuk", I("W_UK"))
    wuv = hin(ctx, "l2a_wuv", I("W_UV"))
    wqr = hin(ctx, "l2a_wqr", I("W_QR"))
    wkr = hin(ctx, "l2a_wkr", I("W_KR"))
    woT = hin(ctx, "l2a_wo", I("W_O"))
    odram = ctx.nc.dram_tensor(f"l2a_od{uid}", [D, T], BF16, kind="Internal").ap()
    with ExitStack() as st:
        c = ctx.scoped(st)
        layernorm(c)
        bdq = load_pp(c, "l2a_bdq", I("b_DQ"))
        bdkv = load_pp(c, "l2a_bdkv", I("b_DKV"))
        buq = load_pp(c, "l2a_buq", I("b_UQ"))
        buk = load_pp(c, "l2a_buk", I("b_UK"))
        buv_row = hin(ctx, "l2a_buv_row", np.ascontiguousarray(I("b_UV")[None, :].astype(np.float32)))
        bqr = load_pp(c, "l2a_bqr", I("b_QR"))
        bkr = load_pp(c, "l2a_bkr", I("b_KR"))
        bo = load_pp(c, "l2a_bo", I("b_O"))
        cq = c.sb("l2a_cq", [32, T], BF16)
        ckv = c.sb("l2a_ckv", [32, T], BF16)
        kr = c.sb("l2a_kr", [32, T], BF16)
        wdq_s = wload(c, st, wdq, D, 32, f"l2a_dq{uid}")
        wdkv_s = wload(c, st, wdkv, D, 32, f"l2a_dkv{uid}")
        wkr_s = wload(c, st, wkr, D, 32, f"l2a_kr{uid}")
        for (wsb, bpp, dst) in ((wdq_s, bdq, cq), (wdkv_s, bdkv, ckv), (wkr_s, bkr, kr)):
            for cc in range(TC):
                ps = c.pp[0] if cc % 2 == 0 else c.pp[1]
                for kt in range(ND):
                    p.mm(ps[0:32, :], wsb[:, kt * 32: kt * 32 + 32],
                         ctx.xn_bf[:, kt * T + cc * 512: kt * T + (cc + 1) * 512],
                         start=(kt == 0), stop=(kt == ND - 1))
                p.act(dst[:, cc * 512:(cc + 1) * 512], ps[0:32, :], AF.Identity,
                      bias=bpp[0:32, 0:1])
        wuq_s = wload(c, st, wuq, 32, 1024, f"l2a_uq{uid}")
        wuk_s = wload(c, st, wuk, 32, 1024, f"l2a_uk{uid}")
        wuv_s = wload(c, st, wuv, 32, 1024, f"l2a_uv{uid}")
        wqr_s = wload(c, st, wqr, 32, 256, f"l2a_qr{uid}")
        qr = c.sb("l2a_qr", [P, 2 * T], BF16)
        buvr = c.sb("l2a_buvr", [1, 1024], F32); p.dma(buvr, buv_row)
        buvb = c.sb("l2a_buvb", [P, 1024], F32)
        bcast_row(c, buvb, buvr)
        for mt in range(2):
            for cc in range(TC):
                ps = c.pp[0] if cc % 2 == 0 else c.pp[1]
                p.mm(ps, wqr_s[:, mt * P:(mt + 1) * P][0:32, :], cq[:, cc * 512:(cc + 1) * 512],
                     start=True, stop=True)
                p.act(qr[:, mt * T + cc * 512: mt * T + (cc + 1) * 512], ps, AF.Identity,
                      bias=bqr[:, mt:mt + 1])
        qrx = c.sb("l2a_qrx", [64, 2 * T], BF16)
        for tile in range(2):
            p.dma(qrx[:, tile * T:(tile + 1) * T], qr[64:128, tile * T:(tile + 1) * T])
        kr3 = c.sb("l2a_kr3", [96, T], BF16)
        for b in range(3):
            p.dma(kr3[b * 32:(b + 1) * 32, :], kr)
        for g2 in range(4):
            with ExitStack() as st2:
                c2 = c.scoped(st2)
                qc = c2.sb("l2a_qc", [P, 2 * T], BF16)
                kc = c2.sb("l2a_kc", [P, 2 * T], BF16)
                vtok = c2.sb("l2a_vt", [P, NT * 256], BF16)
                sf1 = c2.sb("l2a_sf1", [P, 512], F32)
                for mt in range(2):
                    col = g2 * 256 + mt * P
                    for cc in range(TC):
                        ps = c2.pp[0] if cc % 2 == 0 else c2.pp[1]
                        p.mm(ps, wuq_s[:, col:col + P][0:32, :], cq[:, cc * 512:(cc + 1) * 512],
                             start=True, stop=True)
                        p.act(qc[:, mt * T + cc * 512: mt * T + (cc + 1) * 512], ps,
                              AF.Identity, bias=buq[:, g2 * 2 + mt: g2 * 2 + mt + 1])
                        ps2 = c2.pp[2]
                        p.mm(ps2, wuk_s[:, col:col + P][0:32, :], ckv[:, cc * 512:(cc + 1) * 512],
                             start=True, stop=True)
                        p.act(kc[:, mt * T + cc * 512: mt * T + (cc + 1) * 512], ps2,
                              AF.Identity, bias=buk[:, g2 * 2 + mt: g2 * 2 + mt + 1])
                for tt in range(NT):
                    ps = c2.pp[0] if tt % 2 == 0 else c2.pp[1]
                    p.mm(ps[:, 0:256], ckv[:, tt * P:(tt + 1) * P],
                         wuv_s[0:32, g2 * 256:(g2 + 1) * 256], start=True, stop=True)
                    p.copy(sf1[:, 0:256], ps[:, 0:256], engine="scalar")
                    p.tt(vtok[:, tt * 256:(tt + 1) * 256], sf1[:, 0:256],
                         buvb[:, g2 * 256:(g2 + 1) * 256], ALU.add)
                sc = make_attn_scratch(c2, f"l2a{g2}")
                heads = []
                for hl in range(2):
                    h_id = g2 * 2 + hl
                    kcap = kc[:, hl * T:(hl + 1) * T]
                    qcap = qc[:, hl * T:(hl + 1) * T]
                    rr = h_id * 32
                    tile, off = rr // P, rr % P
                    if off < 64:
                        qrap = qr[off:off + 32, tile * T:(tile + 1) * T]
                        krap = kr3[off:off + 32, :]
                    else:
                        qrap = qrx[off - 64:off - 32, tile * T:(tile + 1) * T]
                        krap = kr3[off - 64:off - 32, :]
                    heads.append(([(kcap, qcap, 128), (krap, qrap, 32)], h_id, hl * 128))
                obf = c2.sb("l2a_obf", [P, 512], BF16)

                def out_fn(h_id, ic, o_f):
                    p.copy(obf, o_f[0:P, :])
                    p.dma(odram[h_id * 128:(h_id + 1) * 128, ic * 512:(ic + 1) * 512], obf)
                attn_group(c2, heads, 128, vtok, 256, lambda ic: list(range(NT)),
                           lambda ic, jt: None, 1.0 / np.sqrt(160.0), None, out_fn, **sc)
                p.barrier()
        wo_from_dram(c, odram, woT, bo, f"l2a{uid}")
        p.barrier()


def st_l1_attn_b(ctx, ins, uid):
    p = ctx.p
    I = lambda n: ins[f"L1_attn_b_{n}"]
    wdq = hin(ctx, "l1b_wdq", I("W_DQ"))
    wdkv = hin(ctx, "l1b_wdkv", I("W_DKV"))
    wuq = hin(ctx, "l1b_wuq", I("W_UQ"))
    wuk = hin(ctx, "l1b_wuk", I("W_UK"))
    wuv = hin(ctx, "l1b_wuv", I("W_UV"))
    wqr = hin(ctx, "l1b_wqr", I("W_QR"))
    wkr = hin(ctx, "l1b_wkr", I("W_KR"))
    woT = hin(ctx, "l1b_wo", I("W_O"))
    cosE, sinE, R = rope_tables_l1b()
    cosT = hin(ctx, "l1b_cos", _tile4(cosE, 4))
    sinT = hin(ctx, "l1b_sin", _tile4(sinE, 4))
    rotT = hin(ctx, "l1b_rot", _bf(_blockdiag(R, 4).T))
    for q in range(4):
        hin(ctx, f"c_cmask{q}", causal_mask_host(q))
    odram = ctx.nc.dram_tensor(f"l1b_od{uid}", [D, T], BF16, kind="Internal").ap()
    with ExitStack() as st:
        c = ctx.scoped(st)
        layernorm(c)
        cosS = c.sb("l1b_cosS", [P, T], F32); p.dma(cosS, ctx.hbm["l1b_cos"])
        sinS = c.sb("l1b_sinS", [P, T], F32); p.dma(sinS, ctx.hbm["l1b_sin"])
        rotS = c.sb("l1b_rotS", [P, P], BF16); p.dma(rotS, ctx.hbm["l1b_rot"])
        masks = [c.sb(f"l1b_m{q}", [P, 512], F32) for q in range(4)]
        for q in range(4):
            p.dma(masks[q], ctx.hbm[f"c_cmask{q}"])
        cq = c.sb("l1b_cq", [P, 2 * T], BF16)
        ckv = c.sb("l1b_ckv", [P, 2 * T], BF16)
        kr = c.sb("l1b_kr", [32, T], BF16)
        kr2 = c.sb("l1b_kr2", [64, T], BF16)
        sf1 = c.sb("l1b_sf1", [P, 512], F32)
        sf2 = c.sb("l1b_sf2", [P, 512], F32)
        with ExitStack() as stk:
            ck = c.scoped(stk)
            wdq_s = wload(ck, stk, wdq, D, 256, f"l1b_dq{uid}")
            wdkv_s = wload(ck, stk, wdkv, D, 256, f"l1b_dkv{uid}")
            wkr_s = wload(ck, stk, wkr, D, 32, f"l1b_kr{uid}")
            krf = ck.sb("l1b_krf", [32, T], F32)
            sbf32 = ck.sb("l1b_sbf32", [32, 512], BF16)
            for (wsb, dst, dout) in ((wdq_s, cq, 256), (wdkv_s, ckv, 256)):
                linear(ck, wsb, D, dout, ctx.xn_bf, dst, psA=c.pp[0], psB=c.pp[1])
            for cc in range(TC):
                ps = c.pp[0]
                for kt in range(ND):
                    p.mm(ps[0:32, :], wkr_s[:, kt * 32: kt * 32 + 32],
                         ctx.xn_bf[:, kt * T + cc * 512: kt * T + (cc + 1) * 512],
                         start=(kt == 0), stop=(kt == ND - 1))
                p.copy(krf[:, cc * 512:(cc + 1) * 512], ps[0:32, :], engine="scalar")
            rope_apply2(ck, krf, 32, rotS, cosS, sinS, kr, c.pp[0], sbf32, sf1, sf2)
            for b in range(2):
                p.dma(kr2[b * 32:(b + 1) * 32, :], kr)
            p.barrier()

        def jts_fn(ic):
            return list(range(ic * 4 + 4))

        def mask_fn(ic, jt):
            q = jt - ic * 4
            return masks[q] if q >= 0 else None
        for g4 in range(4):
            with ExitStack() as st2:
                c2 = c.scoped(st2)
                wuq_s = wload(c2, st2, wuq[:, g4 * 256:(g4 + 1) * 256], 256, 256, f"l1b_uq{g4}{uid}")
                wuk_s = wload(c2, st2, wuk[:, g4 * 256:(g4 + 1) * 256], 256, 256, f"l1b_uk{g4}{uid}")
                wuv_s = wload(c2, st2, wuv[:, g4 * 256:(g4 + 1) * 256], 256, 256, f"l1b_uv{g4}{uid}")
                wqr_s = wload(c2, st2, wqr[:, g4 * P:(g4 + 1) * P], 256, P, f"l1b_qr{g4}{uid}")
                qrr = c2.sb("l1b_qrr", [P, T], BF16)
                qr2 = c2.sb("l1b_qr2", [64, T], BF16)
                with ExitStack() as str_:
                    cr = c2.scoped(str_)
                    qrf = cr.sb("l1b_qrf", [P, T], F32)
                    sbf = cr.sb("l1b_sbf", [P, 512], BF16)
                    for cc in range(TC):
                        ps = c2.pp[0]
                        for kt in range(2):
                            p.mm(ps, wqr_s[:, kt * P:(kt + 1) * P],
                                 cq[:, kt * T + cc * 512: kt * T + (cc + 1) * 512],
                                 start=(kt == 0), stop=(kt == 1))
                        p.copy(qrf[:, cc * 512:(cc + 1) * 512], ps, engine="scalar")
                    rope_apply2(cr, qrf, P, rotS, cosS, sinS, qrr, c2.pp[0], sbf, sf1, sf2)
                    p.dma(qr2, qrr[64:128, :])
                    p.barrier()
                qc = c2.sb("l1b_qc", [P, 2 * T], BF16)
                kc = c2.sb("l1b_kc", [P, 2 * T], BF16)
                vtok = c2.sb("l1b_vt", [P, NT * 256], BF16)
                linear(c2, wuq_s, 256, 256, cq, qc, psA=c2.pp[0], psB=c2.pp[1])
                linear(c2, wuk_s, 256, 256, ckv, kc, psA=c2.pp[0], psB=c2.pp[1])
                for tt in range(NT):
                    ps = c2.pp[0] if tt % 2 == 0 else c2.pp[1]
                    for kt in range(2):
                        p.mm(ps[:, 0:256], ckv[:, kt * T + tt * P: kt * T + (tt + 1) * P],
                             wuv_s[:, kt * 256:(kt + 1) * 256],
                             start=(kt == 0), stop=(kt == 1))
                    p.copy(vtok[:, tt * 256:(tt + 1) * 256], ps[:, 0:256], engine="scalar")
                sc = make_attn_scratch(c2, f"l1b{g4}")
                heads = []
                for hl in range(4):
                    h_id = g4 * 4 + hl
                    r0 = (hl * 64) % P
                    tq = (hl * 64) // P
                    kcap = kc[r0:r0 + 64, tq * T:(tq + 1) * T]
                    qcap = qc[r0:r0 + 64, tq * T:(tq + 1) * T]
                    r0r = hl * 32
                    if r0r < 64:
                        qrap = qrr[r0r:r0r + 32, :]
                        krap = kr2[r0r:r0r + 32, :]
                    else:
                        qrap = qr2[r0r - 64:r0r - 32, :]
                        krap = kr2[r0r - 64:r0r - 32, :]
                    heads.append(([(kcap, qcap, 64), (krap, qrap, 32)], h_id, hl * 64))
                obf = c2.sb("l1b_obf", [64, 512], BF16)

                def out_fn(h_id, ic, o_f):
                    p.copy(obf, o_f[0:64, :])
                    p.dma(odram[h_id * 64:(h_id + 1) * 64, ic * 512:(ic + 1) * 512], obf)
                attn_group(c2, heads, 64, vtok, 256, jts_fn, mask_fn,
                           1.0 / np.sqrt(96.0), 80.0, out_fn, **sc)
                p.barrier()
        wo_from_dram(c, odram, woT, None, f"l1b{uid}")
        p.barrier()




def st_l0_attn_b(ctx, ins, uid):
    p = ctx.p
    wqkv = hin(ctx, "l0b_wqkv", ins["L0_attn_b_Wqkv"])
    woT = hin(ctx, "l0b_wo", ins["L0_attn_b_Wout"])
    cosE, sinE, R = rope_tables_l0b()
    cosT = hin(ctx, "l0b_cos", _tile4(cosE, 2))
    sinT = hin(ctx, "l0b_sin", _tile4(sinE, 2))
    rotT = hin(ctx, "l0b_rot", _bf(_blockdiag(R, 2).T))
    for q in range(-4, 4):
        hin(ctx, f"c_wmask{q}", window_mask_host(q))
    odram = ctx.nc.dram_tensor(f"l0b_od{uid}", [D, T], BF16, kind="Internal").ap()
    with ExitStack() as st:
        c = ctx.scoped(st)
        layernorm(c)
        cosS = c.sb("l0b_cosS", [P, T], F32); p.dma(cosS, ctx.hbm["l0b_cos"])
        sinS = c.sb("l0b_sinS", [P, T], F32); p.dma(sinS, ctx.hbm["l0b_sin"])
        rotS = c.sb("l0b_rotS", [P, P], BF16); p.dma(rotS, ctx.hbm["l0b_rot"])
        masks = {}
        for q in range(-4, 4):
            masks[q] = c.sb(f"l0b_m{q}", [P, 512], F32)
            p.dma(masks[q], ctx.hbm[f"c_wmask{q}"])

        def jts_fn(ic):
            return list(range(max(0, ic * 4 - 4), ic * 4 + 4))

        def mask_fn(ic, jt):
            return masks[jt - ic * 4]
        for g4 in range(4):
            with ExitStack() as st2:
                c2 = c.scoped(st2)
                col0 = g4 * 256
                wq = wload(c2, st2, wqkv[:, col0:col0 + 256], D, 256, f"l0bq{g4}{uid}")
                wk = wload(c2, st2, wqkv[:, 1024 + col0:1024 + col0 + 256], D, 256, f"l0bk{g4}{uid}")
                wv = wload(c2, st2, wqkv[:, 2048 + col0:2048 + col0 + 256], D, 256, f"l0bv{g4}{uid}")
                qf = c2.sb("l0b_qf", [P, T], F32)
                sbf = c2.sb("l0b_sbf", [P, 512], BF16)
                sf1 = c2.sb("l0b_sf1", [P, 512], F32)
                sf2 = c2.sb("l0b_sf2", [P, 512], F32)
                qro = c2.sb("l0b_qro", [P, 2 * T], BF16)
                kro = c2.sb("l0b_kro", [P, 2 * T], BF16)
                vtok = c2.sb("l0b_vtok", [P, NT * 256], BF16)
                for gg in range(2):
                    for (w, dst) in ((wq, qro), (wk, kro)):
                        proj_fm_group(c2, w, None, 0, qf, wst=256, wcol0=gg * P)
                        rope_apply2(c2, qf, P, rotS, cosS, sinS,
                                    dst[:, gg * T:(gg + 1) * T], c2.pp[0], sbf, sf1, sf2)
                for tt in range(NT):
                    ps = c2.pp[0] if tt % 2 == 0 else c2.pp[1]
                    for kt in range(ND):
                        p.mm(ps[:, 0:256], ctx.xn_bf[:, kt * T + tt * P: kt * T + (tt + 1) * P],
                             wv[:, kt * 256:(kt + 1) * 256],
                             start=(kt == 0), stop=(kt == ND - 1))
                    p.copy(vtok[:, tt * 256:(tt + 1) * 256], ps[:, 0:256], engine="scalar")
                sc = make_attn_scratch(c2, f"l0b{g4}")
                heads = []
                for hl in range(4):
                    tq = (hl * 64) // P
                    r0 = (hl * 64) % P
                    kap = kro[r0:r0 + 64, tq * T:(tq + 1) * T]
                    qap = qro[r0:r0 + 64, tq * T:(tq + 1) * T]
                    heads.append(([(kap, qap, 64)], g4 * 4 + hl, hl * 64))
                obf = c2.sb("l0b_obf", [64, 512], BF16)

                def out_fn(h_id, ic, o_f):
                    p.copy(obf, o_f[0:64, :])
                    p.dma(odram[h_id * 64:(h_id + 1) * 64, ic * 512:(ic + 1) * 512], obf)
                attn_group(c2, heads, 64, vtok, 256, jts_fn, mask_fn,
                           0.125, None, out_fn, **sc)
                p.barrier()
        wo_from_dram(c, odram, woT, None, f"l0b{uid}")
        p.barrier()


def st_l3_attn_a(ctx, ins, uid):
    p = ctx.p
    Wqk = ins["L3_attn_a_W_qk"].reshape(16, 64, 2, D)
    bqk = ins["L3_attn_a_b_qk"].reshape(16, 64, 2)
    wqT = hin(ctx, "l3a_wq", np.ascontiguousarray(Wqk[:, :, 0, :].reshape(D, D).T))
    wkT = hin(ctx, "l3a_wk", np.ascontiguousarray(Wqk[:, :, 1, :].reshape(D, D).T))
    with ExitStack() as st:
        c = ctx.scoped(st)
        layernorm(c)
        bq = load_pp(c, "l3a_bq", np.ascontiguousarray(bqk[:, :, 0].reshape(D)))
        bk = load_pp(c, "l3a_bk", np.ascontiguousarray(bqk[:, :, 1].reshape(D)))
        s_all = c.sb("l3a_sall", [P, NT * 512], F32)
        mrow = c.sb("l3a_mrow", [1, 512], F32)
        mbc = c.sb("l3a_mbc", [P, 512], F32)
        pb = c.sb("l3a_pb", [P, 512], BF16)
        z1 = c.sb("l3a_z1", [P, 512], F32)
        z2 = c.sb("l3a_z2", [P, 512], F32)
        of = c.sb("l3a_of", [P, 512], F32)
        CP = 1.0507009873554805 * (2048.0 ** -0.5)
        CN = 1.0507009873554805 * 1.6732632423543772 * (2048.0 ** -0.5)
        for hh in range(2):
            with ExitStack() as st2:
                c2 = c.scoped(st2)
                qb = c2.sb("l3a_qb", [P, 4 * T], BF16)
                kb = c2.sb("l3a_kb", [P, 4 * T], BF16)
                vtok = c2.sb("l3a_vt", [P, NT * 512], BF16)
                for (wT, bpp, dst) in ((wqT, bq, qb), (wkT, bk, kb)):
                    with ExitStack() as stw:
                        cw = c2.scoped(stw)
                        w = wload(cw, stw, wT[:, hh * 512:(hh + 1) * 512], D, 512,
                                  f"l3aw{hh}{uid}")
                        for mt in range(4):
                            gmt = hh * 4 + mt
                            for cc in range(TC):
                                ps = c2.pp[0] if cc % 2 == 0 else c2.pp[1]
                                for kt in range(ND):
                                    p.mm(ps, w[:, kt * 512 + mt * P: kt * 512 + (mt + 1) * P],
                                         ctx.xn_bf[:, kt * T + cc * 512: kt * T + (cc + 1) * 512],
                                         start=(kt == 0), stop=(kt == ND - 1))
                                p.act(dst[:, mt * T + cc * 512: mt * T + (cc + 1) * 512],
                                      ps, AF.Identity, bias=bpp[:, gmt:gmt + 1])
                        p.barrier()
                for tt in range(NT):
                    for dt in range(4):
                        gdt = hh * 4 + dt
                        psb = ctx.ppb[:, 0:P]
                        p.transpose(psb, ctx.xn_bf[:, gdt * T + tt * P: gdt * T + (tt + 1) * P],
                                    ctx.ident_b)
                        p.copy(vtok[:, tt * 512 + dt * P: tt * 512 + (dt + 1) * P], psb,
                               engine="scalar")
                for hl in range(8):
                    h_id = hh * 8 + hl
                    r0 = (hl * 64) % P
                    tq = (hl * 64) // P
                    kap = kb[r0:r0 + 64, tq * T:(tq + 1) * T]
                    qap = qb[r0:r0 + 64, tq * T:(tq + 1) * T]
                    for ic in range(TC):
                        psS = c.pp[2]
                        psD = c.pp[4][0:1, :]
                        psO = c.pp[3][r0:r0 + 64, :]
                        for jt in range(NT):
                            p.mm(psS, kap[0:64, jt * P:(jt + 1) * P],
                                 qap[0:64, ic * 512:(ic + 1) * 512], start=True, stop=True)
                            p.copy(s_all[:, jt * 512:(jt + 1) * 512], psS, engine="scalar")
                        for jt in range(NT):
                            p.mm(psD, ctx.ones_col, s_all[:, jt * 512:(jt + 1) * 512],
                                 start=(jt == 0), stop=(jt == NT - 1))
                        p.ts(mrow, psD, 0.125 / T, ALU.mult)
                        p.mm(c.pp[5], ctx.ones_row, mrow, start=True, stop=True)
                        p.copy(mbc, c.pp[5], engine="scalar")
                        for jt in range(NT):
                            sa = s_all[:, jt * 512:(jt + 1) * 512]
                            p.stt(z1, sa, 0.125, mbc, ALU.mult, ALU.subtract)
                            p.act(z2, z1, AF.Exp)
                            p.ts(z2, z2, -1.0, ALU.add, CN, ALU.mult, engine="gpsimd")
                            p.ts(z2, z2, 0.0, ALU.min, engine="gpsimd")
                            p.act(z1, z1, AF.Relu, scale=CP)
                            p.tt(pb, z1, z2, ALU.add)
                            p.mm(psO,
                                 vtok[:, jt * 512 + hl * 64: jt * 512 + hl * 64 + 64],
                                 pb, start=(jt == 0), stop=(jt == NT - 1))
                        p.copy(of[r0:r0 + 64, :], psO, engine="scalar")
                        xr = ctx.xres[(h_id * 64) % P:(h_id * 64) % P + 64,
                                      ((h_id * 64) // P) * T + ic * 512:
                                      ((h_id * 64) // P) * T + (ic + 1) * 512]
                        p.tt(xr, xr, of[r0:r0 + 64, :], ALU.add)
                p.barrier()
        p.barrier()


def st_l0_attn_a(ctx, ins, uid):
    p = ctx.p
    I = lambda n: ins[f"L0_attn_a_{n}"]
    wq = hin(ctx, "l0a_wq", I("Wq"))
    wk = hin(ctx, "l0a_wk", I("Wk"))
    wv = hin(ctx, "l0a_wv", I("Wv"))
    wo = hin(ctx, "l0a_wo", I("Wo"))
    with ExitStack() as st:
        c = ctx.scoped(st)
        layernorm(c)
        zb = c.sb("l0a_zb", [P, ND * T], BF16)
        st_z = ExitStack()
        c = c.scoped(st_z)
        vecs = {}
        for nm in ("q", "k", "v"):
            g = load_pp(c, f"l0a_g{nm}", I(f"g{nm}"))
            b = load_pp(c, f"l0a_b{nm}", I(f"b{nm}"))
            m = load_pp(c, f"l0a_m{nm}", I(f"m{nm}"))
            v = load_pp(c, f"l0a_v{nm}", I(f"v{nm}"))
            s4 = c.sb(f"l0a_s4{nm}", [P, ND], F32)
            b4 = c.sb(f"l0a_b4{nm}", [P, ND], F32)
            p.act(s4, v, AF.Ln, bias=1e-5)
            p.act(s4, s4, AF.Exp, scale=-0.5)
            p.tt(s4, s4, g, ALU.mult)
            p.tt(b4, m, s4, ALU.mult)
            p.tt(b4, b, b4, ALU.subtract)
            p.ts(b4, b4, 4.0, ALU.mult, -4.0, ALU.add)
            p.ts(s4, s4, 4.0, ALU.mult)
            vecs[nm] = (s4, b4)
        tQr = c.sb("l0a_tq", [1, T], F32)
        tKr = c.sb("l0a_tk", [1, T], F32)
        vtok = c.sb("l0a_vt", [P, NT * 1024], BF16)
        qtile = c.sb("l0a_qt", [P, 512], BF16)
        for nm, trow in (("q", tQr), ("k", tKr), ("v", None)):
            with ExitStack() as stw:
                cw = c.scoped(stw)
                w = wload(cw, stw, {"q": wq, "k": wk, "v": wv}[nm], D, D, f"l0a{nm}{uid}")
                s4, b4 = vecs[nm]
                psr = [c.pp[2][0:1, :], c.pp[3][0:1, :], c.pp[4][0:1, :], c.pp[5][0:1, :]]
                for mt in range(ND):
                    for cc in range(TC):
                        ps = c.pp[0] if (mt * TC + cc) % 2 == 0 else c.pp[1]
                        for kt in range(ND):
                            p.mm(ps, w[:, kt * D + mt * P: kt * D + (mt + 1) * P],
                                 ctx.xn_bf[:, kt * T + cc * 512: kt * T + (cc + 1) * 512],
                                 start=(kt == 0), stop=(kt == ND - 1))
                        p.act(qtile, ps, AF.Sigmoid, bias=b4[:, mt:mt + 1],
                              scale=s4[:, mt:mt + 1])
                        if trow is not None:
                            p.mm(psr[cc], ctx.ones_col_bf, qtile,
                                 start=(mt == 0), stop=(mt == ND - 1))
                        else:
                            psb = ctx.ppb[:, 0:P]
                            for q4 in range(4):
                                p.transpose(psb, qtile[:, q4 * P:(q4 + 1) * P], ctx.ident_b)
                                p.copy(vtok[:, (cc * 4 + q4) * 1024 + mt * P:
                                            (cc * 4 + q4) * 1024 + (mt + 1) * P],
                                       psb, engine="scalar")
                if trow is not None:
                    for cc in range(TC):
                        p.ts(trow[:, cc * 512:(cc + 1) * 512], psr[cc],
                             -1.0 / D, ALU.mult, 1.0, ALU.add)
                p.barrier()
        dsc = ctx.nc.dram_tensor(f"l0a_dsc{uid}", [T], F32, kind="Internal").ap()
        p.dma(dsc.rearrange("(a b) -> a b", a=1), tKr)
        tkst = c.sb("l0a_tkst", [NT, P], F32)
        p.dma(tkst, dsc.rearrange("(k p) -> k p", p=P))
        tkc = c.sb("l0a_tkc", [P, NT], F32)
        p.transpose(c.pp[5][:, 0:NT], tkst, ctx.ident_f[0:NT, 0:NT])
        p.copy(tkc, c.pp[5][:, 0:NT], engine="scalar")
        tqb = c.sb("l0a_tqb", [P, T], F32)
        bcast_row(c, tqb, tQr)
        d1 = c.sb("l0a_d1", [P, 512], F32)
        d2 = c.sb("l0a_d2", [P, 512], F32)
        d3 = c.sb("l0a_d3", [P, 512], F32)
        abf = c.sb("l0a_abf", [P, 512], BF16)
        for ic in range(TC):
            for mh in range(2):
                for jt in range(NT):
                    p.ts(d1, tqb[:, ic * 512:(ic + 1) * 512], tkc[:, jt:jt + 1], ALU.subtract)
                    p.act(d2, d1, AF.Sign)
                    p.act(d3, d1, AF.Abs)
                    p.act(d3, d3, AF.Exp, scale=-2.0)
                    p.stt(d2, d2, -0.9, d3, ALU.mult, ALU.mult, engine="gpsimd")
                    p.ts(abf, d2, 0.9, ALU.add, engine="gpsimd")
                    for dt in range(4):
                        gdt = mh * 4 + dt
                        p.mm(c.pp[dt],
                             vtok[:, jt * 1024 + gdt * P: jt * 1024 + (gdt + 1) * P],
                             abf, start=(jt == 0), stop=(jt == NT - 1))
                for dt in range(4):
                    gdt = mh * 4 + dt
                    p.copy(zb[:, gdt * T + ic * 512: gdt * T + (ic + 1) * 512],
                           c.pp[dt], engine="scalar")
        p.barrier()
        st_z.close()
        c = ctx.scoped(st)
        bo = load_pp(c, "l0a_bo", I("bo"))
        with ExitStack() as stw:
            cw = c.scoped(stw)
            wos = wload(cw, stw, wo, D, D, f"l0ao{uid}")
            linear(cw, wos, D, D, zb, None, bias_pp=bo, accum_res=True,
                   psA=c.pp[0], psB=c.pp[1])
            p.barrier()
        p.barrier()




def _gates_topk(c, ctx, wgT_hbm, E, K_sel, uid, bias_row_np=None, clipmin=None):
    p = c.p
    wg = wload(c, c.stack, wgT_hbm, D, E, f"gw{uid}")
    gatesT = c.sb(f"gatesT{uid}", [E, T], F32)
    sc = c.sb(f"gsc{uid}", [P, E], F32)
    prob = c.sb(f"gpr{uid}", [P, E], F32)
    corr = c.sb(f"gco{uid}", [P, E], F32)
    m8 = c.sb(f"gm8{uid}", [P, 8], F32)
    r1 = c.sb(f"gr1{uid}", [P, 1], F32)
    bbc = None
    if bias_row_np is not None:
        br = hin(ctx, f"gbias{uid}", np.ascontiguousarray(bias_row_np[None, :].astype(np.float32)))
        brs = c.sb(f"gbr{uid}", [1, E], F32)
        p.dma(brs, br)
        bbc = c.sb(f"gbb{uid}", [P, E], F32)
        ps = c.pp[5][:, 0:E]
        p.mm(ps, ctx.ones_row, brs, start=True, stop=True)
        p.copy(bbc, ps, engine="scalar")
    for tt in range(NT):
        ps = c.pp[0][:, 0:E]
        for kt in range(ND):
            p.mm(ps, ctx.xn_bf[:, kt * T + tt * P: kt * T + (tt + 1) * P],
                 wg[:, kt * E:(kt + 1) * E], start=(kt == 0), stop=(kt == ND - 1))
        p.red(r1, ps, ALU.max, negate=True)
        p.act(sc, ps, AF.Exp, bias=r1)
        p.red(r1, sc, ALU.add)
        p.recip(r1, r1)
        p.ts(prob, sc, r1, ALU.mult)
        if bbc is not None:
            p.tt(corr, prob, bbc, ALU.add)
        else:
            p.copy(corr, prob)
        p.max8(m8, corr)
        p.ts(sc, corr, m8[:, K_sel - 1:K_sel], ALU.is_ge)
        p.tt(prob, prob, sc, ALU.mult)
        p.red(r1, prob, ALU.add)
        if clipmin is not None:
            p.ts(r1, r1, clipmin, ALU.max)
        p.recip(r1, r1)
        p.ts(prob, prob, r1, ALU.mult)
        ps2 = c.pp[1][0:E, 0:P]
        p.transpose(ps2, prob, ctx.ident_f)
        p.copy(gatesT[:, tt * P:(tt + 1) * P], ps2, engine="scalar")
    gdram = c.nc.dram_tensor(f"gdram{uid}", [E, T], F32, kind="Internal").ap()
    p.dma(gdram, gatesT)
    return gatesT, gdram


def st_l0_ffn(ctx, ins, uid):
    p = ctx.p
    E, HID, R = 8, 2048, 16
    wgT = hin(ctx, "l0f_wg", np.ascontiguousarray(ins["L0_ffn_Wg"].T))
    for e in range(E):
        hin(ctx, f"l0f_w1T{e}", np.ascontiguousarray(ins["L0_ffn_W1"][e].T))
        hin(ctx, f"l0f_w3T{e}", np.ascontiguousarray(ins["L0_ffn_W3"][e].T))
        hin(ctx, f"l0f_w2T{e}", np.ascontiguousarray(ins["L0_ffn_W2"][e].T))
        hin(ctx, f"l0f_a1T{e}", np.ascontiguousarray(ins["L0_ffn_A1"][e].T))
        hin(ctx, f"l0f_a3T{e}", np.ascontiguousarray(ins["L0_ffn_A3"][e].T))
        hin(ctx, f"l0f_a2T{e}", np.ascontiguousarray(ins["L0_ffn_A2"][e].T))
        hin(ctx, f"l0f_b1s{e}", np.ascontiguousarray(ins["L0_ffn_B1"][e].T))
        hin(ctx, f"l0f_b3s{e}", np.ascontiguousarray(ins["L0_ffn_B3"][e].T))
        hin(ctx, f"l0f_b2s{e}", np.ascontiguousarray(ins["L0_ffn_B2"][e].T))
        hin(ctx, f"l0f_bias1pp{e}", pp_host(ins["L0_ffn_b1"][e]))
        hin(ctx, f"l0f_bias3pp{e}", pp_host(ins["L0_ffn_b3"][e]))
        hin(ctx, f"l0f_bias2pp{e}", pp_host(ins["L0_ffn_b2"][e]))
    with ExitStack() as st:
        c = ctx.scoped(st)
        layernorm(c)
        gatesT, gdram = _gates_topk(c, ctx, wgT, E, 2, f"l0f{uid}")
        grow = c.sb("l0f_grow", [1, 512], F32)
        yacc = c.sb("l0f_yacc", [P, ND * 512], F32)
        gsb = c.sb("l0f_g", [P, NT * 512], BF16)
        h3q = c.sb("l0f_h3", [P, 4 * 512], BF16)
        terb = c.sb("l0f_ter", [16, 512], BF16)
        ter2 = c.sb("l0f_ter2", [16, 512], BF16)
        gbc = c.sb("l0f_gbc", [P, 512], F32)
        scr = c.sb("l0f_scr", [P, 512], F32)
        for tq in range(TC):
            toff = tq * 512
            for e in range(E):
                with ExitStack() as st2:
                    c2 = c.scoped(st2)
                    a1 = wload(c2, st2, ctx.hbm[f"l0f_a1T{e}"], D, 16, f"a1_{e}{uid}")
                    a3 = wload(c2, st2, ctx.hbm[f"l0f_a3T{e}"], D, 16, f"a3_{e}{uid}")
                    a2 = wload(c2, st2, ctx.hbm[f"l0f_a2T{e}"], HID, 16, f"a2_{e}{uid}")
                    b1s = c2.sb("b1s", [16, HID], BF16)
                    b3s = c2.sb("b3s", [16, HID], BF16)
                    b2s = c2.sb("b2s", [16, D], BF16)
                    for (bb, hb, w_) in ((b1s, f"l0f_b1s{e}", HID), (b3s, f"l0f_b3s{e}", HID),
                                         (b2s, f"l0f_b2s{e}", D)):
                        p.dma(ctx.wstg[0:16, 0:w_], ctx.hbm[hb])
                        p.copy(bb, ctx.wstg[0:16, 0:w_], engine="gpsimd")
                    bias1 = c2.sb("bi1", [P, 16], F32); p.dma(bias1, ctx.hbm[f"l0f_bias1pp{e}"])
                    bias3 = c2.sb("bi3", [P, 16], F32); p.dma(bias3, ctx.hbm[f"l0f_bias3pp{e}"])
                    bias2 = c2.sb("bi2", [P, ND], F32); p.dma(bias2, ctx.hbm[f"l0f_bias2pp{e}"])
                    for (aa, tb) in ((a1, terb), (a3, ter2)):
                        ps = c2.pp[0][0:16, :]
                        for kt in range(ND):
                            p.mm(ps, aa[:, kt * 16: kt * 16 + 16],
                                 ctx.xn_bf[:, kt * T + toff: kt * T + toff + 512],
                                 start=(kt == 0), stop=(kt == ND - 1))
                        p.copy(tb, ps, engine="scalar")
                    p.dma(grow, gdram[e:e + 1, toff:toff + 512])
                    p.mm(c2.pp[5], ctx.ones_row, grow, start=True, stop=True)
                    p.copy(gbc, c2.pp[5], engine="scalar")
                    for hc in range(4):
                      with ExitStack() as sth:
                        ch = c2.scoped(sth)
                        w1 = wload(ch, sth, ctx.hbm[f"l0f_w1T{e}"][:, hc * 512:(hc + 1) * 512],
                                   D, 512, f"w1_{e}_{hc}_{tq}{uid}")
                        for mt in range(4):
                            ps = c2.pp[0] if mt % 2 == 0 else c2.pp[1]
                            for kt in range(ND):
                                p.mm(ps, w1[:, kt * 512 + mt * P: kt * 512 + (mt + 1) * P],
                                     ctx.xn_bf[:, kt * T + toff: kt * T + toff + 512],
                                     start=(kt == 0), stop=False)
                            p.mm(ps, b1s[:, hc * 512 + mt * P: hc * 512 + (mt + 1) * P],
                                 terb, start=False, stop=True)
                            p.act(gsb[:, (hc * 4 + mt) * 512:(hc * 4 + mt + 1) * 512], ps,
                                  AF.Silu, bias=bias1[:, hc * 4 + mt: hc * 4 + mt + 1])
                        w3 = wload(ch, sth, ctx.hbm[f"l0f_w3T{e}"][:, hc * 512:(hc + 1) * 512],
                                   D, 512, f"w3_{e}_{hc}_{tq}{uid}")
                        for mt in range(4):
                            ps = c2.pp[0] if mt % 2 == 0 else c2.pp[1]
                            for kt in range(ND):
                                p.mm(ps, w3[:, kt * 512 + mt * P: kt * 512 + (mt + 1) * P],
                                     ctx.xn_bf[:, kt * T + toff: kt * T + toff + 512],
                                     start=(kt == 0), stop=False)
                            p.mm(ps, b3s[:, hc * 512 + mt * P: hc * 512 + (mt + 1) * P],
                                 ter2, start=False, stop=True)
                            p.act(h3q[:, mt * 512:(mt + 1) * 512], ps, AF.Identity,
                                  bias=bias3[:, hc * 4 + mt: hc * 4 + mt + 1])
                        for mt in range(4):
                            gslice = gsb[:, (hc * 4 + mt) * 512:(hc * 4 + mt + 1) * 512]
                            p.tt(gslice, gslice, h3q[:, mt * 512:(mt + 1) * 512], ALU.mult,
                                 engine="vector" if mt % 2 else "gpsimd")
                        p.barrier()
                    ps = c2.pp[0][0:16, :]
                    for kt in range(NT):
                        p.mm(ps, a2[:, kt * 16: kt * 16 + 16],
                             gsb[:, kt * 512:(kt + 1) * 512],
                             start=(kt == 0), stop=(kt == NT - 1))
                    p.copy(ter2, ps, engine="scalar")
                    w2 = wload(c2, st2, ctx.hbm[f"l0f_w2T{e}"], HID, D, f"w2_{e}_{tq}{uid}")
                    for mt in range(ND):
                        ps = c2.pp[0] if mt % 2 == 0 else c2.pp[1]
                        for kt in range(NT):
                            p.mm(ps, w2[:, kt * D + mt * P: kt * D + (mt + 1) * P],
                                 gsb[:, kt * 512:(kt + 1) * 512],
                                 start=(kt == 0), stop=False)
                        p.mm(ps, b2s[:, mt * P:(mt + 1) * P], ter2, start=False, stop=True)
                        p.act(scr, ps, AF.Identity, bias=bias2[:, mt:mt + 1])
                        ya = yacc[:, mt * 512:(mt + 1) * 512]
                        if e == 0:
                            p.tt(ya, scr, gbc, ALU.mult,
                                 engine="vector" if mt % 2 else "gpsimd")
                        else:
                            p.tt(scr, scr, gbc, ALU.mult,
                                 engine="vector" if mt % 2 else "gpsimd")
                            p.tt(ya, ya, scr, ALU.add,
                                 engine="gpsimd" if mt % 2 else "vector")
                    p.barrier()
            for mt in range(ND):
                xr = ctx.xres[:, mt * T + toff: mt * T + toff + 512]
                p.tt(xr, xr, yacc[:, mt * 512:(mt + 1) * 512], ALU.add,
                     engine="vector" if mt % 2 else "gpsimd")
        p.barrier()


def st_l1_ffn(ctx, ins, uid):
    p = ctx.p
    E, I_ = 32, 512
    rwT = hin(ctx, "l1f_rw", np.ascontiguousarray(ins["L1_ffn_router_weight"].T))
    sgT = hin(ctx, "l1f_sg", np.ascontiguousarray(ins["L1_ffn_shared_gate_w"].T))
    suT = hin(ctx, "l1f_su", np.ascontiguousarray(ins["L1_ffn_shared_up_w"].T))
    sdT = hin(ctx, "l1f_sd", np.ascontiguousarray(ins["L1_ffn_shared_down_w"].T))
    for e in range(E):
        hin(ctx, f"l1f_gupT{e}", np.ascontiguousarray(ins["L1_ffn_gate_up_proj"][e].T))
        hin(ctx, f"l1f_dT{e}", np.ascontiguousarray(ins["L1_ffn_down_proj"][e].T))
    with ExitStack() as st:
        c = ctx.scoped(st)
        layernorm(c)
        gatesT, gdram = _gates_topk(c, ctx, rwT, E, 6, f"l1f{uid}",
                             bias_row_np=ins["L1_ffn_e_score_bias"], clipmin=1e-12)
        grow = c.sb("l1f_grow", [1, 512], F32)
        with ExitStack() as stsh:
            csh = c.scoped(stsh)
            gsh = csh.sb("l1f_gsh", [P, ND * T], BF16)
            with ExitStack() as st2:
                c2 = csh.scoped(st2)
                sg = wload(c2, st2, sgT, D, D, f"sg{uid}")
                linear(c2, sg, D, D, ctx.xn_bf, gsh, func=AF.Silu, psA=c2.pp[0], psB=c2.pp[1])
                p.barrier()
            with ExitStack() as st2:
                c2 = csh.scoped(st2)
                su = wload(c2, st2, suT, D, D, f"su{uid}")
                ush = c2.sb("l1f_ush", [P, T], F32)
                for mt in range(ND):
                    for cc in range(TC):
                        ps = c2.pp[0] if cc % 2 == 0 else c2.pp[1]
                        for kt in range(ND):
                            p.mm(ps, su[:, kt * D + mt * P: kt * D + (mt + 1) * P],
                                 ctx.xn_bf[:, kt * T + cc * 512: kt * T + (cc + 1) * 512],
                                 start=(kt == 0), stop=(kt == ND - 1))
                        p.copy(ush[:, cc * 512:(cc + 1) * 512], ps, engine="scalar")
                    gs = gsh[:, mt * T:(mt + 1) * T]
                    p.tt(gs, gs, ush, ALU.mult)
                p.barrier()
            with ExitStack() as st2:
                c2 = csh.scoped(st2)
                sd = wload(c2, st2, sdT, D, D, f"sd{uid}")
                linear(c2, sd, D, D, gsh, None, accum_res=True, psA=c2.pp[0], psB=c2.pp[1])
                p.barrier()
            p.barrier()
        yacc = c.sb("l1f_yacc", [P, ND * 1024], F32)
        hmb = c.sb("l1f_hm", [P, 4 * 1024], BF16)
        usb = c.sb("l1f_us", [P, 4 * 1024], BF16)
        gbc = c.sb("l1f_gbc", [P, 1024], F32)
        scr = c.sb("l1f_scr", [P, 512], F32)
        for th in range(2):
            toff = th * 1024
            for e in range(E):
                with ExitStack() as st2:
                    c2 = c.scoped(st2)
                    gup = wload(c2, st2, ctx.hbm[f"l1f_gupT{e}"], D, 1024, f"gup{e}_{th}{uid}")
                    dwn = wload(c2, st2, ctx.hbm[f"l1f_dT{e}"], I_, 1024, f"dwn{e}_{th}{uid}")
                    for cc in range(2):
                        p.dma(grow, gdram[e:e + 1, toff + cc * 512: toff + (cc + 1) * 512])
                        p.mm(c2.pp[5], ctx.ones_row, grow, start=True, stop=True)
                        p.copy(gbc[:, cc * 512:(cc + 1) * 512], c2.pp[5], engine="scalar")
                    for mt in range(8):
                        for cc in range(2):
                            ps = c2.pp[0] if (mt * 2 + cc) % 2 == 0 else c2.pp[1]
                            for kt in range(ND):
                                p.mm(ps, gup[:, kt * 1024 + mt * P: kt * 1024 + (mt + 1) * P],
                                     ctx.xn_bf[:, kt * T + toff + cc * 512: kt * T + toff + (cc + 1) * 512],
                                     start=(kt == 0), stop=(kt == ND - 1))
                            if mt < 4:
                                p.act(hmb[:, mt * 1024 + cc * 512: mt * 1024 + (cc + 1) * 512],
                                      ps, AF.Silu)
                            else:
                                p.copy(usb[:, (mt - 4) * 1024 + cc * 512: (mt - 4) * 1024 + (cc + 1) * 512],
                                       ps, engine="scalar")
                    for mt in range(4):
                        hs = hmb[:, mt * 1024:(mt + 1) * 1024]
                        p.tt(hs, hs, usb[:, mt * 1024:(mt + 1) * 1024], ALU.mult,
                             engine="vector" if mt % 2 else "gpsimd")
                    for mt in range(ND):
                        for cc in range(2):
                            ps = c2.pp[0] if (mt * 2 + cc) % 2 == 0 else c2.pp[1]
                            for kt in range(4):
                                p.mm(ps, dwn[:, kt * 1024 + mt * P: kt * 1024 + (mt + 1) * P],
                                     hmb[:, kt * 1024 + cc * 512: kt * 1024 + (cc + 1) * 512],
                                     start=(kt == 0), stop=(kt == 3))
                            p.copy(scr, ps, engine="scalar")
                            ya = yacc[:, mt * 1024 + cc * 512: mt * 1024 + (cc + 1) * 512]
                            gb = gbc[:, cc * 512:(cc + 1) * 512]
                            if e == 0:
                                p.tt(ya, scr, gb, ALU.mult,
                                     engine="vector" if (mt + cc) % 2 else "gpsimd")
                            else:
                                p.tt(scr, scr, gb, ALU.mult,
                                     engine="vector" if (mt + cc) % 2 else "gpsimd")
                                p.tt(ya, ya, scr, ALU.add,
                                     engine="gpsimd" if (mt + cc) % 2 else "vector")
                    p.barrier()
            for mt in range(ND):
                xr = ctx.xres[:, mt * T + toff: mt * T + toff + 1024]
                p.tt(xr, xr, yacc[:, mt * 1024:(mt + 1) * 1024], ALU.add,
                     engine="vector" if mt % 2 else "gpsimd")
        p.barrier()




def st_l1_attn_a(ctx, ins, uid):
    p = ctx.p
    I = lambda n: ins[f"L1_attn_a_{n}"]
    wB = hin(ctx, "l1a_wB", np.ascontiguousarray(I("W_B").T))
    wC = hin(ctx, "l1a_wC", np.ascontiguousarray(I("W_C").T))
    wdt1 = hin(ctx, "l1a_wdt1", np.ascontiguousarray(I("W_dt1").T))
    wdt2 = hin(ctx, "l1a_wdt2", np.ascontiguousarray(I("W_dt2").T))
    logA = hin(ctx, "l1a_logA", pp_host(np.ascontiguousarray(I("log_A").T)))
    with ExitStack() as st:
        c = ctx.scoped(st)
        xn, bm, br = layernorm(c, keep_stats=True)
        bdt2 = load_pp(c, "l1a_bdt2", I("b_dt2"))
        dsk = load_pp(c, "l1a_dsk", I("D_skip"))
        acol = c.sb("l1a_acol", [P, P], F32)
        p.dma(acol, logA)
        p.act(acol, acol, AF.Exp)
        p.ts(acol, acol, -1.0, ALU.mult)
        rcol = c.sb("l1a_rcol", [P, P], F32)
        p.ts(rcol, acol, 1e-8, ALU.add)
        p.recip(rcol, rcol)
        wB_s = wload(c, st, wB, D, 16, f"l1aB{uid}")
        wC_s = wload(c, st, wC, D, 16, f"l1aC{uid}")
        wdt1_s = wload(c, st, wdt1, D, 64, f"l1adt1{uid}")
        wdt2_s = wload(c, st, wdt2, 64, 1024, f"l1adt2{uid}")
        Brow = c.sb("l1a_Brow", [16, T], F32)
        Crow = c.sb("l1a_Crow", [16, T], F32)
        bcd = ctx.nc.dram_tensor(f"l1a_bcd{uid}", [32, T], F32, kind="Internal").ap()
        brow1 = c.sb("l1a_brow1", [1, 512], F32)
        crow1 = c.sb("l1a_crow1", [1, 512], F32)
        terb = c.sb("l1a_terb", [64, T], BF16)
        for (wsb, dst, rows) in ((wB_s, Brow, 16), (wC_s, Crow, 16), (wdt1_s, terb, 64)):
            for cc in range(TC):
                ps = c.pp[0][0:rows, :]
                for kt in range(ND):
                    p.mm(ps, wsb[:, kt * rows: kt * rows + rows],
                         ctx.xn_bf[:, kt * T + cc * 512: kt * T + (cc + 1) * 512],
                         start=(kt == 0), stop=(kt == ND - 1))
                p.copy(dst[:, cc * 512:(cc + 1) * 512], ps[0:rows, :], engine="scalar")
        p.dma(bcd[0:16, :], Brow)
        p.dma(bcd[16:32, :], Crow)
        xnf = c.sb("l1a_xnf", [P, T], F32)
        delta = c.sb("l1a_delta", [P, T], F32)
        s1 = c.sb("l1a_s1", [P, T], F32)
        s2 = c.sb("l1a_s2", [P, T], F32)
        av = s1
        bv = s2
        hv = ctx.wstg
        yac = c.sb("l1a_yac", [P, T], F32)
        for dt in range(ND):
            p.tt(xnf, ctx.xres_t(dt), bm, ALU.subtract)
            p.tt(xnf, xnf, br, ALU.mult)
            for cc in range(TC):
                ps = c.pp[0] if cc % 2 == 0 else c.pp[1]
                p.mm(ps, wdt2_s[0:64, dt * P:(dt + 1) * P],
                     terb[:, cc * 512:(cc + 1) * 512], start=True, stop=True)
                p.act(delta[:, cc * 512:(cc + 1) * 512], ps, AF.Identity,
                      bias=bdt2[:, dt:dt + 1])
            softplus(c, delta, delta, s1, s2)
            for n in range(16):
                col = n * ND + dt
                p.act(av, delta, AF.Exp, scale=acol[:, col:col + 1])
                p.ts(bv, av, 1.0, ALU.subtract, rcol[:, col:col + 1], ALU.mult)
                p.tt(bv, bv, xnf, ALU.mult, engine="gpsimd")
                for cc in range(TC):
                    cs = slice(cc * 512, (cc + 1) * 512)
                    p.dma(brow1, bcd[n:n + 1, cs])
                    p.mm(c.pp[5], ctx.ones_row, brow1, start=True, stop=True)
                    p.tt(bv[:, cs], bv[:, cs], c.pp[5], ALU.mult)
                p.scan(hv, av, bv, 0.0)
                for cc in range(TC):
                    cs = slice(cc * 512, (cc + 1) * 512)
                    p.dma(crow1, bcd[16 + n:17 + n, cs])
                    p.mm(c.pp[5], ctx.ones_row, crow1, start=True, stop=True)
                    p.tt(hv[:, cs], hv[:, cs], c.pp[5], ALU.mult)
                if n == 0:
                    p.copy(yac, hv, engine="gpsimd")
                else:
                    p.tt(yac, yac, hv, ALU.add)
            p.stt(yac, xnf, dsk[:, dt:dt + 1], yac, ALU.mult, ALU.add)
            xr = ctx.xres_t(dt)
            p.tt(xr, xr, yac, ALU.add, engine="gpsimd")
        p.barrier()


def st_l3_attn_b(ctx, ins, uid):
    p = ctx.p
    I = lambda n: ins[f"L3_attn_b_{n}"]
    wIn = hin(ctx, "l3b_wIn", I("W_in"))
    wX = hin(ctx, "l3b_wX", I("W_x"))
    wOut = hin(ctx, "l3b_wOut", I("W_out"))
    convw = hin(ctx, "l3b_convw", pp_host(np.ascontiguousarray(I("conv_w").T)))
    logA = hin(ctx, "l3b_logA", pp_host(np.ascontiguousarray(I("A_log").T)))
    DI, NDI = 2048, 16
    with ExitStack() as st:
        c = ctx.scoped(st)
        layernorm(c)
        convb = load_pp(c, "l3b_convb", I("conv_b"))
        dtw = load_pp(c, "l3b_dtw", I("dt_w"))
        dtb = load_pp(c, "l3b_dtb", I("dt_b"))
        dpp = load_pp(c, "l3b_D", I("D"))
        acol = c.sb("l3b_acol", [P, 256], F32)
        p.dma(acol, logA)
        p.act(acol, acol, AF.Exp)
        p.ts(acol, acol, -1.0, ALU.mult)
        rcol = c.sb("l3b_rcol", [P, 256], F32)
        p.ts(rcol, acol, 1e-8, ALU.add)
        p.recip(rcol, rcol)
        cwc = c.sb("l3b_cwc", [P, 64], F32)
        p.dma(cwc, convw)
        xcd = ctx.nc.dram_tensor(f"l3b_xcd{uid}", [DI, T], BF16, kind="Internal").ap()
        xbstg = c.sb("l3b_xbstg", [P, T], BF16)
        ssm = c.sb("l3b_ssm", [33, T], F32)

        def xconv_tile(cc_, gdt, out_f32):
            with ExitStack() as st2:
                c2 = cc_.scoped(st2)
                wq_ = wload(c2, st2, wIn[:, gdt * P:(gdt + 1) * P], D, P, f"l3bw{gdt}{uid}x")
                pad = c2.sb("l3b_pad", [P, T + 3], F32)
                p.memset(pad[:, 0:3], 0.0)
                for cc in range(TC):
                    ps = c2.pp[0] if cc % 2 == 0 else c2.pp[1]
                    for kt in range(ND):
                        p.mm(ps, wq_[:, kt * P:(kt + 1) * P],
                             ctx.xn_bf[:, kt * T + cc * 512: kt * T + (cc + 1) * 512],
                             start=(kt == 0), stop=(kt == ND - 1))
                    p.copy(pad[:, 3 + cc * 512: 3 + (cc + 1) * 512], ps, engine="scalar")
                p.ts(out_f32, pad[:, 0:T], cwc[:, gdt:gdt + 1], ALU.mult)
                for k in range(1, 4):
                    p.stt(out_f32, pad[:, k:k + T], cwc[:, k * NDI + gdt: k * NDI + gdt + 1],
                          out_f32, ALU.mult, ALU.add, engine="gpsimd" if k % 2 else "vector")
                p.act(out_f32, out_f32, AF.Silu, bias=convb[:, gdt:gdt + 1])
                p.barrier()
        xcf = c.sb("l3b_xcf", [P, T], F32)
        for gdt in range(NDI):
            xconv_tile(c, gdt, xcf)
            p.copy(xbstg, xcf, engine="gpsimd")
            p.dma(xcd[gdt * P:(gdt + 1) * P, :], xbstg)
        with ExitStack() as st2:
            c2 = c.scoped(st2)
            wxs = wload(c2, st2, wX, DI, 33, f"l3bx{uid}")
            xk = [c2.sb(f"l3b_xk{i}", [P, 512], BF16) for i in range(2)]
            for cc in range(TC):
                ps = c2.pp[0][0:33, :]
                for kt in range(NDI):
                    xs = xk[kt % 2]
                    p.dma(xs, xcd[kt * P:(kt + 1) * P, cc * 512:(cc + 1) * 512])
                    p.mm(ps, wxs[:, kt * 33: kt * 33 + 33], xs,
                         start=(kt == 0), stop=(kt == NDI - 1))
                p.copy(ssm[:, cc * 512:(cc + 1) * 512], ps[0:33, :], engine="scalar")
            p.barrier()
        ssmd = ctx.nc.dram_tensor(f"l3b_ssmd{uid}", [33, T], F32, kind="Internal").ap()
        p.dma(ssmd, ssm)
        stB = ExitStack()
        cB = c.scoped(stB)
        delta = cB.sb("l3b_delta", [P, T], F32)
        dxc = cB.sb("l3b_dxc", [P, T], F32)
        yac = cB.sb("l3b_yac", [P, T], F32)
        s1 = cB.sb("l3b_s1", [P, 1024], F32)
        s2 = cB.sb("l3b_s2", [P, 1024], F32)
        brow1 = cB.sb("l3b_br1", [1, 512], F32)
        crow1 = cB.sb("l3b_cr1", [1, 512], F32)
        hv = ctx.wstg[:, 512:1536]
        hsave = ctx.wstg[:, 1536:1537]
        HF = 1024
        for gdt in range(NDI):
            xconv_tile(c, gdt, xcf)
            for hf in range(2):
                dsl = delta[:, hf * HF:(hf + 1) * HF]
                for q2 in range(2):
                    cs = slice(hf * HF + q2 * 512, hf * HF + q2 * 512 + 512)
                    ds2 = slice(q2 * 512, q2 * 512 + 512)
                    p.dma(brow1, ssmd[0:1, cs])
                    p.mm(c.pp[5], ctx.ones_row, brow1, start=True, stop=True)
                    p.ts(dsl[:, ds2], c.pp[5], dtw[:, gdt:gdt + 1], ALU.mult,
                         dtb[:, gdt:gdt + 1], ALU.add)
                softplus(c, dsl, dsl, s1, s2)
            p.tt(dxc, delta, xcf, ALU.mult, engine="gpsimd")
            for n in range(NDI):
                col = n * NDI + gdt
                for hf in range(2):
                    hsl = slice(hf * HF, (hf + 1) * HF)
                    p.act(s1, delta[:, hsl], AF.Exp, scale=acol[:, col:col + 1])
                    p.copy(s2, dxc[:, hsl], engine="gpsimd")
                    for q2 in range(2):
                        cs = slice(hf * HF + q2 * 512, hf * HF + q2 * 512 + 512)
                        ds = slice(q2 * 512, q2 * 512 + 512)
                        p.dma(brow1, ssmd[1 + n:2 + n, cs])
                        p.mm(c.pp[5], ctx.ones_row, brow1, start=True, stop=True)
                        p.tt(s2[:, ds], s2[:, ds], c.pp[5], ALU.mult)
                    p.scan(hv, s1, s2, 0.0 if hf == 0 else hsave)
                    if hf == 0:
                        p.copy(hsave, hv[:, HF - 1:HF], engine="gpsimd")
                    for q2 in range(2):
                        cs = slice(hf * HF + q2 * 512, hf * HF + q2 * 512 + 512)
                        ds = slice(q2 * 512, q2 * 512 + 512)
                        p.dma(crow1, ssmd[17 + n:18 + n, cs])
                        p.mm(c.pp[4], ctx.ones_row, crow1, start=True, stop=True)
                        p.tt(hv[:, ds], hv[:, ds], c.pp[4], ALU.mult)
                    ysl = yac[:, hsl]
                    if n == 0:
                        p.copy(ysl, hv, engine="gpsimd")
                    else:
                        p.tt(ysl, ysl, hv, ALU.add)
            p.stt(yac, xcf, dpp[:, gdt:gdt + 1], yac, ALU.mult, ALU.add)
            with ExitStack() as st2:
                c2 = c.scoped(st2)
                wz = wload(c2, st2, wIn[:, DI + gdt * P: DI + (gdt + 1) * P], D, P,
                           f"l3bz{gdt}{uid}")
                for cc in range(TC):
                    ps = c2.pp[0] if cc % 2 == 0 else c2.pp[1]
                    for kt in range(ND):
                        p.mm(ps, wz[:, kt * P:(kt + 1) * P],
                             ctx.xn_bf[:, kt * T + cc * 512: kt * T + (cc + 1) * 512],
                             start=(kt == 0), stop=(kt == ND - 1))
                    p.act(xcf[:, cc * 512:(cc + 1) * 512], ps, AF.Silu)
                p.tt(xbstg, yac, xcf, ALU.mult)
                p.dma(xcd[gdt * P:(gdt + 1) * P, :], xbstg)
                p.barrier()
        p.barrier()
        stB.close()
        wo_from_dram(c, xcd, wOut, None, f"l3bo{uid}", d_in=DI)
        p.barrier()



def build_and_run(stage_fns, inputs, x_list, tr_in=False, tr_out=False):
    global _HOST
    _HOST = {}
    nc = bass.Bass("TRN2", target_bir_lowering=False, debug=False)
    with ExitStack() as stack:
        p = Prog(nc, stack)
        ctx = Ctx(p, stack)
        ctx.setup_consts()
        if tr_in:
            x0 = nc.dram_tensor("x0", [T, D], F32, kind="ExternalInput").ap()
            input_transpose(ctx, x0)
        else:
            xin = nc.dram_tensor("xin", [D, T], F32, kind="ExternalInput").ap()
            fm_dma_in(ctx, xin)
        p.barrier()
        for fn, uid in stage_fns:
            fn(ctx, inputs, uid)
        if tr_out:
            xo = nc.dram_tensor("xout", [T, D], F32, kind="ExternalOutput").ap()
            output_transpose(ctx, xo)
        else:
            xo = nc.dram_tensor("xout", [D, T], F32, kind="ExternalOutput").ap()
            fm_dma_out(ctx, xo)
        p.barrier()
        host = dict(_HOST)
        key = "x0" if tr_in else "xin"
        in_maps = [dict(host, **{key: np.ascontiguousarray(x)}) for x in x_list]
        res = run_bass_kernel_spmd(nc, in_maps, core_ids=list(range(len(x_list))))
        return [r["xout"] for r in res.results]


def kernel(**inputs):
    x = np.asarray(inputs["L0_attn_a_x"], dtype=np.float32)
    B = x.shape[0]
    groups = launch_groups()
    outs = None
    for gi, (stages, tr_in, tr_out) in enumerate(groups):
        if tr_in:
            xs = [x[b] for b in range(B)]
        else:
            xs = outs
        outs = build_and_run(stages, inputs, xs, tr_in=tr_in, tr_out=tr_out)
    return np.stack(outs, axis=0)


def launch_groups():
    return [(STAGES, True, True)]


STAGES = [
    (st_l0_attn_a, "a"), (st_l0_ffn, "a"), (st_l0_attn_b, "a"), (st_l0_ffn, "b"),
    (st_l1_attn_a, "a"), (st_l1_ffn, "a"), (st_l1_attn_b, "a"), (st_l1_ffn, "b"),
    (st_l2_attn_a, "a"), (st_l2_ffn, "a"), (st_l2_attn_b, "a"), (st_l2_ffn, "b"),
    (st_l3_attn_a, "a"), (st_l3_ffn, "a"), (st_l3_attn_b, "a"), (st_l3_ffn, "b"),
]


# version 06 (final)
# speedup vs baseline: 1.1539x; 1.1410x over previous
import numpy as np
from contextlib import ExitStack

import concourse.bass as bass
import concourse.mybir as mybir
from concourse.bass_utils import run_bass_kernel_spmd

F32 = mybir.dt.float32
BF16 = mybir.dt.bfloat16
AF = mybir.ActivationFunctionType
ALU = mybir.AluOpType
AX = mybir.AxisListType

ENGINES = ("tensor", "vector", "scalar", "gpsimd", "sync")

P = 128
T = 2048
D = 1024
NT = T // P
ND = D // P
TC = T // 512


def _ap_key(ap):
    t = getattr(ap, "tensor", None)
    if t is None:
        t = ap
    n = getattr(t, "name", None)
    if n is None:
        n = str(t)
    return n


class Prog:

    DMA_SLOTS = {"sync": 8, "scalar": 2, "gpsimd": 2}

    def __init__(self, nc, stack):
        self.nc = nc
        self.stack = stack
        self.sem_cmp = {e: stack.enter_context(nc.semaphore(f"sc_{e}")) for e in ENGINES}
        self.dma_sems = {e: [stack.enter_context(nc.semaphore(f"sd_{e}{i}"))
                             for i in range(n)]
                         for e, n in self.DMA_SLOTS.items()}
        self.cnt = {(e, "cmp"): 0 for e in ENGINES}
        for e, n in self.DMA_SLOTS.items():
            for i in range(n):
                self.cnt[(e, ("dma", i))] = 0
        self.rr = {e: 0 for e in self.DMA_SLOTS}
        self.writers = {}
        self.readers = {}
        self.waited = {e: {} for e in ENGINES}

    def _sem_of(self, se, kind):
        if kind == "cmp":
            return self.sem_cmp[se]
        return self.dma_sems[se][kind[1]]

    def _wait(self, e, src):
        se, kind, cnt = src
        if se == e == "tensor" and kind == "cmp":
            return
        w = self.waited[e]
        if w.get((se, kind), 0) >= cnt:
            return
        getattr(self.nc, e).wait_ge(self._sem_of(se, kind), cnt)
        w[(se, kind)] = cnt

    def _deps(self, e, reads, writes):
        deps = {}
        for ap in reads:
            k = _ap_key(ap)
            for (ek, cnt) in self.writers.get(k, {}).items():
                deps[ek] = max(deps.get(ek, 0), cnt)
        for ap in writes:
            k = _ap_key(ap)
            for dct in (self.writers.get(k, {}), self.readers.get(k, {})):
                for (ek, cnt) in dct.items():
                    deps[ek] = max(deps.get(ek, 0), cnt)
        for (se, kind), cnt in sorted(deps.items(), key=str):
            self._wait(e, (se, kind, cnt))

    def _record(self, e, kind, cnt, reads, writes):
        for ap in reads:
            self.readers.setdefault(_ap_key(ap), {})[(e, kind)] = cnt
        for ap in writes:
            k = _ap_key(ap)
            self.writers[k] = {(e, kind): cnt}
            self.readers[k] = {}

    def emit(self, e, fn, reads=(), writes=(), dma=False):
        if dma:
            slot = self.rr[e] % len(self.dma_sems[e])
            self.rr[e] += 1
            kind = ("dma", slot)
            prev = self.cnt[(e, kind)]
            self._deps(e, reads, writes)
            if prev > 0:
                self._wait(e, (e, kind, prev))
            inst = fn(getattr(self.nc, e))
            self.cnt[(e, kind)] = prev + 16
            inst.then_inc(self.dma_sems[e][slot], 16)
            self._record(e, kind, prev + 16, reads, writes)
            return inst
        self._deps(e, reads, writes)
        inst = fn(getattr(self.nc, e))
        self.cnt[(e, "cmp")] += 1
        cnt = self.cnt[(e, "cmp")]
        inst.then_inc(self.sem_cmp[e], 1)
        self._record(e, "cmp", cnt, reads, writes)
        return inst

    def barrier(self):
        for e in ENGINES:
            for (se, kind), c in self.cnt.items():
                if c:
                    self._wait(e, (se, kind, c))

    def dma(self, out, in_, e="sync", **kw):
        self.emit(e, lambda g: g.dma_start(out=out, in_=in_, **kw),
                  reads=[in_], writes=[out], dma=True)

    def mm(self, out, lhsT, rhs, start, stop):
        self.emit("tensor", lambda g: g.matmul(out, lhsT, rhs, start=start, stop=stop),
                  reads=[lhsT, rhs], writes=[out])

    def transpose(self, out, in_, ident):
        self.emit("tensor", lambda g: g.transpose(out, in_, ident),
                  reads=[in_, ident], writes=[out])

    def act(self, out, in_, func, bias=None, scale=None, engine="scalar"):
        kw = {}
        reads = [in_]
        if bias is not None:
            kw["bias"] = bias
            if not isinstance(bias, (int, float)):
                reads.append(bias)
            elif float(bias) not in (0.0, 1.0):
                reads.append(self.ctx.regconst(float(bias)))
        if scale is not None:
            kw["scale"] = scale
            if not isinstance(scale, (int, float)):
                reads.append(scale)
        self.emit(engine, lambda g: g.activation(out, in_, func, **kw),
                  reads=reads, writes=[out])

    def tt(self, out, a, b, op, engine="vector"):
        self.emit(engine, lambda g: g.tensor_tensor(out, a, b, op),
                  reads=[a, b], writes=[out])

    def ts(self, out, in_, s1, op0, s2=None, op1=None, engine="vector"):
        if engine == "gpsimd":
            engine = "vector"
        reads = [in_]
        if not isinstance(s1, (int, float)):
            reads.append(s1)
        if s2 is not None and not isinstance(s2, (int, float)):
            reads.append(s2)
        if s2 is None:
            self.emit(engine, lambda g: g.tensor_scalar(out, in_, s1, None, op0),
                      reads=reads, writes=[out])
        else:
            self.emit(engine, lambda g: g.tensor_scalar(out, in_, s1, s2, op0, op1),
                      reads=reads, writes=[out])

    def stt(self, out, in0, scalar, in1, op0, op1, engine="vector"):
        if engine == "gpsimd":
            engine = "vector"
        reads = [in0, in1]
        if not isinstance(scalar, (int, float)):
            reads.append(scalar)
        self.emit(engine, lambda g: g.scalar_tensor_tensor(out, in0, scalar, in1, op0, op1),
                  reads=reads, writes=[out])

    def red(self, out, in_, op, engine="vector", negate=None):
        kw = {}
        if negate is not None:
            kw["negate"] = negate
        self.emit(engine, lambda g: g.tensor_reduce(out, in_, AX.X, op, **kw),
                  reads=[in_], writes=[out])

    def copy(self, out, in_, engine="vector"):
        if engine == "scalar":
            self.emit(engine, lambda g: g.copy(out, in_), reads=[in_], writes=[out])
        else:
            self.emit(engine, lambda g: g.tensor_copy(out, in_), reads=[in_], writes=[out])

    def memset(self, ap, val, engine="vector"):
        self.emit(engine, lambda g: g.memset(ap, val), reads=[], writes=[ap])

    def recip(self, out, in_):
        self.emit("vector", lambda g: g.reciprocal(out, in_), reads=[in_], writes=[out])

    def max8(self, out, in_):
        self.emit("vector", lambda g: g.max(out=out, in_=in_), reads=[in_], writes=[out])

    def scan(self, out, a, b, initial, op0=ALU.mult, op1=ALU.add):
        reads = [a, b]
        if not isinstance(initial, (int, float)):
            reads.append(initial)
        self.emit("vector", lambda g: g.tensor_tensor_scan(out, a, b, initial, op0, op1),
                  reads=reads, writes=[out])


_NAME_N = [0]


class Ctx:

    def __init__(self, p, stack):
        self.p = p
        self.stack = stack
        self.nc = p.nc
        self.hbm = {}
        self.xres = self.sb("xres", [P, ND * T], F32)
        self.xn_bf = self.sb("xn_bf", [P, ND * T], BF16)
        self.ones_row = self.sb("ones_row", [1, P], F32)
        self.ones_col = self.sb("ones_col", [P, 1], F32)
        self.ones_col_bf = self.sb("ones_col_bf", [P, 1], BF16)
        p.memset(self.ones_row, 1.0)
        p.memset(self.ones_col, 1.0)
        p.memset(self.ones_col_bf, 1.0)

    def sb(self, name, shape, dtype=F32):
        _NAME_N[0] += 1
        return self.stack.enter_context(
            self.nc.sbuf_tensor(f"{name}_{_NAME_N[0]}", shape, dtype))[:]

    def ps(self, name, shape, dtype=F32):
        _NAME_N[0] += 1
        return self.stack.enter_context(
            self.nc.psum_tensor(f"{name}_{_NAME_N[0]}", shape, dtype))[:]

    def xres_t(self, dt):
        return self.xres[:, dt * T:(dt + 1) * T]

    def xn_t(self, dt):
        return self.xn_bf[:, dt * T:(dt + 1) * T]

    def scoped(self, st):
        c = Ctx.__new__(Ctx)
        c.__dict__.update(self.__dict__)
        c.stack = st
        return c

    def regconst(self, v):
        if v not in self._consts:
            i = len(self._consts)
            assert i < 16, "const pool exhausted"
            t = self.cstpool[:, i:i + 1]
            self.p.emit("vector", lambda g: g.memset(t, v), reads=[], writes=[t])
            self.nc.const_aps.register(F32, v, t)
            self._consts[v] = t
        return self._consts[v]

    def setup_consts(self):
        self._consts = {}
        self.stack0 = self.stack
        self.p.ctx = self
        self.cstpool = self.sb("cstpool", [P, 16], F32)
        self.ident_f = self.sb("ident_f", [P, P], F32)
        self.ident_b = self.sb("ident_b", [P, P], BF16)
        self.p.dma(self.ident_f, hin(self, "c_ident_f", np.eye(P, dtype=np.float32)))
        self.p.dma(self.ident_b, hin(self, "c_ident_b", _bf(np.eye(P))))
        self.pp = [self.ps(f"pool{i}", [P, 512], F32) for i in range(6)]
        self.ppb = self.ps("poolbf", [P, 512], BF16)
        self.wstg = self.sb("wstg", [P, 2048], F32)


def _bf(a):
    import ml_dtypes
    return np.asarray(a, dtype=ml_dtypes.bfloat16)



_HOST = {}


def hin(ctx, name, arr, dtype=None):
    if name in ctx.hbm:
        return ctx.hbm[name]
    a = np.ascontiguousarray(arr)
    if dtype is None:
        dtype = F32 if a.dtype == np.float32 else BF16
    ap = ctx.nc.dram_tensor(name, list(a.shape), dtype, kind="ExternalInput").ap()
    ctx.hbm[name] = ap
    _HOST[name] = a
    return ap



def wload(ctx, stack, hbm_ap, d_in, d_out, tag, priv=False):
    p = ctx.p
    nkt = (d_in + P - 1) // P
    wsb = ctx.sb(f"w_{tag}", [P, nkt * d_out], BF16)
    assert 2 * d_out <= 2048
    if priv:
        nst = min(2, nkt)
        stgs = [ctx.sb(f"wst_{tag}{i}", [P, d_out], F32) for i in range(nst)]
        slc = [s[:, :] for s in stgs]
    else:
        nst = 2
        slc = [ctx.wstg[:, i * d_out: (i + 1) * d_out] for i in range(2)]
    for kt in range(nkt):
        rows = min(P, d_in - kt * P)
        s = slc[kt % nst]
        p.dma(s[0:rows, :], hbm_ap[kt * P: kt * P + rows, :])
        p.copy(wsb[0:rows, kt * d_out:(kt + 1) * d_out], s[0:rows, :], engine="gpsimd")
    return wsb


def vload_pp(ctx, hbm_ap, n, tag, ident):
    p = ctx.p
    if n <= P:
        stg = ctx.sb(f"vst_{tag}", [1, n], F32)
        p.dma(stg, hbm_ap.rearrange("(a n) -> a n", a=1) if len(hbm_ap.shape) == 1 else hbm_ap)
        ps = ctx.pp[5][0:n, 0:1]
        p.transpose(ps, stg, ident[0:1, 0:1])
        out = ctx.sb(f"vpp_{tag}", [P, 1], F32)
        p.memset(out, 0.0)
        p.copy(out[0:n, :], ps, engine="scalar")
        return out
    nk = n // P
    assert nk <= P
    stg = ctx.sb(f"vst_{tag}", [nk, P], F32)
    p.dma(stg, hbm_ap.rearrange("(k p) -> k p", p=P))
    ps = ctx.pp[5][:, 0:nk]
    p.transpose(ps, stg, ident[0:nk, 0:nk])
    out = ctx.sb(f"vpp_{tag}", [P, nk], F32)
    p.copy(out, ps, engine="scalar")
    return out


def linear(ctx, wsb, d_in, d_out, in_sb, out_sb, in_cols=T, bias_pp=None,
           func=AF.Identity, psA=None, psB=None, out_scale=None, accum_res=False,
           in_col_off=0, w_col_off=0, w_row_tiles=None, in_T_stride=None,
           scale_pp=None, out_col_off=0):
    p = ctx.p
    nkt = w_row_tiles if w_row_tiles is not None else d_in // P
    nmt = (d_out + P - 1) // P
    wst = wsb.shape[1] // nkt
    its = in_T_stride if in_T_stride is not None else in_cols
    ncol = 512
    ncs = in_cols // ncol
    if psA is None:
        psA = ctx.pp[0]
        psB = ctx.pp[1]
    pss = [psA, psB]
    scr = ctx.sb("lin_scr", [P, ncol], F32) if accum_res else None
    i = 0
    for mt in range(nmt):
        mw = min(P, d_out - mt * P)
        for c in range(ncs):
            ps = pss[i % 2]
            i += 1
            for kt in range(nkt):
                lhs = wsb[:, kt * wst + w_col_off + mt * P: kt * wst + w_col_off + mt * P + mw]
                rhs = in_sb[:, kt * its + in_col_off + c * ncol: kt * its + in_col_off + (c + 1) * ncol]
                p.mm(ps[0:mw, :], lhs, rhs, start=(kt == 0), stop=(kt == nkt - 1))
            b = bias_pp[:, mt:mt + 1] if bias_pp is not None else None
            sc = out_scale if out_scale is not None else 1.0
            if scale_pp is not None:
                sc = scale_pp[:, mt:mt + 1]
            if accum_res:
                p.act(scr, ps[0:mw, :], func, bias=b, scale=sc)
                xr = ctx.xres[:, mt * T + in_col_off + c * ncol: mt * T + in_col_off + (c + 1) * ncol]
                p.tt(xr, xr, scr, ALU.add, engine="vector" if (mt + c) % 2 else "gpsimd")
            else:
                o = out_sb[:, mt * in_cols + out_col_off + c * ncol: mt * in_cols + out_col_off + (c + 1) * ncol]
                p.act(o[0:mw, :] if o.shape[0] > mw else o, ps[0:mw, :], func, bias=b, scale=sc)


def bcast_row(ctx, out_sb, row_ap, ps=None, cols=None):
    p = ctx.p
    n = cols if cols is not None else row_ap.shape[-1]
    if ps is None:
        ps = ctx.pp[5]
    for c0 in range(0, n, 512):
        c1 = min(c0 + 512, n)
        p.mm(ps[:, 0:c1 - c0], ctx.ones_row, row_ap[0:1, c0:c1], start=True, stop=True)
        p.copy(out_sb[:, c0:c1], ps[:, 0:c1 - c0], engine="scalar")


def colsum(ctx, ps_row, tile_bf, start, stop):
    ctx.p.mm(ps_row, ctx.ones_col_bf, tile_bf, start=start, stop=stop)


def softplus(ctx, out, in_, scr1, scr2):
    p = ctx.p
    p.act(scr1, in_, AF.Abs)
    p.act(scr1, scr1, AF.Exp, scale=-1.0)
    p.act(scr1, scr1, AF.Ln, bias=1.0)
    p.act(scr2, in_, AF.Relu)
    p.tt(out, scr1, scr2, ALU.add)


def layernorm(ctx, need_f32=False, keep_stats=False):
    p = ctx.p
    xn_f32 = ctx.sb("ln_xnf", [P, ND * T], F32) if need_f32 else None
    if keep_stats:
        bm = ctx.sb("ln_bm", [P, T], F32)
        br = ctx.sb("ln_br", [P, T], F32)
    with ExitStack() as st:
        c = ctx.scoped(st)
        if not keep_stats:
            bm = c.sb("ln_bm", [P, T], F32)
            br = c.sb("ln_br", [P, T], F32)
        sqc = c.sb("ln_sqc", [P, 512], F32)
        r_sum = c.sb("ln_rs", [1, 512], F32)
        r_rstd = c.sb("ln_rv", [1, 512], F32)
        ps_a = ctx.pp[4][0:1, :]
        ps_b = ctx.pp[5][0:1, :]
        ps_c = ctx.pp[3]
        for cc in range(TC):
            cs = slice(cc * 512, (cc + 1) * 512)
            for dt in range(ND):
                p.mm(ps_a, ctx.ones_col, ctx.xres[:, dt * T + cc * 512: dt * T + (cc + 1) * 512],
                     start=(dt == 0), stop=(dt == ND - 1))
            for dt in range(ND):
                p.act(sqc, ctx.xres[:, dt * T + cc * 512: dt * T + (cc + 1) * 512], AF.Square)
                p.mm(ps_b, ctx.ones_col, sqc, start=(dt == 0), stop=(dt == ND - 1))
            p.ts(r_sum, ps_a, 1.0 / D, ALU.mult)
            p.act(r_rstd, r_sum, AF.Square)
            p.stt(r_rstd, ps_b, 1.0 / D, r_rstd, ALU.mult, ALU.subtract)
            p.act(r_rstd, r_rstd, AF.Ln, bias=1e-5)
            p.act(r_rstd, r_rstd, AF.Exp, scale=-0.5)
            p.mm(ps_c, ctx.ones_row, r_sum, start=True, stop=True)
            p.copy(bm[:, cs], ps_c, engine="scalar")
            p.mm(ps_c, ctx.ones_row, r_rstd, start=True, stop=True)
            p.copy(br[:, cs], ps_c, engine="scalar")
        sq = c.sb("ln_sq", [P, T], F32)
        for dt in range(ND):
            p.tt(sq, ctx.xres_t(dt), bm, ALU.subtract)
            if need_f32:
                xt = xn_f32[:, dt * T:(dt + 1) * T]
                p.tt(xt, sq, br, ALU.mult)
                p.copy(ctx.xn_t(dt), xt, engine="gpsimd")
            else:
                p.tt(ctx.xn_t(dt), sq, br, ALU.mult)
        p.barrier()
    if keep_stats:
        return xn_f32, bm, br
    return xn_f32, None, None



def linear_sbacc(ctx, wsb, nkt, d_out, in_sb, in_cols, out_acc, out_T_stride,
                 first, scr, in_col_off=0, w_col_off=0, wst=None, in_T_stride=None,
                 psA=None, psB=None):
    p = ctx.p
    nmt = d_out // P
    wst = wst if wst is not None else wsb.shape[1] // nkt
    its = in_T_stride if in_T_stride is not None else in_cols
    ncol = 512
    ncs = in_cols // ncol
    pss = [psA, psB]
    i = 0
    for mt in range(nmt):
        for c in range(ncs):
            ps = pss[i % 2]; i += 1
            for kt in range(nkt):
                lhs = wsb[:, kt * wst + w_col_off + mt * P: kt * wst + w_col_off + (mt + 1) * P]
                rhs = in_sb[:, kt * its + in_col_off + c * ncol: kt * its + in_col_off + (c + 1) * ncol]
                p.mm(ps, lhs, rhs, start=(kt == 0), stop=(kt == nkt - 1))
            o = out_acc[:, mt * out_T_stride + c * ncol: mt * out_T_stride + (c + 1) * ncol]
            if first:
                p.copy(o, ps, engine="scalar")
            else:
                p.copy(scr, ps, engine="scalar")
                eng = "vector" if (i % 2) else "gpsimd"
                p.tt(o, o, scr, ALU.add, engine=eng)


def residual_add(ctx, src_sb, scale_pp=None, src_stride=T, nmt=ND):
    p = ctx.p
    for mt in range(nmt):
        xr = ctx.xres_t(mt)
        s = src_sb[:, mt * src_stride:(mt + 1) * src_stride]
        eng = "vector" if mt % 2 else "gpsimd"
        if scale_pp is not None:
            p.stt(xr, s, scale_pp[:, mt:mt + 1], xr, ALU.mult, ALU.add, engine=eng)
        else:
            p.tt(xr, xr, s, ALU.add, engine=eng)



def input_transpose(ctx, x0_ap):
    p = ctx.p
    with ExitStack() as st:
        c = ctx.scoped(st)
        _input_transpose(c, x0_ap)
        p.barrier()


def _input_transpose(ctx, x0_ap):
    p = ctx.p
    stg = [ctx.sb(f"itr{i}", [P, D], F32) for i in range(2)]
    pss = [ctx.pp[0][:, 0:P], ctx.pp[1][:, 0:P]]
    for tt in range(NT):
        s = stg[tt % 2]
        p.dma(s, x0_ap[tt * P:(tt + 1) * P, :])
        for dt in range(ND):
            ps = pss[dt % 2]
            p.transpose(ps, s[:, dt * P:(dt + 1) * P], ctx.ident_f)
            p.copy(ctx.xres[:, dt * T + tt * P: dt * T + (tt + 1) * P], ps, engine="scalar")


def output_transpose(ctx, xout_ap):
    p = ctx.p
    with ExitStack() as st:
        c = ctx.scoped(st)
        _output_transpose(c, xout_ap)
        p.barrier()


def _output_transpose(ctx, xout_ap):
    p = ctx.p
    stg = [ctx.sb(f"otr{i}", [P, D], F32) for i in range(2)]
    pss = [ctx.pp[0][:, 0:P], ctx.pp[1][:, 0:P]]
    for tt in range(NT):
        s = stg[tt % 2]
        for dt in range(ND):
            ps = pss[dt % 2]
            p.transpose(ps, ctx.xres[:, dt * T + tt * P: dt * T + (tt + 1) * P], ctx.ident_f)
            p.copy(s[:, dt * P:(dt + 1) * P], ps, engine="scalar")
        p.dma(xout_ap[tt * P:(tt + 1) * P, :], s)


def fm_dma_in(ctx, xin_ap):
    for dt in range(ND):
        ctx.p.dma(ctx.xres_t(dt), xin_ap[dt * P:(dt + 1) * P, :])


def fm_dma_out(ctx, xout_ap):
    for dt in range(ND):
        ctx.p.dma(xout_ap[dt * P:(dt + 1) * P, :], ctx.xres_t(dt))



def st_l2_ffn(ctx, ins, uid):
    p = ctx.p
    H = 4096
    w1T = hin(ctx, "l2f_w1T", ins["L2_ffn_w1"].T)
    w2T = hin(ctx, "l2f_w2T", ins["L2_ffn_w2"].T)
    b1 = hin(ctx, "l2f_b1", ins["L2_ffn_b1"])
    b2 = hin(ctx, "l2f_b2", ins["L2_ffn_b2"])
    with ExitStack() as st:
        c = ctx.scoped(st)
        _, bm, br = layernorm(c)
        b1pp = load_pp(c, "l2f_b1pp", ins["L2_ffn_b1"])
        b2pp = load_pp(c, "l2f_b2pp", ins["L2_ffn_b2"])
        stg = c.sb("bstg", [P, 1024], F32)
        ast = c.sb("bast", [P, 1024], F32)
        a1row = c.sb("a1row", [1, H], F32)
        a2row = c.sb("a2row", [1, D], F32)
        psr = [c.pp[2][0:1, :], c.pp[3][0:1, :]]
        for hc in range(4):
            for kt in range(ND):
                p.dma(stg, w1T[kt * P:(kt + 1) * P, hc * 1024:(hc + 1) * 1024])
                p.act(ast, stg, AF.Abs)
                for cc in range(2):
                    p.mm(psr[cc], ctx.ones_col, ast[:, cc * 512:(cc + 1) * 512],
                         start=(kt == 0), stop=(kt == ND - 1))
            for cc in range(2):
                p.copy(a1row[:, hc * 1024 + cc * 512: hc * 1024 + (cc + 1) * 512], psr[cc], engine="scalar")
        for kt in range(32):
            p.dma(stg, w2T[kt * P:(kt + 1) * P, :])
            p.act(ast, stg, AF.Abs)
            for cc in range(2):
                p.mm(psr[cc], ctx.ones_col, ast[:, cc * 512:(cc + 1) * 512],
                     start=(kt == 0), stop=(kt == 31))
        for cc in range(2):
            p.copy(a2row[:, cc * 512:(cc + 1) * 512], psr[cc], engine="scalar")
        p.ts(a1row, a1row, 1.0 / D, ALU.mult)
        p.ts(a2row, a2row, 1.0 / H, ALU.mult)
        dsc = ctx.nc.dram_tensor(f"l2f_dsc{uid}", [H + D], F32, kind="Internal").ap()
        p.dma(dsc.rearrange("(a b) -> a b", a=1)[0:1, 0:H], a1row)
        p.dma(dsc.rearrange("(a b) -> a b", a=1)[0:1, H:H + D], a2row)
        a1pp = vload_pp(c, dsc[0:H], H, f"a1pp{uid}", ctx.ident_f)
        a2pp = vload_pp(c, dsc[H:H + D], D, f"a2pp{uid}", ctx.ident_f)
        w1s = c.sb("w1s", [P, ND * 1024], BF16)
        w2s = c.sb("w2s", [P, 8 * 1024], BF16)
        hbf = c.sb("hbf", [P, 8 * 1024], BF16)
        oacc = c.sb("oacc", [P, ND * 512], F32)
        scr = c.sb("bscr", [P, 512], F32)
        psA = c.pp[0]
        psB = c.pp[1]
        for tq in range(4):
            toff = tq * 512
            for hc in range(4):
                for kt in range(ND):
                    p.dma(stg, w1T[kt * P:(kt + 1) * P, hc * 1024:(hc + 1) * 1024])
                    p.act(w1s[:, kt * 1024:(kt + 1) * 1024], stg, AF.Sign)
                for mt in range(8):
                    ps = psA if mt % 2 == 0 else psB
                    for kt in range(ND):
                        p.mm(ps, w1s[:, kt * 1024 + mt * P: kt * 1024 + (mt + 1) * P],
                             ctx.xn_bf[:, kt * T + toff: kt * T + toff + 512],
                             start=(kt == 0), stop=(kt == ND - 1))
                    gi = hc * 8 + mt
                    p.act(hbf[:, mt * 1024: mt * 1024 + 512],
                          ps, AF.Gelu,
                          bias=b1pp[:, gi:gi + 1], scale=a1pp[:, gi:gi + 1])
                for kt in range(8):
                    p.dma(stg, w2T[(hc * 8 + kt) * P:(hc * 8 + kt + 1) * P, :])
                    p.act(w2s[:, kt * 1024:(kt + 1) * 1024], stg, AF.Sign)
                linear_sbacc(ctx, w2s, 8, D, hbf, 512, oacc, 512,
                             first=(hc == 0), scr=scr, in_T_stride=1024,
                             psA=psA, psB=psB)
            for mt in range(ND):
                p.ts(oacc[:, mt * 512:(mt + 1) * 512], oacc[:, mt * 512:(mt + 1) * 512],
                     a2pp[:, mt:mt + 1], ALU.mult, b2pp[:, mt:mt + 1], ALU.add,
                     engine="vector" if mt % 2 else "gpsimd")
                xr = ctx.xres[:, mt * T + toff: mt * T + toff + 512]
                p.tt(xr, xr, oacc[:, mt * 512:(mt + 1) * 512], ALU.add,
                     engine="gpsimd" if mt % 2 else "vector")
        p.barrier()



def st_l3_ffn(ctx, ins, uid):
    p = ctx.p
    names = ["cn_w1", "cn_b1", "cn_w2", "cn_b2", "cn_w3", "cn_b3", "cn_w4", "cn_b4",
             "gc_w1", "gc_b1", "gc_w2", "gc_b2", "gc_w3", "gc_b3", "adaptive_range"]
    h = {n: hin(ctx, f"l3f_{n}", ins[f"L3_ffn_{n}"]) for n in names}
    with ExitStack() as st:
        c = ctx.scoped(st)
        xn, bm, br = layernorm(c, need_f32=True)
        w1 = wload(c, st, h["cn_w1"], D, D, f"cw1{uid}")
        w2 = wload(c, st, h["cn_w2"], D, 512, f"cw2{uid}")
        w3 = wload(c, st, h["cn_w3"], 512, 256, f"cw3{uid}")
        w4 = wload(c, st, h["cn_w4"], 256, 1, f"cw4{uid}")
        b1pp = load_pp(c, "l3f_b1pp", ins["L3_ffn_cn_b1"])
        b2pp = load_pp(c, "l3f_b2pp", ins["L3_ffn_cn_b2"])
        b3pp = load_pp(c, "l3f_b3pp", ins["L3_ffn_cn_b3"])
        b4pp = load_pp(c, "l3f_b4pp", ins["L3_ffn_cn_b4"])
        arsb = load_pp(c, "l3f_arpp", ins["L3_ffn_adaptive_range"])
        gfs = c.sb("gfs", [P, 1], F32)
        ar2 = c.sb("ar2", [P, 1], F32)
        psA = c.pp[0]
        psB = c.pp[1]
        with ExitStack() as st_g:
            cg = c.scoped(st_g)
            g1 = wload(cg, st_g, h["gc_w1"], D, 512, f"gw1{uid}")
            g2 = wload(cg, st_g, h["gc_w2"], 512, 256, f"gw2{uid}")
            g3 = wload(cg, st_g, h["gc_w3"], 256, 1, f"gw3{uid}")
            gb1pp = load_pp(cg, "l3f_gb1pp", ins["L3_ffn_gc_b1"])
            gb2pp = load_pp(cg, "l3f_gb2pp", ins["L3_ffn_gc_b2"])
            gb3pp = load_pp(cg, "l3f_gb3pp", ins["L3_ffn_gc_b3"])
            gc = cg.sb("gc", [P, ND], F32)
            gcb = cg.sb("gcb", [P, ND], BF16)
            gh1 = cg.sb("gh1", [P, 4], BF16)
            gh2 = cg.sb("gh2", [P, 2], BF16)
            for dt in range(ND):
                p.red(gc[:, dt:dt + 1], xn[:, dt * T:(dt + 1) * T], ALU.add)
            p.ts(gc, gc, 1.0 / T, ALU.mult)
            p.copy(gcb, gc)
            for mt in range(4):
                for kt in range(ND):
                    p.mm(psA[:, 0:1], g1[:, kt * 512 + mt * P: kt * 512 + (mt + 1) * P],
                         gcb[:, kt:kt + 1], start=(kt == 0), stop=(kt == ND - 1))
                p.act(gh1[:, mt:mt + 1], psA[:, 0:1], AF.Gelu, bias=gb1pp[:, mt:mt + 1])
            for mt in range(2):
                for kt in range(4):
                    p.mm(psA[:, 0:1], g2[:, kt * 256 + mt * P: kt * 256 + (mt + 1) * P],
                         gh1[:, kt:kt + 1], start=(kt == 0), stop=(kt == 3))
                p.act(gh2[:, mt:mt + 1], psA[:, 0:1], AF.Gelu, bias=gb2pp[:, mt:mt + 1])
            for kt in range(2):
                p.mm(psA[0:1, 0:1], g3[:, kt * 1: kt * 1 + 1], gh2[:, kt:kt + 1],
                     start=(kt == 0), stop=(kt == 1))
            p.act(gfs[0:1, :], psA[0:1, 0:1], AF.Sigmoid, bias=gb3pp[0:1, 0:1])
            p.ts(gfs[0:1, :], gfs[0:1, :], 0.7, ALU.mult, -0.5, ALU.add)
            p.ts(ar2, arsb, 2.0, ALU.mult)
            p.barrier()
        out = xn
        outb = ctx.xn_bf
        hbuf = c.sb("hbuf", [P, ND * 512], BF16)
        h1b = hbuf
        h2b = c.sb("h2b", [P, 4 * 512], BF16)
        h3b = hbuf[:, 0:2 * 512]
        cpl = ctx.wstg[:, 0:512]
        t1 = ctx.wstg[:, 512:1024]
        snap = ctx.wstg[:, 1024:1536]
        tfr = ctx.wstg[0:1, 1536:2048]
        HT = 512
        pairs = [(0, 4, 1.0, -1.0), (1, 5, 1.0, -1.0),
                 (2, 6, -1.0, 1.0), (3, 7, -1.0, 1.0)]
        for cyc in range(3):
            for th in range(4):
                toff = th * HT
                linear(c, w1, D, D, outb, h1b, in_cols=HT, in_col_off=toff,
                       in_T_stride=T, bias_pp=b1pp, func=AF.Gelu, psA=psA, psB=psB)
                linear(c, w2, D, 512, h1b, h2b, in_cols=HT, bias_pp=b2pp,
                       func=AF.Gelu, psA=psA, psB=psB)
                linear(c, w3, 512, 256, h2b, h3b, in_cols=HT, bias_pp=b3pp,
                       func=AF.Gelu, psA=psA, psB=psB)
                for kt in range(2):
                    p.mm(psA[0:1, :], w4[:, kt:kt + 1],
                         h3b[:, kt * HT: kt * HT + 512],
                         start=(kt == 0), stop=(kt == 1))
                p.act(tfr, psA[0:1, :], AF.Sigmoid, bias=b4pp[0:1, 0:1])
                p.ts(tfr, tfr, 0.3, ALU.mult, gfs[0:1, 0:1], ALU.add)
                p.ts(tfr, tfr, ar2[0:1, 0:1], ALU.mult, 0.1, ALU.add)
                bcast_row(c, cpl, tfr, ps=psB, cols=HT)
                for (a, b, sa, sgb) in pairs:
                    oa = out[:, a * T + toff: a * T + toff + HT]
                    ob = out[:, b * T + toff: b * T + toff + HT]
                    p.copy(snap, oa, engine="gpsimd")
                    p.tt(t1, cpl, ob, ALU.mult)
                    p.stt(oa, t1, sa, oa, ALU.mult, ALU.add)
                    p.tt(t1, cpl, snap, ALU.mult)
                    p.stt(ob, t1, sgb, ob, ALU.mult, ALU.add, engine="gpsimd")
                if cyc < 2:
                    for dt in range(ND):
                        p.copy(outb[:, dt * T + toff: dt * T + toff + HT],
                               out[:, dt * T + toff: dt * T + toff + HT],
                               engine="gpsimd" if dt % 2 else "vector")
        residual_add(ctx, out)
        p.barrier()



def pp_host(vec):
    v = np.asarray(vec, dtype=np.float32).reshape(-1)
    n = v.shape[0]
    if n < P:
        out = np.zeros((P, 1), np.float32)
        out[:n, 0] = v
        return out
    return np.ascontiguousarray(v.reshape(n // P, P).T)


def load_pp(ctx, name, vec):
    h = hin(ctx, name, pp_host(vec))
    sb = ctx.sb(f"pp_{name}", list(h.shape), F32)
    ctx.p.dma(sb, h)
    return sb


def rope_apply2(c, x_f32, rows, rotT_bf, cosE, sinE, out_bf, psA, scr_bf, scr_f, scr_f2):
    p = c.p
    for cc in range(TC):
        cs = slice(cc * 512, (cc + 1) * 512)
        p.copy(scr_bf[0:rows, 0:512], x_f32[0:rows, cs])
        p.mm(psA[0:rows, :], rotT_bf[0:rows, 0:rows], scr_bf[0:rows, 0:512],
             start=True, stop=True)
        p.copy(scr_f[0:rows, :], psA[0:rows, :], engine="scalar")
        p.tt(scr_f[0:rows, :], scr_f[0:rows, :], sinE[0:rows, cs], ALU.mult)
        p.tt(scr_f2[0:rows, :], x_f32[0:rows, cs], cosE[0:rows, cs], ALU.mult,
             engine="gpsimd")
        p.tt(out_bf[0:rows, cs], scr_f[0:rows, :], scr_f2[0:rows, :], ALU.add)


def attn_group(c, heads, dv, vtok, vst, jts_fn, mask_fn, scale, clip, out_fn,
               psS, psO, psD, psB2, s_f, pb, den, rc, rcb, o_f):
    p = c.p
    for (parts, h_id, voff) in heads:
        for ic in range(TC):
            jts = jts_fn(ic)
            nj = len(jts)
            for ji, jt in enumerate(jts):
                npart = len(parts)
                for pi, (kap, qap, dk) in enumerate(parts):
                    p.mm(psS, kap[0:dk, jt * P:(jt + 1) * P],
                         qap[0:dk, ic * 512:(ic + 1) * 512],
                         start=(pi == 0), stop=(pi == npart - 1))
                mask = mask_fn(ic, jt)
                src = psS
                if clip is not None:
                    p.ts(s_f, psS, clip, ALU.min, -clip, ALU.max)
                    src = s_f
                if mask is not None:
                    if src is psS:
                        p.tt(s_f, psS, mask, ALU.add)
                    else:
                        p.tt(s_f, s_f, mask, ALU.add)
                    src = s_f
                pbt = pb[ji % 2]
                p.act(pbt, src, AF.Exp, scale=scale)
                colsum(c, psD, pbt, start=(ji == 0), stop=(ji == nj - 1))
                p.mm(psO[0:dv, :], vtok[:, jt * vst + voff: jt * vst + voff + dv],
                     pbt, start=(ji == 0), stop=(ji == nj - 1))
            p.copy(den, psD, engine="scalar")
            p.recip(rc, den)
            p.mm(psB2, c.ones_row, rc, start=True, stop=True)
            p.copy(rcb, psB2, engine="scalar")
            p.copy(o_f[0:dv, :], psO[0:dv, :], engine="scalar")
            p.tt(o_f[0:dv, :], o_f[0:dv, :], rcb[0:dv, :], ALU.mult)
            out_fn(h_id, ic, o_f)


def make_attn_scratch(c, tag):
    psS = c.pp[2]
    psO = c.pp[3]
    psD = c.pp[4][0:1, :]
    psB2 = c.pp[5]
    s_f = c.sb(f"asf_{tag}", [P, 512], F32)
    pb = [c.sb(f"apb{i}_{tag}", [P, 512], BF16) for i in range(2)]
    den = c.sb(f"aden_{tag}", [1, 512], F32)
    rc = c.sb(f"arc_{tag}", [1, 512], F32)
    rcb = c.sb(f"arcb_{tag}", [P, 512], F32)
    o_f = c.sb(f"aof_{tag}", [P, 512], F32)
    return dict(psS=psS, psO=psO, psD=psD, psB2=psB2, s_f=s_f, pb=pb, den=den,
                rc=rc, rcb=rcb, o_f=o_f)


def causal_mask_host(q):
    jl = np.arange(P)[:, None]
    il = np.arange(512)[None, :]
    j = q * P + jl
    return np.where(j > il, -30000.0, 0.0).astype(np.float32)


def window_mask_host(q, w=512):
    jl = np.arange(P)[:, None]
    il = np.arange(512)[None, :]
    j = q * P + jl
    d = il - j
    return np.where((d < 0) | (d > w - 1), -30000.0, 0.0).astype(np.float32)


def rope_tables_l0b():
    half = 32
    theta = 1.0 / 10000.0 ** (np.arange(half, dtype=np.float64) / half)
    pos = np.arange(T, dtype=np.float64)
    fr = pos[:, None] * theta[None, :]
    cos = np.cos(fr).T
    sin = np.sin(fr).T
    cosE = np.concatenate([cos, cos], 0)
    sinE = np.concatenate([sin, sin], 0)
    R = np.zeros((64, 64), np.float64)
    for i in range(32):
        R[i, i + 32] = -1.0
        R[i + 32, i] = 1.0
    return cosE, sinE, R


def rope_tables_l2b():
    n = 32
    j = np.arange(n, dtype=np.float64)
    freqs = 1.0 * (40.0 / 1.0) ** (j / (n - 1))
    pos = np.arange(T, dtype=np.float64) / T
    th = 2.0 * np.pi * pos[:, None] * freqs[None, :]
    cos = np.cos(th).T
    sin = np.sin(th).T
    cosE = np.concatenate([cos, cos], 0)
    sinE = np.concatenate([sin, sin], 0)
    R = np.zeros((64, 64), np.float64)
    for i in range(32):
        R[i, i + 32] = -1.0
        R[i + 32, i] = 1.0
    return cosE, sinE, R


def rope_tables_l1b():
    d = 32
    inv = 10000.0 ** (-np.arange(0, d, 2, dtype=np.float64) / d)
    ang = np.arange(T, dtype=np.float64)[:, None] * inv[None, :]
    cos = np.cos(ang).T
    sin = np.sin(ang).T
    cosE = np.repeat(cos, 2, axis=0)
    sinE = np.repeat(sin, 2, axis=0)
    R = np.zeros((32, 32), np.float64)
    for k in range(16):
        R[2 * k, 2 * k + 1] = -1.0
        R[2 * k + 1, 2 * k] = 1.0
    return cosE, sinE, R


def _tile4(a, reps):
    return np.ascontiguousarray(np.tile(a, (reps, 1)).astype(np.float32))


def _blockdiag(R, reps):
    n = R.shape[0]
    out = np.zeros((n * reps, n * reps), np.float64)
    for i in range(reps):
        out[i * n:(i + 1) * n, i * n:(i + 1) * n] = R
    return out



def proj_fm_group(c, wsb, bias_pp, bi0, dst_f32, nkt=ND, wst=512, wcol0=0):
    p = c.p
    for cc in range(TC):
        ps = c.pp[0] if cc % 2 == 0 else c.pp[1]
        for kt in range(nkt):
            p.mm(ps, wsb[:, kt * wst + wcol0: kt * wst + wcol0 + P],
                 c.xn_bf[:, kt * T + cc * 512: kt * T + (cc + 1) * 512],
                 start=(kt == 0), stop=(kt == nkt - 1))
        b = bias_pp[:, bi0:bi0 + 1] if bias_pp is not None else None
        p.act(dst_f32[:, cc * 512:(cc + 1) * 512], ps, AF.Identity, bias=b)


def wo_from_dram(c, odram, woT_hbm, bias_pp, uid, d_in=D):
    p = c.p
    nkt = d_in // P
    with ExitStack() as st:
        c2 = c.scoped(st)
        wo = wload(c2, st, woT_hbm, d_in, D, f"wo{uid}")
        ostg = c2.sb(f"ostg{uid}", [P, nkt * 512], BF16)
        scr = c2.sb(f"oscr{uid}", [P, 512], F32)
        for cc in range(TC):
            for kt in range(nkt):
                p.dma(ostg[:, kt * 512:(kt + 1) * 512],
                      odram[kt * P:(kt + 1) * P, cc * 512:(cc + 1) * 512])
            for mt in range(ND):
                ps = c.pp[0] if mt % 2 == 0 else c.pp[1]
                for kt in range(nkt):
                    p.mm(ps, wo[:, kt * D + mt * P: kt * D + (mt + 1) * P],
                         ostg[:, kt * 512:(kt + 1) * 512],
                         start=(kt == 0), stop=(kt == nkt - 1))
                b = bias_pp[:, mt:mt + 1] if bias_pp is not None else None
                p.act(scr, ps, AF.Identity, bias=b)
                xr = c.xres[:, mt * T + cc * 512: mt * T + (cc + 1) * 512]
                p.tt(xr, xr, scr, ALU.add, engine="vector" if mt % 2 else "gpsimd")
        p.barrier()


def st_l2_attn_b(ctx, ins, uid):
    p = ctx.p
    qkvT = hin(ctx, "l2b_qkvT", ins["L2_attn_b_qkv_w"].T)
    projT = hin(ctx, "l2b_projT", ins["L2_attn_b_proj_w"].T)
    cosE, sinE, R = rope_tables_l2b()
    cosT = hin(ctx, "l2b_cos", _tile4(cosE, 2))
    sinT = hin(ctx, "l2b_sin", _tile4(sinE, 2))
    rotT = hin(ctx, "l2b_rot", _bf(_blockdiag(R, 2).T))
    odram = ctx.nc.dram_tensor(f"l2b_od{uid}", [D, T], BF16, kind="Internal").ap()
    with ExitStack() as st:
        c = ctx.scoped(st)
        layernorm(c)
        qb_pp = load_pp(c, "l2b_qb", ins["L2_attn_b_qkv_b"][0:1024])
        kb_pp = load_pp(c, "l2b_kb", ins["L2_attn_b_qkv_b"][1024:2048])
        pjb_pp = load_pp(c, "l2b_pjb", ins["L2_attn_b_proj_b"])
        vb_row = hin(ctx, "l2b_vb_row",
                     np.ascontiguousarray(ins["L2_attn_b_qkv_b"][None, 2048:3072].astype(np.float32)))
        cosS = c.sb("l2b_cosS", [P, T], F32); p.dma(cosS, cosT)
        sinS = c.sb("l2b_sinS", [P, T], F32); p.dma(sinS, sinT)
        rotS = c.sb("l2b_rotS", [P, P], BF16); p.dma(rotS, rotT)
        vbr = c.sb("l2b_vbr", [1, 1024], F32); p.dma(vbr, vb_row)
        vbb = c.sb("l2b_vbb", [P, 1024], F32)
        bcast_row(c, vbb, vbr)
        for g4 in range(4):
            with ExitStack() as st2:
                c2 = c.scoped(st2)
                col0 = g4 * 256
                wq = wload(c2, st2, qkvT[:, col0:col0 + 256], D, 256, f"l2bq{g4}{uid}")
                wk = wload(c2, st2, qkvT[:, 1024 + col0:1024 + col0 + 256], D, 256, f"l2bk{g4}{uid}")
                wv = wload(c2, st2, qkvT[:, 2048 + col0:2048 + col0 + 256], D, 256, f"l2bv{g4}{uid}")
                qf = c2.sb("l2b_qf", [P, T], F32)
                sbf = c2.sb("l2b_sbf", [P, 512], BF16)
                sf1 = c2.sb("l2b_sf1", [P, 512], F32)
                sf2 = c2.sb("l2b_sf2", [P, 512], F32)
                qro = c2.sb("l2b_qro", [P, 2 * T], BF16)
                kro = c2.sb("l2b_kro", [P, 2 * T], BF16)
                vtok = c2.sb("l2b_vtok", [P, NT * 256], BF16)
                for gg in range(2):
                    gi = g4 * 2 + gg
                    for (w, bpp, dst) in ((wq, qb_pp, qro), (wk, kb_pp, kro)):
                        proj_fm_group(c2, w, bpp, gi, qf, wst=256, wcol0=gg * P)
                        rope_apply2(c2, qf, P, rotS, cosS, sinS,
                                    dst[:, gg * T:(gg + 1) * T], c2.pp[0], sbf, sf1, sf2)
                for tt in range(NT):
                    ps = c2.pp[0] if tt % 2 == 0 else c2.pp[1]
                    for kt in range(ND):
                        p.mm(ps[:, 0:256], ctx.xn_bf[:, kt * T + tt * P: kt * T + (tt + 1) * P],
                             wv[:, kt * 256:(kt + 1) * 256],
                             start=(kt == 0), stop=(kt == ND - 1))
                    p.copy(sf1[:, 0:256], ps[:, 0:256], engine="scalar")
                    p.tt(vtok[:, tt * 256:(tt + 1) * 256], sf1[:, 0:256],
                         vbb[:, col0:col0 + 256], ALU.add)
                sc = make_attn_scratch(c2, f"l2b{g4}")
                heads = []
                for hl in range(4):
                    tq = (hl * 64) // P
                    r0 = (hl * 64) % P
                    kap = kro[r0:r0 + 64, tq * T:(tq + 1) * T]
                    qap = qro[r0:r0 + 64, tq * T:(tq + 1) * T]
                    heads.append(([(kap, qap, 64)], g4 * 4 + hl, hl * 64))
                obf = c2.sb("l2b_obf", [64, 512], BF16)

                def out_fn(h_id, ic, o_f):
                    p.copy(obf, o_f[0:64, :])
                    p.dma(odram[h_id * 64:(h_id + 1) * 64, ic * 512:(ic + 1) * 512], obf)
                attn_group(c2, heads, 64, vtok, 256, lambda ic: list(range(NT)),
                           lambda ic, jt: None, 0.125, None, out_fn, **sc)
                p.barrier()
        wo_from_dram(c, odram, projT, pjb_pp, f"l2b{uid}")
        p.barrier()


def st_l2_attn_a(ctx, ins, uid):
    p = ctx.p
    I = lambda n: ins[f"L2_attn_a_{n}"]
    wdq = hin(ctx, "l2a_wdq", I("W_DQ"))
    wdkv = hin(ctx, "l2a_wdkv", I("W_DKV"))
    wuq = hin(ctx, "l2a_wuq", I("W_UQ"))
    wuk = hin(ctx, "l2a_wuk", I("W_UK"))
    wuv = hin(ctx, "l2a_wuv", I("W_UV"))
    wqr = hin(ctx, "l2a_wqr", I("W_QR"))
    wkr = hin(ctx, "l2a_wkr", I("W_KR"))
    woT = hin(ctx, "l2a_wo", I("W_O"))
    odram = ctx.nc.dram_tensor(f"l2a_od{uid}", [D, T], BF16, kind="Internal").ap()
    with ExitStack() as st:
        c = ctx.scoped(st)
        layernorm(c)
        bdq = load_pp(c, "l2a_bdq", I("b_DQ"))
        bdkv = load_pp(c, "l2a_bdkv", I("b_DKV"))
        buq = load_pp(c, "l2a_buq", I("b_UQ"))
        buk = load_pp(c, "l2a_buk", I("b_UK"))
        buv_row = hin(ctx, "l2a_buv_row", np.ascontiguousarray(I("b_UV")[None, :].astype(np.float32)))
        bqr = load_pp(c, "l2a_bqr", I("b_QR"))
        bkr = load_pp(c, "l2a_bkr", I("b_KR"))
        bo = load_pp(c, "l2a_bo", I("b_O"))
        cq = c.sb("l2a_cq", [32, T], BF16)
        ckv = c.sb("l2a_ckv", [32, T], BF16)
        kr = c.sb("l2a_kr", [32, T], BF16)
        wdq_s = wload(c, st, wdq, D, 32, f"l2a_dq{uid}")
        wdkv_s = wload(c, st, wdkv, D, 32, f"l2a_dkv{uid}")
        wkr_s = wload(c, st, wkr, D, 32, f"l2a_kr{uid}")
        for (wsb, bpp, dst) in ((wdq_s, bdq, cq), (wdkv_s, bdkv, ckv), (wkr_s, bkr, kr)):
            for cc in range(TC):
                ps = c.pp[0] if cc % 2 == 0 else c.pp[1]
                for kt in range(ND):
                    p.mm(ps[0:32, :], wsb[:, kt * 32: kt * 32 + 32],
                         ctx.xn_bf[:, kt * T + cc * 512: kt * T + (cc + 1) * 512],
                         start=(kt == 0), stop=(kt == ND - 1))
                p.act(dst[:, cc * 512:(cc + 1) * 512], ps[0:32, :], AF.Identity,
                      bias=bpp[0:32, 0:1])
        wuq_s = wload(c, st, wuq, 32, 1024, f"l2a_uq{uid}")
        wuk_s = wload(c, st, wuk, 32, 1024, f"l2a_uk{uid}")
        wuv_s = wload(c, st, wuv, 32, 1024, f"l2a_uv{uid}")
        wqr_s = wload(c, st, wqr, 32, 256, f"l2a_qr{uid}")
        qr = c.sb("l2a_qr", [P, 2 * T], BF16)
        buvr = c.sb("l2a_buvr", [1, 1024], F32); p.dma(buvr, buv_row)
        buvb = c.sb("l2a_buvb", [P, 1024], F32)
        bcast_row(c, buvb, buvr)
        for mt in range(2):
            for cc in range(TC):
                ps = c.pp[0] if cc % 2 == 0 else c.pp[1]
                p.mm(ps, wqr_s[:, mt * P:(mt + 1) * P][0:32, :], cq[:, cc * 512:(cc + 1) * 512],
                     start=True, stop=True)
                p.act(qr[:, mt * T + cc * 512: mt * T + (cc + 1) * 512], ps, AF.Identity,
                      bias=bqr[:, mt:mt + 1])
        qrx = c.sb("l2a_qrx", [64, 2 * T], BF16)
        for tile in range(2):
            p.dma(qrx[:, tile * T:(tile + 1) * T], qr[64:128, tile * T:(tile + 1) * T])
        kr3 = c.sb("l2a_kr3", [96, T], BF16)
        for b in range(3):
            p.dma(kr3[b * 32:(b + 1) * 32, :], kr)
        for g2 in range(4):
            with ExitStack() as st2:
                c2 = c.scoped(st2)
                qc = c2.sb("l2a_qc", [P, 2 * T], BF16)
                kc = c2.sb("l2a_kc", [P, 2 * T], BF16)
                vtok = c2.sb("l2a_vt", [P, NT * 256], BF16)
                sf1 = c2.sb("l2a_sf1", [P, 512], F32)
                for mt in range(2):
                    col = g2 * 256 + mt * P
                    for cc in range(TC):
                        ps = c2.pp[0] if cc % 2 == 0 else c2.pp[1]
                        p.mm(ps, wuq_s[:, col:col + P][0:32, :], cq[:, cc * 512:(cc + 1) * 512],
                             start=True, stop=True)
                        p.act(qc[:, mt * T + cc * 512: mt * T + (cc + 1) * 512], ps,
                              AF.Identity, bias=buq[:, g2 * 2 + mt: g2 * 2 + mt + 1])
                        ps2 = c2.pp[2]
                        p.mm(ps2, wuk_s[:, col:col + P][0:32, :], ckv[:, cc * 512:(cc + 1) * 512],
                             start=True, stop=True)
                        p.act(kc[:, mt * T + cc * 512: mt * T + (cc + 1) * 512], ps2,
                              AF.Identity, bias=buk[:, g2 * 2 + mt: g2 * 2 + mt + 1])
                for tt in range(NT):
                    ps = c2.pp[0] if tt % 2 == 0 else c2.pp[1]
                    p.mm(ps[:, 0:256], ckv[:, tt * P:(tt + 1) * P],
                         wuv_s[0:32, g2 * 256:(g2 + 1) * 256], start=True, stop=True)
                    p.copy(sf1[:, 0:256], ps[:, 0:256], engine="scalar")
                    p.tt(vtok[:, tt * 256:(tt + 1) * 256], sf1[:, 0:256],
                         buvb[:, g2 * 256:(g2 + 1) * 256], ALU.add)
                sc = make_attn_scratch(c2, f"l2a{g2}")
                heads = []
                for hl in range(2):
                    h_id = g2 * 2 + hl
                    kcap = kc[:, hl * T:(hl + 1) * T]
                    qcap = qc[:, hl * T:(hl + 1) * T]
                    rr = h_id * 32
                    tile, off = rr // P, rr % P
                    if off < 64:
                        qrap = qr[off:off + 32, tile * T:(tile + 1) * T]
                        krap = kr3[off:off + 32, :]
                    else:
                        qrap = qrx[off - 64:off - 32, tile * T:(tile + 1) * T]
                        krap = kr3[off - 64:off - 32, :]
                    heads.append(([(kcap, qcap, 128), (krap, qrap, 32)], h_id, hl * 128))
                obf = c2.sb("l2a_obf", [P, 512], BF16)

                def out_fn(h_id, ic, o_f):
                    p.copy(obf, o_f[0:P, :])
                    p.dma(odram[h_id * 128:(h_id + 1) * 128, ic * 512:(ic + 1) * 512], obf)
                attn_group(c2, heads, 128, vtok, 256, lambda ic: list(range(NT)),
                           lambda ic, jt: None, 1.0 / np.sqrt(160.0), None, out_fn, **sc)
                p.barrier()
        wo_from_dram(c, odram, woT, bo, f"l2a{uid}")
        p.barrier()


def st_l1_attn_b(ctx, ins, uid):
    p = ctx.p
    I = lambda n: ins[f"L1_attn_b_{n}"]
    wdq = hin(ctx, "l1b_wdq", I("W_DQ"))
    wdkv = hin(ctx, "l1b_wdkv", I("W_DKV"))
    wuq = hin(ctx, "l1b_wuq", I("W_UQ"))
    wuk = hin(ctx, "l1b_wuk", I("W_UK"))
    wuv = hin(ctx, "l1b_wuv", I("W_UV"))
    wqr = hin(ctx, "l1b_wqr", I("W_QR"))
    wkr = hin(ctx, "l1b_wkr", I("W_KR"))
    woT = hin(ctx, "l1b_wo", I("W_O"))
    cosE, sinE, R = rope_tables_l1b()
    cosT = hin(ctx, "l1b_cos", _tile4(cosE, 4))
    sinT = hin(ctx, "l1b_sin", _tile4(sinE, 4))
    rotT = hin(ctx, "l1b_rot", _bf(_blockdiag(R, 4).T))
    for q in range(4):
        hin(ctx, f"c_cmask{q}", causal_mask_host(q))
    odram = ctx.nc.dram_tensor(f"l1b_od{uid}", [D, T], BF16, kind="Internal").ap()
    with ExitStack() as st:
        c = ctx.scoped(st)
        layernorm(c)
        cosS = c.sb("l1b_cosS", [P, T], F32); p.dma(cosS, ctx.hbm["l1b_cos"])
        sinS = c.sb("l1b_sinS", [P, T], F32); p.dma(sinS, ctx.hbm["l1b_sin"])
        rotS = c.sb("l1b_rotS", [P, P], BF16); p.dma(rotS, ctx.hbm["l1b_rot"])
        masks = [c.sb(f"l1b_m{q}", [P, 512], F32) for q in range(4)]
        for q in range(4):
            p.dma(masks[q], ctx.hbm[f"c_cmask{q}"])
        cq = c.sb("l1b_cq", [P, 2 * T], BF16)
        ckv = c.sb("l1b_ckv", [P, 2 * T], BF16)
        kr = c.sb("l1b_kr", [32, T], BF16)
        kr2 = c.sb("l1b_kr2", [64, T], BF16)
        sf1 = c.sb("l1b_sf1", [P, 512], F32)
        sf2 = c.sb("l1b_sf2", [P, 512], F32)
        with ExitStack() as stk:
            ck = c.scoped(stk)
            wdq_s = wload(ck, stk, wdq, D, 256, f"l1b_dq{uid}")
            wdkv_s = wload(ck, stk, wdkv, D, 256, f"l1b_dkv{uid}")
            wkr_s = wload(ck, stk, wkr, D, 32, f"l1b_kr{uid}")
            krf = ck.sb("l1b_krf", [32, T], F32)
            sbf32 = ck.sb("l1b_sbf32", [32, 512], BF16)
            for (wsb, dst, dout) in ((wdq_s, cq, 256), (wdkv_s, ckv, 256)):
                linear(ck, wsb, D, dout, ctx.xn_bf, dst, psA=c.pp[0], psB=c.pp[1])
            for cc in range(TC):
                ps = c.pp[0]
                for kt in range(ND):
                    p.mm(ps[0:32, :], wkr_s[:, kt * 32: kt * 32 + 32],
                         ctx.xn_bf[:, kt * T + cc * 512: kt * T + (cc + 1) * 512],
                         start=(kt == 0), stop=(kt == ND - 1))
                p.copy(krf[:, cc * 512:(cc + 1) * 512], ps[0:32, :], engine="scalar")
            rope_apply2(ck, krf, 32, rotS, cosS, sinS, kr, c.pp[0], sbf32, sf1, sf2)
            for b in range(2):
                p.dma(kr2[b * 32:(b + 1) * 32, :], kr)
            p.barrier()

        def jts_fn(ic):
            return list(range(ic * 4 + 4))

        def mask_fn(ic, jt):
            q = jt - ic * 4
            return masks[q] if q >= 0 else None
        for g4 in range(4):
            with ExitStack() as st2:
                c2 = c.scoped(st2)
                wuq_s = wload(c2, st2, wuq[:, g4 * 256:(g4 + 1) * 256], 256, 256, f"l1b_uq{g4}{uid}")
                wuk_s = wload(c2, st2, wuk[:, g4 * 256:(g4 + 1) * 256], 256, 256, f"l1b_uk{g4}{uid}")
                wuv_s = wload(c2, st2, wuv[:, g4 * 256:(g4 + 1) * 256], 256, 256, f"l1b_uv{g4}{uid}")
                wqr_s = wload(c2, st2, wqr[:, g4 * P:(g4 + 1) * P], 256, P, f"l1b_qr{g4}{uid}")
                qrr = c2.sb("l1b_qrr", [P, T], BF16)
                qr2 = c2.sb("l1b_qr2", [64, T], BF16)
                with ExitStack() as str_:
                    cr = c2.scoped(str_)
                    qrf = cr.sb("l1b_qrf", [P, T], F32)
                    sbf = cr.sb("l1b_sbf", [P, 512], BF16)
                    for cc in range(TC):
                        ps = c2.pp[0]
                        for kt in range(2):
                            p.mm(ps, wqr_s[:, kt * P:(kt + 1) * P],
                                 cq[:, kt * T + cc * 512: kt * T + (cc + 1) * 512],
                                 start=(kt == 0), stop=(kt == 1))
                        p.copy(qrf[:, cc * 512:(cc + 1) * 512], ps, engine="scalar")
                    rope_apply2(cr, qrf, P, rotS, cosS, sinS, qrr, c2.pp[0], sbf, sf1, sf2)
                    p.dma(qr2, qrr[64:128, :])
                    p.barrier()
                qc = c2.sb("l1b_qc", [P, 2 * T], BF16)
                kc = c2.sb("l1b_kc", [P, 2 * T], BF16)
                vtok = c2.sb("l1b_vt", [P, NT * 256], BF16)
                linear(c2, wuq_s, 256, 256, cq, qc, psA=c2.pp[0], psB=c2.pp[1])
                linear(c2, wuk_s, 256, 256, ckv, kc, psA=c2.pp[0], psB=c2.pp[1])
                for tt in range(NT):
                    ps = c2.pp[0] if tt % 2 == 0 else c2.pp[1]
                    for kt in range(2):
                        p.mm(ps[:, 0:256], ckv[:, kt * T + tt * P: kt * T + (tt + 1) * P],
                             wuv_s[:, kt * 256:(kt + 1) * 256],
                             start=(kt == 0), stop=(kt == 1))
                    p.copy(vtok[:, tt * 256:(tt + 1) * 256], ps[:, 0:256], engine="scalar")
                sc = make_attn_scratch(c2, f"l1b{g4}")
                heads = []
                for hl in range(4):
                    h_id = g4 * 4 + hl
                    r0 = (hl * 64) % P
                    tq = (hl * 64) // P
                    kcap = kc[r0:r0 + 64, tq * T:(tq + 1) * T]
                    qcap = qc[r0:r0 + 64, tq * T:(tq + 1) * T]
                    r0r = hl * 32
                    if r0r < 64:
                        qrap = qrr[r0r:r0r + 32, :]
                        krap = kr2[r0r:r0r + 32, :]
                    else:
                        qrap = qr2[r0r - 64:r0r - 32, :]
                        krap = kr2[r0r - 64:r0r - 32, :]
                    heads.append(([(kcap, qcap, 64), (krap, qrap, 32)], h_id, hl * 64))
                obf = c2.sb("l1b_obf", [64, 512], BF16)

                def out_fn(h_id, ic, o_f):
                    p.copy(obf, o_f[0:64, :])
                    p.dma(odram[h_id * 64:(h_id + 1) * 64, ic * 512:(ic + 1) * 512], obf)
                attn_group(c2, heads, 64, vtok, 256, jts_fn, mask_fn,
                           1.0 / np.sqrt(96.0), 80.0, out_fn, **sc)
                p.barrier()
        wo_from_dram(c, odram, woT, None, f"l1b{uid}")
        p.barrier()




def st_l0_attn_b(ctx, ins, uid):
    p = ctx.p
    wqkv = hin(ctx, "l0b_wqkv", ins["L0_attn_b_Wqkv"])
    woT = hin(ctx, "l0b_wo", ins["L0_attn_b_Wout"])
    cosE, sinE, R = rope_tables_l0b()
    cosT = hin(ctx, "l0b_cos", _tile4(cosE, 2))
    sinT = hin(ctx, "l0b_sin", _tile4(sinE, 2))
    rotT = hin(ctx, "l0b_rot", _bf(_blockdiag(R, 2).T))
    for q in range(-4, 4):
        hin(ctx, f"c_wmask{q}", window_mask_host(q))
    odram = ctx.nc.dram_tensor(f"l0b_od{uid}", [D, T], BF16, kind="Internal").ap()
    with ExitStack() as st:
        c = ctx.scoped(st)
        layernorm(c)
        cosS = c.sb("l0b_cosS", [P, T], F32); p.dma(cosS, ctx.hbm["l0b_cos"])
        sinS = c.sb("l0b_sinS", [P, T], F32); p.dma(sinS, ctx.hbm["l0b_sin"])
        rotS = c.sb("l0b_rotS", [P, P], BF16); p.dma(rotS, ctx.hbm["l0b_rot"])
        masks = {}
        for q in range(-4, 4):
            masks[q] = c.sb(f"l0b_m{q}", [P, 512], F32)
            p.dma(masks[q], ctx.hbm[f"c_wmask{q}"])

        def jts_fn(ic):
            return list(range(max(0, ic * 4 - 4), ic * 4 + 4))

        def mask_fn(ic, jt):
            return masks[jt - ic * 4]
        for g4 in range(4):
            with ExitStack() as st2:
                c2 = c.scoped(st2)
                col0 = g4 * 256
                wq = wload(c2, st2, wqkv[:, col0:col0 + 256], D, 256, f"l0bq{g4}{uid}")
                wk = wload(c2, st2, wqkv[:, 1024 + col0:1024 + col0 + 256], D, 256, f"l0bk{g4}{uid}")
                wv = wload(c2, st2, wqkv[:, 2048 + col0:2048 + col0 + 256], D, 256, f"l0bv{g4}{uid}")
                qf = c2.sb("l0b_qf", [P, T], F32)
                sbf = c2.sb("l0b_sbf", [P, 512], BF16)
                sf1 = c2.sb("l0b_sf1", [P, 512], F32)
                sf2 = c2.sb("l0b_sf2", [P, 512], F32)
                qro = c2.sb("l0b_qro", [P, 2 * T], BF16)
                kro = c2.sb("l0b_kro", [P, 2 * T], BF16)
                vtok = c2.sb("l0b_vtok", [P, NT * 256], BF16)
                for gg in range(2):
                    for (w, dst) in ((wq, qro), (wk, kro)):
                        proj_fm_group(c2, w, None, 0, qf, wst=256, wcol0=gg * P)
                        rope_apply2(c2, qf, P, rotS, cosS, sinS,
                                    dst[:, gg * T:(gg + 1) * T], c2.pp[0], sbf, sf1, sf2)
                for tt in range(NT):
                    ps = c2.pp[0] if tt % 2 == 0 else c2.pp[1]
                    for kt in range(ND):
                        p.mm(ps[:, 0:256], ctx.xn_bf[:, kt * T + tt * P: kt * T + (tt + 1) * P],
                             wv[:, kt * 256:(kt + 1) * 256],
                             start=(kt == 0), stop=(kt == ND - 1))
                    p.copy(vtok[:, tt * 256:(tt + 1) * 256], ps[:, 0:256], engine="scalar")
                sc = make_attn_scratch(c2, f"l0b{g4}")
                heads = []
                for hl in range(4):
                    tq = (hl * 64) // P
                    r0 = (hl * 64) % P
                    kap = kro[r0:r0 + 64, tq * T:(tq + 1) * T]
                    qap = qro[r0:r0 + 64, tq * T:(tq + 1) * T]
                    heads.append(([(kap, qap, 64)], g4 * 4 + hl, hl * 64))
                obf = c2.sb("l0b_obf", [64, 512], BF16)

                def out_fn(h_id, ic, o_f):
                    p.copy(obf, o_f[0:64, :])
                    p.dma(odram[h_id * 64:(h_id + 1) * 64, ic * 512:(ic + 1) * 512], obf)
                attn_group(c2, heads, 64, vtok, 256, jts_fn, mask_fn,
                           0.125, None, out_fn, **sc)
                p.barrier()
        wo_from_dram(c, odram, woT, None, f"l0b{uid}")
        p.barrier()


def st_l3_attn_a(ctx, ins, uid):
    p = ctx.p
    Wqk = ins["L3_attn_a_W_qk"].reshape(16, 64, 2, D)
    bqk = ins["L3_attn_a_b_qk"].reshape(16, 64, 2)
    wqT = hin(ctx, "l3a_wq", np.ascontiguousarray(Wqk[:, :, 0, :].reshape(D, D).T))
    wkT = hin(ctx, "l3a_wk", np.ascontiguousarray(Wqk[:, :, 1, :].reshape(D, D).T))
    with ExitStack() as st:
        c = ctx.scoped(st)
        layernorm(c)
        bq = load_pp(c, "l3a_bq", np.ascontiguousarray(bqk[:, :, 0].reshape(D)))
        bk = load_pp(c, "l3a_bk", np.ascontiguousarray(bqk[:, :, 1].reshape(D)))
        s_all = c.sb("l3a_sall", [P, NT * 512], F32)
        mrow = c.sb("l3a_mrow", [1, 512], F32)
        mbc = c.sb("l3a_mbc", [P, 512], F32)
        pb = c.sb("l3a_pb", [P, 512], BF16)
        z1 = c.sb("l3a_z1", [P, 512], F32)
        z2 = c.sb("l3a_z2", [P, 512], F32)
        of = c.sb("l3a_of", [P, 512], F32)
        CP = 1.0507009873554805 * (2048.0 ** -0.5)
        CN = 1.0507009873554805 * 1.6732632423543772 * (2048.0 ** -0.5)
        for hh in range(2):
            with ExitStack() as st2:
                c2 = c.scoped(st2)
                qb = c2.sb("l3a_qb", [P, 4 * T], BF16)
                kb = c2.sb("l3a_kb", [P, 4 * T], BF16)
                vtok = c2.sb("l3a_vt", [P, NT * 512], BF16)
                for (wT, bpp, dst) in ((wqT, bq, qb), (wkT, bk, kb)):
                    with ExitStack() as stw:
                        cw = c2.scoped(stw)
                        w = wload(cw, stw, wT[:, hh * 512:(hh + 1) * 512], D, 512,
                                  f"l3aw{hh}{uid}")
                        for mt in range(4):
                            gmt = hh * 4 + mt
                            for cc in range(TC):
                                ps = c2.pp[0] if cc % 2 == 0 else c2.pp[1]
                                for kt in range(ND):
                                    p.mm(ps, w[:, kt * 512 + mt * P: kt * 512 + (mt + 1) * P],
                                         ctx.xn_bf[:, kt * T + cc * 512: kt * T + (cc + 1) * 512],
                                         start=(kt == 0), stop=(kt == ND - 1))
                                p.act(dst[:, mt * T + cc * 512: mt * T + (cc + 1) * 512],
                                      ps, AF.Identity, bias=bpp[:, gmt:gmt + 1])
                        p.barrier()
                for tt in range(NT):
                    for dt in range(4):
                        gdt = hh * 4 + dt
                        psb = ctx.ppb[:, 0:P]
                        p.transpose(psb, ctx.xn_bf[:, gdt * T + tt * P: gdt * T + (tt + 1) * P],
                                    ctx.ident_b)
                        p.copy(vtok[:, tt * 512 + dt * P: tt * 512 + (dt + 1) * P], psb,
                               engine="scalar")
                for hl in range(8):
                    h_id = hh * 8 + hl
                    r0 = (hl * 64) % P
                    tq = (hl * 64) // P
                    kap = kb[r0:r0 + 64, tq * T:(tq + 1) * T]
                    qap = qb[r0:r0 + 64, tq * T:(tq + 1) * T]
                    for ic in range(TC):
                        psS = c.pp[2]
                        psD = c.pp[4][0:1, :]
                        psO = c.pp[3][r0:r0 + 64, :]
                        for jt in range(NT):
                            p.mm(psS, kap[0:64, jt * P:(jt + 1) * P],
                                 qap[0:64, ic * 512:(ic + 1) * 512], start=True, stop=True)
                            p.copy(s_all[:, jt * 512:(jt + 1) * 512], psS, engine="scalar")
                        for jt in range(NT):
                            p.mm(psD, ctx.ones_col, s_all[:, jt * 512:(jt + 1) * 512],
                                 start=(jt == 0), stop=(jt == NT - 1))
                        p.ts(mrow, psD, 0.125 / T, ALU.mult)
                        p.mm(c.pp[5], ctx.ones_row, mrow, start=True, stop=True)
                        p.copy(mbc, c.pp[5], engine="scalar")
                        for jt in range(NT):
                            sa = s_all[:, jt * 512:(jt + 1) * 512]
                            p.stt(z1, sa, 0.125, mbc, ALU.mult, ALU.subtract)
                            p.act(z2, z1, AF.Exp)
                            p.ts(z2, z2, -1.0, ALU.add, CN, ALU.mult, engine="gpsimd")
                            p.ts(z2, z2, 0.0, ALU.min, engine="gpsimd")
                            p.act(z1, z1, AF.Relu, scale=CP)
                            p.tt(pb, z1, z2, ALU.add)
                            p.mm(psO,
                                 vtok[:, jt * 512 + hl * 64: jt * 512 + hl * 64 + 64],
                                 pb, start=(jt == 0), stop=(jt == NT - 1))
                        p.copy(of[r0:r0 + 64, :], psO, engine="scalar")
                        xr = ctx.xres[(h_id * 64) % P:(h_id * 64) % P + 64,
                                      ((h_id * 64) // P) * T + ic * 512:
                                      ((h_id * 64) // P) * T + (ic + 1) * 512]
                        p.tt(xr, xr, of[r0:r0 + 64, :], ALU.add)
                p.barrier()
        p.barrier()


def st_l0_attn_a(ctx, ins, uid):
    p = ctx.p
    I = lambda n: ins[f"L0_attn_a_{n}"]
    wq = hin(ctx, "l0a_wq", I("Wq"))
    wk = hin(ctx, "l0a_wk", I("Wk"))
    wv = hin(ctx, "l0a_wv", I("Wv"))
    wo = hin(ctx, "l0a_wo", I("Wo"))
    with ExitStack() as st:
        c = ctx.scoped(st)
        layernorm(c)
        zb = c.sb("l0a_zb", [P, ND * T], BF16)
        st_z = ExitStack()
        c = c.scoped(st_z)
        vecs = {}
        for nm in ("q", "k", "v"):
            g = load_pp(c, f"l0a_g{nm}", I(f"g{nm}"))
            b = load_pp(c, f"l0a_b{nm}", I(f"b{nm}"))
            m = load_pp(c, f"l0a_m{nm}", I(f"m{nm}"))
            v = load_pp(c, f"l0a_v{nm}", I(f"v{nm}"))
            s4 = c.sb(f"l0a_s4{nm}", [P, ND], F32)
            b4 = c.sb(f"l0a_b4{nm}", [P, ND], F32)
            p.act(s4, v, AF.Ln, bias=1e-5)
            p.act(s4, s4, AF.Exp, scale=-0.5)
            p.tt(s4, s4, g, ALU.mult)
            p.tt(b4, m, s4, ALU.mult)
            p.tt(b4, b, b4, ALU.subtract)
            p.ts(b4, b4, 4.0, ALU.mult, -4.0, ALU.add)
            p.ts(s4, s4, 4.0, ALU.mult)
            vecs[nm] = (s4, b4)
        tQr = c.sb("l0a_tq", [1, T], F32)
        tKr = c.sb("l0a_tk", [1, T], F32)
        vtok = c.sb("l0a_vt", [P, NT * 1024], BF16)
        qtile = c.sb("l0a_qt", [P, 512], BF16)
        for nm, trow in (("q", tQr), ("k", tKr), ("v", None)):
            with ExitStack() as stw:
                cw = c.scoped(stw)
                w = wload(cw, stw, {"q": wq, "k": wk, "v": wv}[nm], D, D, f"l0a{nm}{uid}")
                s4, b4 = vecs[nm]
                psr = [c.pp[2][0:1, :], c.pp[3][0:1, :], c.pp[4][0:1, :], c.pp[5][0:1, :]]
                for mt in range(ND):
                    for cc in range(TC):
                        ps = c.pp[0] if (mt * TC + cc) % 2 == 0 else c.pp[1]
                        for kt in range(ND):
                            p.mm(ps, w[:, kt * D + mt * P: kt * D + (mt + 1) * P],
                                 ctx.xn_bf[:, kt * T + cc * 512: kt * T + (cc + 1) * 512],
                                 start=(kt == 0), stop=(kt == ND - 1))
                        p.act(qtile, ps, AF.Sigmoid, bias=b4[:, mt:mt + 1],
                              scale=s4[:, mt:mt + 1])
                        if trow is not None:
                            p.mm(psr[cc], ctx.ones_col_bf, qtile,
                                 start=(mt == 0), stop=(mt == ND - 1))
                        else:
                            psb = ctx.ppb[:, 0:P]
                            for q4 in range(4):
                                p.transpose(psb, qtile[:, q4 * P:(q4 + 1) * P], ctx.ident_b)
                                p.copy(vtok[:, (cc * 4 + q4) * 1024 + mt * P:
                                            (cc * 4 + q4) * 1024 + (mt + 1) * P],
                                       psb, engine="scalar")
                if trow is not None:
                    for cc in range(TC):
                        p.ts(trow[:, cc * 512:(cc + 1) * 512], psr[cc],
                             -1.0 / D, ALU.mult, 1.0, ALU.add)
                p.barrier()
        dsc = ctx.nc.dram_tensor(f"l0a_dsc{uid}", [T], F32, kind="Internal").ap()
        p.dma(dsc.rearrange("(a b) -> a b", a=1), tKr)
        tkst = c.sb("l0a_tkst", [NT, P], F32)
        p.dma(tkst, dsc.rearrange("(k p) -> k p", p=P))
        tkc = c.sb("l0a_tkc", [P, NT], F32)
        p.transpose(c.pp[5][:, 0:NT], tkst, ctx.ident_f[0:NT, 0:NT])
        p.copy(tkc, c.pp[5][:, 0:NT], engine="scalar")
        tqb = c.sb("l0a_tqb", [P, T], F32)
        bcast_row(c, tqb, tQr)
        d1 = c.sb("l0a_d1", [P, 512], F32)
        d2 = c.sb("l0a_d2", [P, 512], F32)
        d3 = c.sb("l0a_d3", [P, 512], F32)
        abf = c.sb("l0a_abf", [P, 512], BF16)
        for ic in range(TC):
            for mh in range(2):
                for jt in range(NT):
                    p.ts(d1, tqb[:, ic * 512:(ic + 1) * 512], tkc[:, jt:jt + 1], ALU.subtract)
                    p.act(d2, d1, AF.Sign)
                    p.act(d3, d1, AF.Abs)
                    p.act(d3, d3, AF.Exp, scale=-2.0)
                    p.stt(d2, d2, -0.9, d3, ALU.mult, ALU.mult, engine="gpsimd")
                    p.ts(abf, d2, 0.9, ALU.add, engine="gpsimd")
                    for dt in range(4):
                        gdt = mh * 4 + dt
                        p.mm(c.pp[dt],
                             vtok[:, jt * 1024 + gdt * P: jt * 1024 + (gdt + 1) * P],
                             abf, start=(jt == 0), stop=(jt == NT - 1))
                for dt in range(4):
                    gdt = mh * 4 + dt
                    p.copy(zb[:, gdt * T + ic * 512: gdt * T + (ic + 1) * 512],
                           c.pp[dt], engine="scalar")
        p.barrier()
        st_z.close()
        c = ctx.scoped(st)
        bo = load_pp(c, "l0a_bo", I("bo"))
        with ExitStack() as stw:
            cw = c.scoped(stw)
            wos = wload(cw, stw, wo, D, D, f"l0ao{uid}")
            linear(cw, wos, D, D, zb, None, bias_pp=bo, accum_res=True,
                   psA=c.pp[0], psB=c.pp[1])
            p.barrier()
        p.barrier()




def _gates_topk(c, ctx, wgT_hbm, E, K_sel, uid, bias_row_np=None, clipmin=None):
    p = c.p
    wg = wload(c, c.stack, wgT_hbm, D, E, f"gw{uid}")
    gatesT = c.sb(f"gatesT{uid}", [E, T], F32)
    sc = c.sb(f"gsc{uid}", [P, E], F32)
    prob = c.sb(f"gpr{uid}", [P, E], F32)
    corr = c.sb(f"gco{uid}", [P, E], F32)
    m8 = c.sb(f"gm8{uid}", [P, 8], F32)
    r1 = c.sb(f"gr1{uid}", [P, 1], F32)
    bbc = None
    if bias_row_np is not None:
        br = hin(ctx, f"gbias{uid}", np.ascontiguousarray(bias_row_np[None, :].astype(np.float32)))
        brs = c.sb(f"gbr{uid}", [1, E], F32)
        p.dma(brs, br)
        bbc = c.sb(f"gbb{uid}", [P, E], F32)
        ps = c.pp[5][:, 0:E]
        p.mm(ps, ctx.ones_row, brs, start=True, stop=True)
        p.copy(bbc, ps, engine="scalar")
    for tt in range(NT):
        ps = c.pp[0][:, 0:E]
        for kt in range(ND):
            p.mm(ps, ctx.xn_bf[:, kt * T + tt * P: kt * T + (tt + 1) * P],
                 wg[:, kt * E:(kt + 1) * E], start=(kt == 0), stop=(kt == ND - 1))
        p.red(r1, ps, ALU.max, negate=True)
        p.act(sc, ps, AF.Exp, bias=r1)
        p.red(r1, sc, ALU.add)
        p.recip(r1, r1)
        p.ts(prob, sc, r1, ALU.mult)
        if bbc is not None:
            p.tt(corr, prob, bbc, ALU.add)
        else:
            p.copy(corr, prob)
        p.max8(m8, corr)
        p.ts(sc, corr, m8[:, K_sel - 1:K_sel], ALU.is_ge)
        p.tt(prob, prob, sc, ALU.mult)
        p.red(r1, prob, ALU.add)
        if clipmin is not None:
            p.ts(r1, r1, clipmin, ALU.max)
        p.recip(r1, r1)
        p.ts(prob, prob, r1, ALU.mult)
        ps2 = c.pp[1][0:E, 0:P]
        p.transpose(ps2, prob, ctx.ident_f)
        p.copy(gatesT[:, tt * P:(tt + 1) * P], ps2, engine="scalar")
    gdram = c.nc.dram_tensor(f"gdram{uid}", [E, T], F32, kind="Internal").ap()
    p.dma(gdram, gatesT)
    return gatesT, gdram


def st_l0_ffn(ctx, ins, uid):
    p = ctx.p
    E, HID, R = 8, 2048, 16
    wgT = hin(ctx, "l0f_wg", np.ascontiguousarray(ins["L0_ffn_Wg"].T))
    for e in range(E):
        hin(ctx, f"l0f_w1T{e}", np.ascontiguousarray(ins["L0_ffn_W1"][e].T))
        hin(ctx, f"l0f_w3T{e}", np.ascontiguousarray(ins["L0_ffn_W3"][e].T))
        hin(ctx, f"l0f_w2T{e}", np.ascontiguousarray(ins["L0_ffn_W2"][e].T))
        hin(ctx, f"l0f_a1T{e}", np.ascontiguousarray(ins["L0_ffn_A1"][e].T))
        hin(ctx, f"l0f_a3T{e}", np.ascontiguousarray(ins["L0_ffn_A3"][e].T))
        hin(ctx, f"l0f_a2T{e}", np.ascontiguousarray(ins["L0_ffn_A2"][e].T))
        hin(ctx, f"l0f_b1s{e}", np.ascontiguousarray(ins["L0_ffn_B1"][e].T))
        hin(ctx, f"l0f_b3s{e}", np.ascontiguousarray(ins["L0_ffn_B3"][e].T))
        hin(ctx, f"l0f_b2s{e}", np.ascontiguousarray(ins["L0_ffn_B2"][e].T))
        hin(ctx, f"l0f_bias1pp{e}", pp_host(ins["L0_ffn_b1"][e]))
        hin(ctx, f"l0f_bias3pp{e}", pp_host(ins["L0_ffn_b3"][e]))
        hin(ctx, f"l0f_bias2pp{e}", pp_host(ins["L0_ffn_b2"][e]))
    with ExitStack() as st:
        c = ctx.scoped(st)
        layernorm(c)
        gatesT, gdram = _gates_topk(c, ctx, wgT, E, 2, f"l0f{uid}")
        grow = c.sb("l0f_grow", [1, 512], F32)
        yacc = c.sb("l0f_yacc", [P, ND * 512], F32)
        gsb = c.sb("l0f_g", [P, NT * 512], BF16)
        h3q = c.sb("l0f_h3", [P, 4 * 512], BF16)
        terb = c.sb("l0f_ter", [16, 512], BF16)
        ter2 = c.sb("l0f_ter2", [16, 512], BF16)
        gbc = c.sb("l0f_gbc", [P, 512], F32)
        scr = c.sb("l0f_scr", [P, 512], F32)
        for tq in range(TC):
            toff = tq * 512
            for e in range(E):
                with ExitStack() as st2:
                    c2 = c.scoped(st2)
                    a1 = wload(c2, st2, ctx.hbm[f"l0f_a1T{e}"], D, 16, f"a1_{e}{uid}")
                    a3 = wload(c2, st2, ctx.hbm[f"l0f_a3T{e}"], D, 16, f"a3_{e}{uid}")
                    a2 = wload(c2, st2, ctx.hbm[f"l0f_a2T{e}"], HID, 16, f"a2_{e}{uid}")
                    b1s = c2.sb("b1s", [16, HID], BF16)
                    b3s = c2.sb("b3s", [16, HID], BF16)
                    b2s = c2.sb("b2s", [16, D], BF16)
                    for (bb, hb, w_) in ((b1s, f"l0f_b1s{e}", HID), (b3s, f"l0f_b3s{e}", HID),
                                         (b2s, f"l0f_b2s{e}", D)):
                        p.dma(ctx.wstg[0:16, 0:w_], ctx.hbm[hb])
                        p.copy(bb, ctx.wstg[0:16, 0:w_], engine="gpsimd")
                    bias1 = c2.sb("bi1", [P, 16], F32); p.dma(bias1, ctx.hbm[f"l0f_bias1pp{e}"])
                    bias3 = c2.sb("bi3", [P, 16], F32); p.dma(bias3, ctx.hbm[f"l0f_bias3pp{e}"])
                    bias2 = c2.sb("bi2", [P, ND], F32); p.dma(bias2, ctx.hbm[f"l0f_bias2pp{e}"])
                    for (aa, tb) in ((a1, terb), (a3, ter2)):
                        ps = c2.pp[0][0:16, :]
                        for kt in range(ND):
                            p.mm(ps, aa[:, kt * 16: kt * 16 + 16],
                                 ctx.xn_bf[:, kt * T + toff: kt * T + toff + 512],
                                 start=(kt == 0), stop=(kt == ND - 1))
                        p.copy(tb, ps, engine="scalar")
                    p.dma(grow, gdram[e:e + 1, toff:toff + 512])
                    p.mm(c2.pp[5], ctx.ones_row, grow, start=True, stop=True)
                    p.copy(gbc, c2.pp[5], engine="scalar")
                    for hc in range(4):
                      with ExitStack() as sth:
                        ch = c2.scoped(sth)
                        w1 = wload(ch, sth, ctx.hbm[f"l0f_w1T{e}"][:, hc * 512:(hc + 1) * 512],
                                   D, 512, f"w1_{e}_{hc}_{tq}{uid}", priv=True)
                        for mt in range(4):
                            ps = c2.pp[0] if mt % 2 == 0 else c2.pp[1]
                            for kt in range(ND):
                                p.mm(ps, w1[:, kt * 512 + mt * P: kt * 512 + (mt + 1) * P],
                                     ctx.xn_bf[:, kt * T + toff: kt * T + toff + 512],
                                     start=(kt == 0), stop=False)
                            p.mm(ps, b1s[:, hc * 512 + mt * P: hc * 512 + (mt + 1) * P],
                                 terb, start=False, stop=True)
                            p.act(gsb[:, (hc * 4 + mt) * 512:(hc * 4 + mt + 1) * 512], ps,
                                  AF.Silu, bias=bias1[:, hc * 4 + mt: hc * 4 + mt + 1])
                        w3 = wload(ch, sth, ctx.hbm[f"l0f_w3T{e}"][:, hc * 512:(hc + 1) * 512],
                                   D, 512, f"w3_{e}_{hc}_{tq}{uid}", priv=True)
                        for mt in range(4):
                            ps = c2.pp[0] if mt % 2 == 0 else c2.pp[1]
                            for kt in range(ND):
                                p.mm(ps, w3[:, kt * 512 + mt * P: kt * 512 + (mt + 1) * P],
                                     ctx.xn_bf[:, kt * T + toff: kt * T + toff + 512],
                                     start=(kt == 0), stop=False)
                            p.mm(ps, b3s[:, hc * 512 + mt * P: hc * 512 + (mt + 1) * P],
                                 ter2, start=False, stop=True)
                            p.act(h3q[:, mt * 512:(mt + 1) * 512], ps, AF.Identity,
                                  bias=bias3[:, hc * 4 + mt: hc * 4 + mt + 1])
                        for mt in range(4):
                            gslice = gsb[:, (hc * 4 + mt) * 512:(hc * 4 + mt + 1) * 512]
                            p.tt(gslice, gslice, h3q[:, mt * 512:(mt + 1) * 512], ALU.mult,
                                 engine="vector" if mt % 2 else "gpsimd")
                        p.barrier()
                    ps = c2.pp[0][0:16, :]
                    for kt in range(NT):
                        p.mm(ps, a2[:, kt * 16: kt * 16 + 16],
                             gsb[:, kt * 512:(kt + 1) * 512],
                             start=(kt == 0), stop=(kt == NT - 1))
                    p.copy(ter2, ps, engine="scalar")
                    w2 = wload(c2, st2, ctx.hbm[f"l0f_w2T{e}"], HID, D, f"w2_{e}_{tq}{uid}")
                    for mt in range(ND):
                        ps = c2.pp[0] if mt % 2 == 0 else c2.pp[1]
                        for kt in range(NT):
                            p.mm(ps, w2[:, kt * D + mt * P: kt * D + (mt + 1) * P],
                                 gsb[:, kt * 512:(kt + 1) * 512],
                                 start=(kt == 0), stop=False)
                        p.mm(ps, b2s[:, mt * P:(mt + 1) * P], ter2, start=False, stop=True)
                        p.act(scr, ps, AF.Identity, bias=bias2[:, mt:mt + 1])
                        ya = yacc[:, mt * 512:(mt + 1) * 512]
                        if e == 0:
                            p.tt(ya, scr, gbc, ALU.mult,
                                 engine="vector" if mt % 2 else "gpsimd")
                        else:
                            p.tt(scr, scr, gbc, ALU.mult,
                                 engine="vector" if mt % 2 else "gpsimd")
                            p.tt(ya, ya, scr, ALU.add,
                                 engine="gpsimd" if mt % 2 else "vector")
                    p.barrier()
            for mt in range(ND):
                xr = ctx.xres[:, mt * T + toff: mt * T + toff + 512]
                p.tt(xr, xr, yacc[:, mt * 512:(mt + 1) * 512], ALU.add,
                     engine="vector" if mt % 2 else "gpsimd")
        p.barrier()


def st_l1_ffn(ctx, ins, uid):
    p = ctx.p
    E, I_ = 32, 512
    rwT = hin(ctx, "l1f_rw", np.ascontiguousarray(ins["L1_ffn_router_weight"].T))
    sgT = hin(ctx, "l1f_sg", np.ascontiguousarray(ins["L1_ffn_shared_gate_w"].T))
    suT = hin(ctx, "l1f_su", np.ascontiguousarray(ins["L1_ffn_shared_up_w"].T))
    sdT = hin(ctx, "l1f_sd", np.ascontiguousarray(ins["L1_ffn_shared_down_w"].T))
    for e in range(E):
        hin(ctx, f"l1f_gupT{e}", np.ascontiguousarray(ins["L1_ffn_gate_up_proj"][e].T))
        hin(ctx, f"l1f_dT{e}", np.ascontiguousarray(ins["L1_ffn_down_proj"][e].T))
    with ExitStack() as st:
        c = ctx.scoped(st)
        layernorm(c)
        gatesT, gdram = _gates_topk(c, ctx, rwT, E, 6, f"l1f{uid}",
                             bias_row_np=ins["L1_ffn_e_score_bias"], clipmin=1e-12)
        grow = c.sb("l1f_grow", [1, 512], F32)
        with ExitStack() as stsh:
            csh = c.scoped(stsh)
            gsh = csh.sb("l1f_gsh", [P, ND * T], BF16)
            with ExitStack() as st2:
                c2 = csh.scoped(st2)
                sg = wload(c2, st2, sgT, D, D, f"sg{uid}")
                linear(c2, sg, D, D, ctx.xn_bf, gsh, func=AF.Silu, psA=c2.pp[0], psB=c2.pp[1])
                p.barrier()
            with ExitStack() as st2:
                c2 = csh.scoped(st2)
                su = wload(c2, st2, suT, D, D, f"su{uid}")
                ush = c2.sb("l1f_ush", [P, T], F32)
                for mt in range(ND):
                    for cc in range(TC):
                        ps = c2.pp[0] if cc % 2 == 0 else c2.pp[1]
                        for kt in range(ND):
                            p.mm(ps, su[:, kt * D + mt * P: kt * D + (mt + 1) * P],
                                 ctx.xn_bf[:, kt * T + cc * 512: kt * T + (cc + 1) * 512],
                                 start=(kt == 0), stop=(kt == ND - 1))
                        p.copy(ush[:, cc * 512:(cc + 1) * 512], ps, engine="scalar")
                    gs = gsh[:, mt * T:(mt + 1) * T]
                    p.tt(gs, gs, ush, ALU.mult)
                p.barrier()
            with ExitStack() as st2:
                c2 = csh.scoped(st2)
                sd = wload(c2, st2, sdT, D, D, f"sd{uid}")
                linear(c2, sd, D, D, gsh, None, accum_res=True, psA=c2.pp[0], psB=c2.pp[1])
                p.barrier()
            p.barrier()
        yacc = c.sb("l1f_yacc", [P, ND * 1024], F32)
        hmb = c.sb("l1f_hm", [P, 4 * 1024], BF16)
        usb = c.sb("l1f_us", [P, 4 * 1024], BF16)
        gbc = c.sb("l1f_gbc", [P, 1024], F32)
        scr = c.sb("l1f_scr", [P, 512], F32)
        for th in range(2):
            toff = th * 1024
            for e in range(E):
                with ExitStack() as st2:
                    c2 = c.scoped(st2)
                    gup = wload(c2, st2, ctx.hbm[f"l1f_gupT{e}"], D, 1024, f"gup{e}_{th}{uid}", priv=True)
                    dwn = wload(c2, st2, ctx.hbm[f"l1f_dT{e}"], I_, 1024, f"dwn{e}_{th}{uid}")
                    for cc in range(2):
                        p.dma(grow, gdram[e:e + 1, toff + cc * 512: toff + (cc + 1) * 512])
                        p.mm(c2.pp[5], ctx.ones_row, grow, start=True, stop=True)
                        p.copy(gbc[:, cc * 512:(cc + 1) * 512], c2.pp[5], engine="scalar")
                    for mt in range(8):
                        for cc in range(2):
                            ps = c2.pp[0] if (mt * 2 + cc) % 2 == 0 else c2.pp[1]
                            for kt in range(ND):
                                p.mm(ps, gup[:, kt * 1024 + mt * P: kt * 1024 + (mt + 1) * P],
                                     ctx.xn_bf[:, kt * T + toff + cc * 512: kt * T + toff + (cc + 1) * 512],
                                     start=(kt == 0), stop=(kt == ND - 1))
                            if mt < 4:
                                p.act(hmb[:, mt * 1024 + cc * 512: mt * 1024 + (cc + 1) * 512],
                                      ps, AF.Silu)
                            else:
                                p.copy(usb[:, (mt - 4) * 1024 + cc * 512: (mt - 4) * 1024 + (cc + 1) * 512],
                                       ps, engine="scalar")
                    for mt in range(4):
                        hs = hmb[:, mt * 1024:(mt + 1) * 1024]
                        p.tt(hs, hs, usb[:, mt * 1024:(mt + 1) * 1024], ALU.mult,
                             engine="vector" if mt % 2 else "gpsimd")
                    for mt in range(ND):
                        for cc in range(2):
                            ps = c2.pp[0] if (mt * 2 + cc) % 2 == 0 else c2.pp[1]
                            for kt in range(4):
                                p.mm(ps, dwn[:, kt * 1024 + mt * P: kt * 1024 + (mt + 1) * P],
                                     hmb[:, kt * 1024 + cc * 512: kt * 1024 + (cc + 1) * 512],
                                     start=(kt == 0), stop=(kt == 3))
                            p.copy(scr, ps, engine="scalar")
                            ya = yacc[:, mt * 1024 + cc * 512: mt * 1024 + (cc + 1) * 512]
                            gb = gbc[:, cc * 512:(cc + 1) * 512]
                            if e == 0:
                                p.tt(ya, scr, gb, ALU.mult,
                                     engine="vector" if (mt + cc) % 2 else "gpsimd")
                            else:
                                p.tt(scr, scr, gb, ALU.mult,
                                     engine="vector" if (mt + cc) % 2 else "gpsimd")
                                p.tt(ya, ya, scr, ALU.add,
                                     engine="gpsimd" if (mt + cc) % 2 else "vector")
                    p.barrier()
            for mt in range(ND):
                xr = ctx.xres[:, mt * T + toff: mt * T + toff + 1024]
                p.tt(xr, xr, yacc[:, mt * 1024:(mt + 1) * 1024], ALU.add,
                     engine="vector" if mt % 2 else "gpsimd")
        p.barrier()




def st_l1_attn_a(ctx, ins, uid):
    p = ctx.p
    I = lambda n: ins[f"L1_attn_a_{n}"]
    wB = hin(ctx, "l1a_wB", np.ascontiguousarray(I("W_B").T))
    wC = hin(ctx, "l1a_wC", np.ascontiguousarray(I("W_C").T))
    wdt1 = hin(ctx, "l1a_wdt1", np.ascontiguousarray(I("W_dt1").T))
    wdt2 = hin(ctx, "l1a_wdt2", np.ascontiguousarray(I("W_dt2").T))
    logA = hin(ctx, "l1a_logA", pp_host(np.ascontiguousarray(I("log_A").T)))
    with ExitStack() as st:
        c = ctx.scoped(st)
        xn, bm, br = layernorm(c, keep_stats=True)
        bdt2 = load_pp(c, "l1a_bdt2", I("b_dt2"))
        dsk = load_pp(c, "l1a_dsk", I("D_skip"))
        acol = c.sb("l1a_acol", [P, P], F32)
        p.dma(acol, logA)
        p.act(acol, acol, AF.Exp)
        p.ts(acol, acol, -1.0, ALU.mult)
        rcol = c.sb("l1a_rcol", [P, P], F32)
        p.ts(rcol, acol, 1e-8, ALU.add)
        p.recip(rcol, rcol)
        wB_s = wload(c, st, wB, D, 16, f"l1aB{uid}")
        wC_s = wload(c, st, wC, D, 16, f"l1aC{uid}")
        wdt1_s = wload(c, st, wdt1, D, 64, f"l1adt1{uid}")
        wdt2_s = wload(c, st, wdt2, 64, 1024, f"l1adt2{uid}")
        Brow = c.sb("l1a_Brow", [16, T], F32)
        Crow = c.sb("l1a_Crow", [16, T], F32)
        bcd = ctx.nc.dram_tensor(f"l1a_bcd{uid}", [32, T], F32, kind="Internal").ap()
        brow1 = c.sb("l1a_brow1", [1, 512], F32)
        crow1 = c.sb("l1a_crow1", [1, 512], F32)
        terb = c.sb("l1a_terb", [64, T], BF16)
        for (wsb, dst, rows) in ((wB_s, Brow, 16), (wC_s, Crow, 16), (wdt1_s, terb, 64)):
            for cc in range(TC):
                ps = c.pp[0][0:rows, :]
                for kt in range(ND):
                    p.mm(ps, wsb[:, kt * rows: kt * rows + rows],
                         ctx.xn_bf[:, kt * T + cc * 512: kt * T + (cc + 1) * 512],
                         start=(kt == 0), stop=(kt == ND - 1))
                p.copy(dst[:, cc * 512:(cc + 1) * 512], ps[0:rows, :], engine="scalar")
        p.dma(bcd[0:16, :], Brow)
        p.dma(bcd[16:32, :], Crow)
        xnf = c.sb("l1a_xnf", [P, T], F32)
        delta = c.sb("l1a_delta", [P, T], F32)
        s1 = c.sb("l1a_s1", [P, T], F32)
        s2 = c.sb("l1a_s2", [P, T], F32)
        av = s1
        bv = s2
        hv = ctx.wstg
        yac = c.sb("l1a_yac", [P, T], F32)
        for dt in range(ND):
            p.tt(xnf, ctx.xres_t(dt), bm, ALU.subtract)
            p.tt(xnf, xnf, br, ALU.mult)
            for cc in range(TC):
                ps = c.pp[0] if cc % 2 == 0 else c.pp[1]
                p.mm(ps, wdt2_s[0:64, dt * P:(dt + 1) * P],
                     terb[:, cc * 512:(cc + 1) * 512], start=True, stop=True)
                p.act(delta[:, cc * 512:(cc + 1) * 512], ps, AF.Identity,
                      bias=bdt2[:, dt:dt + 1])
            softplus(c, delta, delta, s1, s2)
            for n in range(16):
                col = n * ND + dt
                p.act(av, delta, AF.Exp, scale=acol[:, col:col + 1])
                p.ts(bv, av, 1.0, ALU.subtract, rcol[:, col:col + 1], ALU.mult)
                p.tt(bv, bv, xnf, ALU.mult, engine="gpsimd")
                for cc in range(TC):
                    cs = slice(cc * 512, (cc + 1) * 512)
                    p.dma(brow1, bcd[n:n + 1, cs])
                    p.mm(c.pp[5], ctx.ones_row, brow1, start=True, stop=True)
                    p.tt(bv[:, cs], bv[:, cs], c.pp[5], ALU.mult)
                p.scan(hv, av, bv, 0.0)
                for cc in range(TC):
                    cs = slice(cc * 512, (cc + 1) * 512)
                    p.dma(crow1, bcd[16 + n:17 + n, cs])
                    p.mm(c.pp[5], ctx.ones_row, crow1, start=True, stop=True)
                    p.tt(hv[:, cs], hv[:, cs], c.pp[5], ALU.mult)
                if n == 0:
                    p.copy(yac, hv, engine="gpsimd")
                else:
                    p.tt(yac, yac, hv, ALU.add)
            p.stt(yac, xnf, dsk[:, dt:dt + 1], yac, ALU.mult, ALU.add)
            xr = ctx.xres_t(dt)
            p.tt(xr, xr, yac, ALU.add, engine="gpsimd")
        p.barrier()


def st_l3_attn_b(ctx, ins, uid):
    p = ctx.p
    I = lambda n: ins[f"L3_attn_b_{n}"]
    wIn = hin(ctx, "l3b_wIn", I("W_in"))
    wX = hin(ctx, "l3b_wX", I("W_x"))
    wOut = hin(ctx, "l3b_wOut", I("W_out"))
    convw = hin(ctx, "l3b_convw", pp_host(np.ascontiguousarray(I("conv_w").T)))
    logA = hin(ctx, "l3b_logA", pp_host(np.ascontiguousarray(I("A_log").T)))
    DI, NDI = 2048, 16
    with ExitStack() as st:
        c = ctx.scoped(st)
        layernorm(c)
        convb = load_pp(c, "l3b_convb", I("conv_b"))
        dtw = load_pp(c, "l3b_dtw", I("dt_w"))
        dtb = load_pp(c, "l3b_dtb", I("dt_b"))
        dpp = load_pp(c, "l3b_D", I("D"))
        acol = c.sb("l3b_acol", [P, 256], F32)
        p.dma(acol, logA)
        p.act(acol, acol, AF.Exp)
        p.ts(acol, acol, -1.0, ALU.mult)
        rcol = c.sb("l3b_rcol", [P, 256], F32)
        p.ts(rcol, acol, 1e-8, ALU.add)
        p.recip(rcol, rcol)
        cwc = c.sb("l3b_cwc", [P, 64], F32)
        p.dma(cwc, convw)
        xcd = ctx.nc.dram_tensor(f"l3b_xcd{uid}", [DI, T], BF16, kind="Internal").ap()
        xbstg = c.sb("l3b_xbstg", [P, T], BF16)
        ssm = c.sb("l3b_ssm", [33, T], F32)

        def xconv_tile(cc_, gdt, out_f32):
            with ExitStack() as st2:
                c2 = cc_.scoped(st2)
                wq_ = wload(c2, st2, wIn[:, gdt * P:(gdt + 1) * P], D, P, f"l3bw{gdt}{uid}x")
                pad = c2.sb("l3b_pad", [P, T + 3], F32)
                p.memset(pad[:, 0:3], 0.0)
                for cc in range(TC):
                    ps = c2.pp[0] if cc % 2 == 0 else c2.pp[1]
                    for kt in range(ND):
                        p.mm(ps, wq_[:, kt * P:(kt + 1) * P],
                             ctx.xn_bf[:, kt * T + cc * 512: kt * T + (cc + 1) * 512],
                             start=(kt == 0), stop=(kt == ND - 1))
                    p.copy(pad[:, 3 + cc * 512: 3 + (cc + 1) * 512], ps, engine="scalar")
                p.ts(out_f32, pad[:, 0:T], cwc[:, gdt:gdt + 1], ALU.mult)
                for k in range(1, 4):
                    p.stt(out_f32, pad[:, k:k + T], cwc[:, k * NDI + gdt: k * NDI + gdt + 1],
                          out_f32, ALU.mult, ALU.add, engine="gpsimd" if k % 2 else "vector")
                p.act(out_f32, out_f32, AF.Silu, bias=convb[:, gdt:gdt + 1])
                p.barrier()
        xcf = c.sb("l3b_xcf", [P, T], F32)
        for gdt in range(NDI):
            xconv_tile(c, gdt, xcf)
            p.copy(xbstg, xcf, engine="gpsimd")
            p.dma(xcd[gdt * P:(gdt + 1) * P, :], xbstg)
        with ExitStack() as st2:
            c2 = c.scoped(st2)
            wxs = wload(c2, st2, wX, DI, 33, f"l3bx{uid}")
            xk = [c2.sb(f"l3b_xk{i}", [P, 512], BF16) for i in range(2)]
            for cc in range(TC):
                ps = c2.pp[0][0:33, :]
                for kt in range(NDI):
                    xs = xk[kt % 2]
                    p.dma(xs, xcd[kt * P:(kt + 1) * P, cc * 512:(cc + 1) * 512])
                    p.mm(ps, wxs[:, kt * 33: kt * 33 + 33], xs,
                         start=(kt == 0), stop=(kt == NDI - 1))
                p.copy(ssm[:, cc * 512:(cc + 1) * 512], ps[0:33, :], engine="scalar")
            p.barrier()
        ssmd = ctx.nc.dram_tensor(f"l3b_ssmd{uid}", [33, T], F32, kind="Internal").ap()
        p.dma(ssmd, ssm)
        stB = ExitStack()
        cB = c.scoped(stB)
        delta = cB.sb("l3b_delta", [P, T], F32)
        dxc = cB.sb("l3b_dxc", [P, T], F32)
        yac = cB.sb("l3b_yac", [P, T], F32)
        s1 = cB.sb("l3b_s1", [P, 1024], F32)
        s2 = cB.sb("l3b_s2", [P, 1024], F32)
        brow1 = cB.sb("l3b_br1", [1, 512], F32)
        crow1 = cB.sb("l3b_cr1", [1, 512], F32)
        hv = ctx.wstg[:, 512:1536]
        hsave = ctx.wstg[:, 1536:1537]
        HF = 1024
        for gdt in range(NDI):
            xconv_tile(c, gdt, xcf)
            for hf in range(2):
                dsl = delta[:, hf * HF:(hf + 1) * HF]
                for q2 in range(2):
                    cs = slice(hf * HF + q2 * 512, hf * HF + q2 * 512 + 512)
                    ds2 = slice(q2 * 512, q2 * 512 + 512)
                    p.dma(brow1, ssmd[0:1, cs])
                    p.mm(c.pp[5], ctx.ones_row, brow1, start=True, stop=True)
                    p.ts(dsl[:, ds2], c.pp[5], dtw[:, gdt:gdt + 1], ALU.mult,
                         dtb[:, gdt:gdt + 1], ALU.add)
                softplus(c, dsl, dsl, s1, s2)
            p.tt(dxc, delta, xcf, ALU.mult, engine="gpsimd")
            for n in range(NDI):
                col = n * NDI + gdt
                for hf in range(2):
                    hsl = slice(hf * HF, (hf + 1) * HF)
                    p.act(s1, delta[:, hsl], AF.Exp, scale=acol[:, col:col + 1])
                    p.copy(s2, dxc[:, hsl], engine="gpsimd")
                    for q2 in range(2):
                        cs = slice(hf * HF + q2 * 512, hf * HF + q2 * 512 + 512)
                        ds = slice(q2 * 512, q2 * 512 + 512)
                        p.dma(brow1, ssmd[1 + n:2 + n, cs])
                        p.mm(c.pp[5], ctx.ones_row, brow1, start=True, stop=True)
                        p.tt(s2[:, ds], s2[:, ds], c.pp[5], ALU.mult)
                    p.scan(hv, s1, s2, 0.0 if hf == 0 else hsave)
                    if hf == 0:
                        p.copy(hsave, hv[:, HF - 1:HF], engine="gpsimd")
                    for q2 in range(2):
                        cs = slice(hf * HF + q2 * 512, hf * HF + q2 * 512 + 512)
                        ds = slice(q2 * 512, q2 * 512 + 512)
                        p.dma(crow1, ssmd[17 + n:18 + n, cs])
                        p.mm(c.pp[4], ctx.ones_row, crow1, start=True, stop=True)
                        p.tt(hv[:, ds], hv[:, ds], c.pp[4], ALU.mult)
                    ysl = yac[:, hsl]
                    if n == 0:
                        p.copy(ysl, hv, engine="gpsimd")
                    else:
                        p.tt(ysl, ysl, hv, ALU.add)
            p.stt(yac, xcf, dpp[:, gdt:gdt + 1], yac, ALU.mult, ALU.add)
            with ExitStack() as st2:
                c2 = c.scoped(st2)
                wz = wload(c2, st2, wIn[:, DI + gdt * P: DI + (gdt + 1) * P], D, P,
                           f"l3bz{gdt}{uid}")
                for cc in range(TC):
                    ps = c2.pp[0] if cc % 2 == 0 else c2.pp[1]
                    for kt in range(ND):
                        p.mm(ps, wz[:, kt * P:(kt + 1) * P],
                             ctx.xn_bf[:, kt * T + cc * 512: kt * T + (cc + 1) * 512],
                             start=(kt == 0), stop=(kt == ND - 1))
                    p.act(xcf[:, cc * 512:(cc + 1) * 512], ps, AF.Silu)
                p.tt(xbstg, yac, xcf, ALU.mult)
                p.dma(xcd[gdt * P:(gdt + 1) * P, :], xbstg)
                p.barrier()
        p.barrier()
        stB.close()
        wo_from_dram(c, xcd, wOut, None, f"l3bo{uid}", d_in=DI)
        p.barrier()



def build_and_run(stage_fns, inputs, x_list, tr_in=False, tr_out=False):
    global _HOST
    _HOST = {}
    nc = bass.Bass("TRN2", target_bir_lowering=False, debug=False)
    with ExitStack() as stack:
        p = Prog(nc, stack)
        ctx = Ctx(p, stack)
        ctx.setup_consts()
        if tr_in:
            x0 = nc.dram_tensor("x0", [T, D], F32, kind="ExternalInput").ap()
            input_transpose(ctx, x0)
        else:
            xin = nc.dram_tensor("xin", [D, T], F32, kind="ExternalInput").ap()
            fm_dma_in(ctx, xin)
        p.barrier()
        for fn, uid in stage_fns:
            fn(ctx, inputs, uid)
        if tr_out:
            xo = nc.dram_tensor("xout", [T, D], F32, kind="ExternalOutput").ap()
            output_transpose(ctx, xo)
        else:
            xo = nc.dram_tensor("xout", [D, T], F32, kind="ExternalOutput").ap()
            fm_dma_out(ctx, xo)
        p.barrier()
        host = dict(_HOST)
        key = "x0" if tr_in else "xin"
        in_maps = [dict(host, **{key: np.ascontiguousarray(x)}) for x in x_list]
        res = run_bass_kernel_spmd(nc, in_maps, core_ids=list(range(len(x_list))))
        return [r["xout"] for r in res.results]


def kernel(**inputs):
    x = np.asarray(inputs["L0_attn_a_x"], dtype=np.float32)
    B = x.shape[0]
    groups = launch_groups()
    outs = None
    for gi, (stages, tr_in, tr_out) in enumerate(groups):
        if tr_in:
            xs = [x[b] for b in range(B)]
        else:
            xs = outs
        outs = build_and_run(stages, inputs, xs, tr_in=tr_in, tr_out=tr_out)
    return np.stack(outs, axis=0)


def launch_groups():
    return [(STAGES, True, True)]


STAGES = [
    (st_l0_attn_a, "a"), (st_l0_ffn, "a"), (st_l0_attn_b, "a"), (st_l0_ffn, "b"),
    (st_l1_attn_a, "a"), (st_l1_ffn, "a"), (st_l1_attn_b, "a"), (st_l1_ffn, "b"),
    (st_l2_attn_a, "a"), (st_l2_ffn, "a"), (st_l2_attn_b, "a"), (st_l2_ffn, "b"),
    (st_l3_attn_a, "a"), (st_l3_ffn, "a"), (st_l3_attn_b, "a"), (st_l3_ffn, "b"),
]
